# version 09
# baseline (speedup 1.0000x reference)
import numpy as np
import ml_dtypes
import concourse.bass as bass
import concourse.mybir as mybir
from concourse.bass_utils import run_bass_kernel_spmd

F32 = mybir.dt.float32
BF16 = mybir.dt.bfloat16
AF = mybir.ActivationFunctionType
ALU = mybir.AluOpType
AX = mybir.AxisListType

B, S, HID = 4, 2048, 2048
NH, HD = 16, 128
R = 16
DFF = 5461
NE = 8
KCAP = 1024.0
TOK = 1024
NT_OWN = 8
NT_ALL = 16
TKT = 1152
CW = 132
NFT = 43
EPS_LN = 1e-5
EPS_RMS = 1e-6
NEG = -1e9
SCALE = float(1.0 / np.sqrt(128.0))
BISECT_ITERS = 44
EBIAS = -30.0

DEBUG_TAPS = ()
TAP_GROUP = 0



def _rope_tables(pos):
    inv_freq = 1.0 / 10000.0 ** (np.arange(32, dtype=np.float64) * 2.0 / 64.0)
    ang = pos.astype(np.float64)[:, None] * inv_freq[None, :]
    return np.cos(ang).astype(np.float32), np.sin(ang).astype(np.float32)


def _pad_cols(a, n):
    out = np.zeros((a.shape[0], n), a.dtype)
    out[:, :a.shape[1]] = a
    return out


def host_inputs(core, inp):
    b, half = core // 2, core % 2
    d = {}
    x = np.asarray(inp["L0_attn_hidden_states"], np.float32)
    d["x_own"] = np.ascontiguousarray(x[b, half * TOK:(half + 1) * TOK])
    d["x_oth"] = np.ascontiguousarray(x[b, (1 - half) * TOK:(2 - half) * TOK])

    own_start = half * TOK
    pos_q = own_start + np.arange(TOK)
    cq, sq = _rope_tables(pos_q)
    d["cos_q"], d["sin_q"] = cq, sq
    pos_k = np.concatenate([own_start - 128 + np.arange(128), pos_q])
    ck_, sk_ = _rope_tables(pos_k)
    d["cos_k"], d["sin_k"] = ck_, sk_
    be = np.concatenate([own_start + np.arange(64) * R + (R - 1),
                         (1 - half) * TOK + np.arange(64) * R + (R - 1)])
    cb, sbt_ = _rope_tables(be)
    d["cosT_b"] = np.ascontiguousarray(cb.T)
    d["sinT_b"] = np.ascontiguousarray(sbt_.T)

    cm = np.zeros((TOK, CW), np.float32)
    cm[:, :128] = (be[None, :] <= pos_q[:, None]).astype(np.float32)
    cm[:, 128] = 1.0
    d["cmaskM"] = np.ascontiguousarray(
        cm.reshape(8, 128, CW).transpose(1, 0, 2).reshape(128, 8 * CW)
    ).astype(ml_dtypes.bfloat16)

    i = np.arange(128)[:, None]
    j = np.arange(256)[None, :]
    rest = (((j < 128) & (j >= i)) | ((j >= 128) & (j - 128 <= i))).astype(np.float32)
    first = rest.copy()
    if half == 0:
        first[:, :128] = 0.0
    d["mrestM2"] = np.concatenate([rest, rest], 1).astype(ml_dtypes.bfloat16)
    d["mfirstM2"] = np.concatenate([first, rest], 1).astype(ml_dtypes.bfloat16)

    import jax
    _old_part = jax.config.jax_threefry_partitionable
    jax.config.update("jax_threefry_partitionable", True)
    with jax.default_device(jax.devices("cpu")[0]):
        u = np.asarray(jax.random.uniform(
            jax.random.key(42, impl="threefry2x32"), (B * S, NE),
            dtype=np.float32))
    jax.config.update("jax_threefry_partitionable", _old_part)
    gum = (-np.log(-np.log(u + 1e-10) + 1e-10)).astype(np.float32)
    d["gumbelT"] = np.ascontiguousarray(
        gum[b * S + own_start: b * S + own_start + TOK].T)

    for k, srck in [("Wq", "L0_attn_Wq"), ("Wlk", "L0_attn_Wlk"),
                    ("Wlv", "L0_attn_Wlv"), ("Wo", "L0_attn_Wo"),
                    ("Wk", "L0_attn_Wk"), ("Wv", "L0_attn_Wv"),
                    ("rw", "L0_ffn_router_w")]:
        d[k] = np.asarray(inp[srck], np.float32)
    d["Wc"] = np.ascontiguousarray(
        np.asarray(inp["L0_attn_Wc"], np.float32).reshape(16, 128).T)
    for k, srck in [("bq", "L0_attn_bq"), ("blk", "L0_attn_blk"),
                    ("blv", "L0_attn_blv"), ("bo", "L0_attn_bo"),
                    ("bk", "L0_attn_bk"), ("bv", "L0_attn_bv"),
                    ("bc", "L0_attn_bc"), ("bd", "L0_ffn_b_down")]:
        d[k] = np.asarray(inp[srck], np.float32)[None, :]
    d["qn_row"] = np.asarray(inp["L0_attn_qn_w"], np.float32)[None, :]
    d["kn_row"] = np.asarray(inp["L0_attn_kn_w"], np.float32)[None, :]
    d["kn_col"] = np.asarray(inp["L0_attn_kn_w"], np.float32)[:, None]
    d["sinkTk"] = _pad_cols(np.ascontiguousarray(
        np.asarray(inp["L0_attn_sink_k"], np.float32).T), 20)
    d["sinkv_row"] = np.ascontiguousarray(
        np.asarray(inp["L0_attn_sink_v"], np.float32).reshape(1, NH * HD))
    d["rb_col"] = np.asarray(inp["L0_ffn_router_b"], np.float32)[:, None]
    d["wg"] = _pad_cols(np.asarray(inp["L0_ffn_w_gate"], np.float32), NFT * 128)
    d["wu"] = _pad_cols(np.asarray(inp["L0_ffn_w_up"], np.float32), NFT * 128)
    wdp = np.zeros((NFT * 128, HID), np.float32)
    wdp[:DFF] = np.asarray(inp["L0_ffn_w_down"], np.float32)
    d["wd"] = wdp
    d["bg_col"] = np.ascontiguousarray(_pad_cols(
        np.asarray(inp["L0_ffn_b_gate"], np.float32)[None, :], NFT * 128)
        .reshape(NFT, 128).T)
    d["bu_col"] = np.ascontiguousarray(_pad_cols(
        np.asarray(inp["L0_ffn_b_up"], np.float32)[None, :], NFT * 128)
        .reshape(NFT, 128).T)
    return d



class Sem:
    def __init__(self, nc, name):
        self.h = nc.alloc_semaphore(name)
        self.n = 0

    def mark(self):
        return (self, self.n)


class Prog:
    ENGINES = ("tensor", "vector", "scalar", "gpsimd", "sync")

    def __init__(self, nc):
        self.nc = nc
        self.thunks = {e: [] for e in self.ENGINES}
        self.last_wait = {e: {} for e in self.ENGINES}
        self.ps_reader = {}

    def op(self, eng, f, waits=(), inc=None, inc_n=1):
        wl = []
        lw = self.last_wait[eng]
        for w in waits:
            if w is None:
                continue
            sem, v = w
            if v > 0 and lw.get(sem, -1) < v:
                wl.append((sem.h, v))
                lw[sem] = v
        if inc is not None:
            inc.n += inc_n
        ih = inc.h if inc is not None else None

        def thunk(e, wl=wl, f=f, ih=ih, inc_n=inc_n):
            for h, v in wl:
                e.wait_ge(h, v)
            inst = f(e)
            if ih is not None:
                inst.then_inc(ih, inc_n)
        self.thunks[eng].append(thunk)

    def ps_take(self, *keys):
        return [m for k in keys if (m := self.ps_reader.get(k)) is not None]

    def ps_done(self, key, mark):
        self.ps_reader[key] = mark

    def emit(self, block):
        for eng in self.ENGINES:
            ths = self.thunks[eng]
            getattr(block, eng)(lambda e, ths=ths: [t(e) for t in ths])



def build_program():
    nc = bass.Bass("TRN2", target_bir_lowering=False, debug=False)
    P = Prog(nc)

    def inp(name, shape, dtype=F32):
        return nc.dram_tensor(name, list(shape), dtype, kind="ExternalInput").ap()

    x_own = inp("x_own", (TOK, HID))
    x_oth = inp("x_oth", (TOK, HID))
    cos_q = inp("cos_q", (TOK, 32)); sin_q = inp("sin_q", (TOK, 32))
    cos_k = inp("cos_k", (TKT, 32)); sin_k = inp("sin_k", (TKT, 32))
    cosT_b = inp("cosT_b", (32, 128)); sinT_b = inp("sinT_b", (32, 128))
    cmaskM = inp("cmaskM", (128, 8 * CW), BF16)
    mfirstM2 = inp("mfirstM2", (128, 512), BF16)
    mrestM2 = inp("mrestM2", (128, 512), BF16)
    gumbelT = inp("gumbelT", (NE, TOK))
    Wq = inp("Wq", (HID, HID)); Wlk = inp("Wlk", (HID, HID)); Wlv = inp("Wlv", (HID, HID))
    Wo = inp("Wo", (HID, HID))
    Wk = inp("Wk", (HID, HD)); Wv = inp("Wv", (HID, HD)); Wc = inp("Wc", (128, 16))
    bq = inp("bq", (1, HID)); blk = inp("blk", (1, HID)); blv = inp("blv", (1, HID))
    bo = inp("bo", (1, HID)); bk = inp("bk", (1, HD)); bv = inp("bv", (1, HD))
    bc = inp("bc", (1, 1)); bd = inp("bd", (1, HID))
    qn_row = inp("qn_row", (1, HD)); kn_row = inp("kn_row", (1, HD))
    kn_col = inp("kn_col", (HD, 1))
    sinkTk = inp("sinkTk", (HD, 20)); sinkv_row = inp("sinkv_row", (1, NH * HD))
    rw = inp("rw", (HID, NE)); rb_col = inp("rb_col", (NE, 1))
    wg = inp("wg", (HID, NFT * 128)); wu = inp("wu", (HID, NFT * 128))
    wd = inp("wd", (NFT * 128, HID))
    bg_col = inp("bg_col", (128, NFT)); bu_col = inp("bu_col", (128, NFT))

    y = nc.dram_tensor("y", [TOK, HID], F32, kind="ExternalOutput").ap()

    id_bf = nc.inline_tensor(np.eye(128, dtype=ml_dtypes.bfloat16), "id_bf").ap()
    ones_row_c = nc.inline_tensor(np.ones((1, 512), np.float32), "ones_row").ap()
    ones_col_c = nc.inline_tensor(np.ones((128, 1), np.float32), "ones_col").ap()
    m8np = np.zeros((128, 8), np.float32)
    m8np[np.arange(128), np.arange(128) // 16] = 1.0
    M8c = nc.inline_tensor(m8np, "M8c").ap()
    seg16T_c = nc.inline_tensor(np.ascontiguousarray(m8np.T), "seg16Tc").ap()
    ones8_c = nc.inline_tensor(np.ones((8, 1), np.float32), "ones8c").ap()
    eps_c = nc.inline_tensor(
        np.broadcast_to(np.array([EPS_LN, EPS_RMS, EBIAS], np.float32),
                        (128, 3)).copy(), "eps_c").ap()

    smt_d = nc.dram_tensor("smt_d", [NE, TOK], F32, kind="Internal").ap()
    gath_d = nc.dram_tensor("gath_d", [NE * 8, TOK], F32, kind="Internal").ap()
    ao_d = nc.dram_tensor("ao_d", [TOK, HID], F32, kind="Internal").ap()

    taps = {}

    def tap(name, shape, dtype=F32):
        taps[name] = nc.dram_tensor("tap_" + name, list(shape), dtype,
                                    kind="ExternalOutput").ap()
        return taps[name]

    sbt = nc.alloc_sbuf_tensor
    xnT = sbt("xnT", [128, 16 * TKT], BF16).ap()
    arenaA = sbt("arenaA", [128, 16384], F32).ap()
    arenaB = sbt("arenaB", [128, 8192], F32).ap()
    arenaC = sbt("arenaC", [128, 4096], F32).ap()
    aA = arenaA.bitcast(BF16)
    aB = arenaB.bitcast(BF16)
    aC = arenaC.bitcast(BF16)

    xn = aA
    O_W0 = 0
    O_QTAB = 4096
    O_KTAB = O_QTAB + 1024
    O_QNW = O_KTAB + 1152
    O_KNW = O_QNW + 128
    O_QN2 = O_KNW + 128
    O_SQB = O_QN2 + 512
    O_SSQ = O_SQB + 512
    O_RTMP = O_SSQ + 32
    O_Q4R = O_RTMP + 256
    O_QT4 = O_Q4R + 256
    O_LKT4 = O_QT4 + 2048
    O_LV4 = O_LKT4 + 2304
    O_PL2 = O_LV4 + 2304
    O_PC2 = O_PL2 + 512
    O_PTL = O_PC2 + 272
    O_PTC = O_PTL + 256
    O_PTS = O_PTC + 128
    O_RR = O_PTS + 128
    O_TRASH = O_PL2
    O_WOST = O_QN2
    assert O_RR + 32 <= 16384

    wbuf_bf = aA[:, 2 * O_W0: 2 * O_W0 + 8192]
    qtab = arenaA[:, O_QTAB:O_QTAB + 1024]
    ktab = arenaA[:, O_KTAB:O_KTAB + 1152]
    qnwbc = arenaA[:, O_QNW:O_QNW + 128]
    knwbc = arenaA[:, O_KNW:O_KNW + 128]
    qn2 = arenaA[:, O_QN2:O_QN2 + 512]
    sqb = arenaA[:, O_SQB:O_SQB + 512]
    ssq = arenaA[:, O_SSQ:O_SSQ + 32]
    rtmp = arenaA[:, O_RTMP:O_RTMP + 256]
    q4r = aA[:, 2 * O_Q4R: 2 * O_Q4R + 512]
    qT4 = aA[:, 2 * O_QT4: 2 * O_QT4 + 4096]
    lkT4 = aA[:, 2 * O_LKT4: 2 * O_LKT4 + 4608]
    lv4 = aA[:, 2 * O_LV4: 2 * O_LV4 + 4608]
    pl2 = [aA[:, 2 * O_PL2: 2 * O_PL2 + 512],
           aA[:, 2 * O_PL2 + 512: 2 * O_PL2 + 1024]]
    pc2 = [aA[:, 2 * O_PC2: 2 * O_PC2 + 264],
           aA[:, 2 * O_PC2 + 272: 2 * O_PC2 + 536]]
    pTl = aA[:, 2 * O_PTL: 2 * O_PTL + 512]
    pTc = aA[:, 2 * O_PTC: 2 * O_PTC + 256]
    pTs = aA[:, 2 * O_PTS: 2 * O_PTS + 256]
    rrp = arenaA[:, O_RR:O_RR + 32]
    trashA = arenaA[:, O_TRASH:O_TRASH + 512]
    swb = arenaA[:, O_PL2:O_PL2 + 512]
    rtmp1 = arenaA[:, O_PC2:O_PC2 + 512]
    negq = arenaA[:, O_RR + 32:O_RR + 64]
    negk = arenaA[:, O_RR + 64:O_RR + 96]
    wo_stage = arenaA[:, O_WOST:O_WOST + 1024]

    xnTt = aB[:, 0:2048]
    zr = arenaB[:, 1024:2048]
    uexp = arenaB[:, 2048:3072]
    gumT = arenaB[:, 3072:4096]
    logc = arenaB[:, 4096:6144]
    trashB = arenaB[:, 6144:8192]
    entT = aB[:, 2 * 6144: 2 * 6144 + 2048]
    wblkb = aB[:, 2 * 7296: 2 * 7296 + 128]
    ctxT = aB[:, 0:16384]
    O_SG = 5632
    O_YST = 6656
    sg_sb = aB[:, 2 * O_SG: 2 * O_SG + 2048]
    ystage = arenaB[:, O_YST:O_YST + 1024]
    aostage = arenaB[:, O_SG:O_SG + 1024]

    def hT_tile(f):
        if f < 32:
            return aA[:, f * 1024:(f + 1) * 1024]
        return aB[:, (f - 32) * 1024:(f - 32 + 1) * 1024]

    xst = [arenaC[:, 0:2048], arenaC[:, 2048:4096]]
    wo_sb = aC[:, 0:8192]
    wff = [aC[:, 0:2048], aC[:, 2048:4096], aC[:, 4096:6144], aC[:, 6144:8192]]
    wdb_sl = [aC[:, 0:4096], aC[:, 4096:8192]]

    lnA_t = sbt("lnA", [128, 16], F32).ap()
    eps_sb = sbt("eps_sb", [128, 3], F32).ap()
    nc.const_aps.register(F32, EPS_LN, eps_sb[:, 0:1])
    nc.const_aps.register(F32, EPS_RMS, eps_sb[:, 1:2])
    nc.const_aps.register(F32, EBIAS, eps_sb[:, 2:3])
    cmaskM_sb = sbt("cmaskM_sb", [128, 8 * CW], BF16).ap()
    mfirstM_sb = sbt("mfirstM_sb", [128, 512], BF16).ap()
    mrestM_sb = sbt("mrestM_sb", [128, 512], BF16).ap()
    cosq_sb = sbt("cosq_sb", [128, 8 * 32], F32).ap()
    sinq_sb = sbt("sinq_sb", [128, 8 * 32], F32).ap()
    cosk_sb = sbt("cosk_sb", [128, 9 * 32], F32).ap()
    sink_sb = sbt("sink_sb", [128, 9 * 32], F32).ap()
    cosb_sb = sbt("cosb_sb", [32, 128], F32).ap()
    sinb_sb = sbt("sinb_sb", [32, 128], F32).ap()
    id_sb = sbt("id_sb", [128, 128], BF16).ap()
    onesr = sbt("onesr", [1, 512], F32).ap()
    onesc = sbt("onesc", [128, 1], F32).ap()
    m8_sb = sbt("m8_sb", [128, 8], F32).ap()
    seg16T_sb = sbt("seg16T_sb", [8, 128], F32).ap()
    ones8 = sbt("ones8", [8, 1], F32).ap()
    qnr_sb = sbt("qnr_sb", [1, HD], F32).ap()
    knr_sb = sbt("knr_sb", [1, HD], F32).ap()
    knc_sb = sbt("knc_sb", [HD, 1], F32).ap()
    rbc_sb = sbt("rbc_sb", [NE, 1], F32).ap()
    sinkTk_sb = sbt("sinkTk_sb", [HD, 20], BF16).ap()
    sinkv_sb = sbt("sinkv_sb", [1, NH * HD], BF16).ap()
    bgc_sb = sbt("bgc_sb", [128, NFT], F32).ap()
    buc_sb = sbt("buc_sb", [128, NFT], F32).ap()
    brow = sbt("brow", [1, 1536], F32).ap()
    borow = sbt("borow", [1, 2048], F32).ap()
    bdrow = sbt("bdrow", [1, 2048], F32).ap()
    smt_sb = sbt("smt_sb", [NE, TOK], F32).ap()
    bis_sb = sbt("bis_sb", [128, 512], F32).ap()
    bst = sbt("bst", [8, 16], F32).ap()
    mid128 = sbt("mid128", [128, 2], F32).ap()
    cnt128 = sbt("cnt128", [128, 1], F32).ap()
    sT_row = sbt("sT_row", [NE, TOK], F32).ap()
    s_col = sbt("s_col", [128, 8], F32).ap()
    ckT_sb = sbt("ckT_sb", [128, 128], BF16).ap()
    cv_sb = sbt("cv_sb", [128, 128], BF16).ap()
    ckn_sb = sbt("ckn_sb", [128, 128], F32).ap()
    tmpR = sbt("tmpR", [32, 512], F32).ap()
    tmpo = sbt("tmpo", [32, 128], BF16).ap()
    expw_sb = sbt("expw_sb", [128, 16], F32).ap()
    sumw_sb = sbt("sumw_sb", [1, 128], F32).ap()
    rsumw_sb = sbt("rsumw_sb", [1, 128], F32).ap()
    rsb_sb = sbt("rsb_sb", [128, 128], F32).ap()
    invb_sb = sbt("invb_sb", [128, 128], F32).ap()
    ckr_sb = sbt("ckr_sb", [1, 128], F32).ap()
    wc_sb = sbt("wc_sb", [128, 16], BF16).ap()
    rw_sb = sbt("rw_sb", [128, 16 * NE], BF16).ap()
    qtab2 = arenaB[:, 6144:7168]
    ktab2 = arenaB[:, 7168:8192]
    ktab2t8 = arenaA[:, 16144:16272]

    def kt2tile(t):
        return ktab2[:, t * 128:(t + 1) * 128] if t < 8 else ktab2t8
    wk_sb = aC[:, 0:2048]
    wv_sb = aC[:, 2048:4096]

    s_x = Sem(nc, "s_x")
    s_x2 = Sem(nc, "s_x2")
    s_c = Sem(nc, "s_c")
    s_w = Sem(nc, "s_w")
    s_t = Sem(nc, "s_t")
    s_v = Sem(nc, "s_v")
    s_s = Sem(nc, "s_s")
    s_cc = Sem(nc, "s_cc")
    s_st = Sem(nc, "s_st")

    PS = [nc.alloc_psum_tensor(f"ps{i}", [128, 1024], F32).ap() for i in range(4)]
    PSbf = [p.bitcast(BF16) for p in PS]

    def ld(dst, src, waits=()):
        P.op("sync", lambda e, d=dst, s=src: e.dma_start(out=d, in_=s),
             waits=waits, inc=s_c, inc_n=16)

    def wld(dst, src, waits=()):
        P.op("gpsimd", lambda e, d=dst, s=src: e.dma_start(out=d, in_=s),
             waits=waits, inc=s_w, inc_n=16)

    def ld3(dst2d, src, n, q="sync", waits=()):
        s3 = src.rearrange("(c p) f -> p c f", p=128)
        d3 = dst2d.rearrange("p (c f) -> p c f", c=n)
        if q == "sync":
            P.op("sync", lambda e, d=d3, s_=s3: e.dma_start(out=d, in_=s_),
                 waits=waits, inc=s_c, inc_n=16)
        else:
            P.op("gpsimd", lambda e, d=d3, s_=s3: e.dma_start(out=d, in_=s_),
                 waits=waits, inc=s_w, inc_n=16)

    ld(id_sb, id_bf); ld(onesr, ones_row_c); ld(onesc, ones_col_c)
    ld(m8_sb, M8c); ld(seg16T_sb, seg16T_c); ld(ones8, ones8_c)
    ld(mfirstM_sb, mfirstM2); ld(mrestM_sb, mrestM2)
    ld(cmaskM_sb, cmaskM)
    ld3(cosq_sb, cos_q, 8)
    ld3(sinq_sb, sin_q, 8)
    ld3(cosk_sb, cos_k, 9)
    ld3(sink_sb, sin_k, 9)
    ld(cosb_sb, cosT_b); ld(sinb_sb, sinT_b)
    ld(qnr_sb, qn_row); ld(knr_sb, kn_row); ld(knc_sb, kn_col)
    ld(rbc_sb, rb_col); ld(bgc_sb, bg_col); ld(buc_sb, bu_col)
    ld(eps_sb, eps_c)
    ld(gumT[0:8, :], gumbelT)
    ld(brow[0:1, 1024:1152], bk)
    ld(brow[0:1, 1152:1280], bv)
    ld(brow[0:1, 1280:1281], bc)
    CONSTS = s_c.mark()


    wld(sinkTk_sb, sinkTk)
    wld(sinkv_sb, sinkv_row)
    wld(wc_sb, Wc)
    ld3(rw_sb, rw, 16, q="g")
    W_PRE = s_w.mark()

    tile_done = []
    xnT_marks = []
    for tt in range(NT_ALL):
        buf = xst[tt % 2]
        src = x_own if tt < NT_OWN else x_oth
        r0 = (tt % NT_OWN) * 128
        w = [] if tt < 2 else [tile_done[tt - 2]]
        sx_p = s_x if tt % 2 == 0 else s_x2
        P.op("sync", lambda e, bu_=buf, s=src, r=r0: e.dma_start(
            out=bu_, in_=s[r:r + 128, :]), waits=w, inc=sx_p, inc_n=16)
        xw = sx_p.mark()
        lnA = lnA_t[:, 0:8]
        P.op("vector", lambda e, bu_=buf, L=lnA: e.reduce_sum(L[:, 0:1], bu_, axis=AX.X),
             waits=[xw], inc=s_v)
        VS = s_v.mark()
        P.op("scalar", lambda e, bu_=buf: e.activation(
            trashB, bu_, AF.Square, accum_out=lnA[:, 1:2]), waits=[xw], inc=s_s)
        SQm = s_s.mark()
        P.op("vector", lambda e: e.tensor_scalar(
            lnA[:, 2:3], lnA[:, 0:1], 1.0 / HID, None, ALU.mult), waits=[VS], inc=s_v)
        P.op("vector", lambda e: e.tensor_scalar(
            lnA[:, 3:4], lnA[:, 1:2], 1.0 / HID, None, ALU.mult), waits=[SQm], inc=s_v)
        vB = s_v.mark()
        P.op("vector", lambda e: e.scalar_tensor_tensor(
            lnA[:, 4:5], lnA[:, 2:3], lnA[:, 2:3], lnA[:, 3:4], ALU.mult, ALU.subtract),
            waits=[vB], inc=s_v)
        vC = s_v.mark()
        P.op("scalar", lambda e: e.activation(
            lnA[:, 5:6], lnA[:, 4:5], AF.Ln, bias=EPS_LN, scale=-1.0),
            waits=[vC, CONSTS], inc=s_s)
        sA = s_s.mark()
        P.op("scalar", lambda e: e.activation(
            lnA[:, 6:7], lnA[:, 5:6], AF.Exp, scale=-0.5), waits=[sA], inc=s_s)
        sB = s_s.mark()
        P.op("vector", lambda e: e.tensor_scalar(
            lnA[:, 7:8], lnA[:, 2:3], lnA[:, 6:7], -1.0, ALU.mult, ALU.mult),
            waits=[sB], inc=s_v)
        vD = s_v.mark()
        P.op("scalar", lambda e, bu_=buf, t=tt: e.activation(
            xn[:, t * 2048:(t + 1) * 2048], bu_, AF.Identity,
            bias=lnA[:, 7:8], scale=lnA[:, 6:7]), waits=[vD], inc=s_s)
        sAPP = s_s.mark()
        tile_done.append(sAPP)

        for grp in range(4):
            hf = grp % 2
            key = "3a" if hf == 0 else "3b"
            tw = [sAPP, CONSTS] + P.ps_take(key)
            for k in range(4):
                hc = grp * 4 + k
                P.op("tensor", lambda e, t=tt, h=hc, k_=k, hf_=hf: e.transpose(
                    PSbf[3][:, hf_ * 1024 + k_ * 128: hf_ * 1024 + (k_ + 1) * 128],
                    xn[:, t * 2048 + h * 128: t * 2048 + (h + 1) * 128], id_sb),
                    waits=tw if k == 0 else (), inc=s_t if k == 3 else None)
            tE = s_t.mark()
            if tt < NT_OWN:
                dst = xnT.rearrange("p (h t) -> p h t", h=16)[
                    :, grp * 4:(grp + 1) * 4, tt * 128:(tt + 1) * 128]
            else:
                dst = xnTt.rearrange("p (h t) -> p h t", h=16)[
                    :, grp * 4:(grp + 1) * 4, 0:128]
            P.op("vector", lambda e, d=dst, hf_=hf: e.tensor_copy(
                d, PSbf[3][:, hf_ * 1024:hf_ * 1024 + 512].rearrange(
                    "p (h t) -> p h t", h=4)), waits=[tE], inc=s_v)
            P.ps_done(key, s_v.mark())
            if tt == NT_ALL - 1:
                P.op("vector", lambda e, g=grp, hf_=hf: e.tensor_copy(
                    xnT.rearrange("p (h t) -> p h t", h=16)[
                        :, g * 4:(g + 1) * 4, 1024:1152],
                    PSbf[3][:, hf_ * 1024:hf_ * 1024 + 512].rearrange(
                        "p (h t) -> p h t", h=4)), inc=s_v)
                P.ps_done(key, s_v.mark())
        TILE_T = s_v.mark()
        xnT_marks.append(TILE_T)

        if tt < NT_OWN:
            def src_xnT(h, t=tt):
                return xnT[:, h * TKT + t * 128: h * TKT + (t + 1) * 128]
        else:
            def src_xnT(h):
                return xnTt[:, h * 128:(h + 1) * 128]
        lw = [TILE_T, W_PRE, CONSTS] + P.ps_take("2a")
        for hc in range(16):
            P.op("tensor", lambda e, h=hc, fx=src_xnT: e.matmul(
                PS[2][0:1, 0:128], wc_sb[:, h:h + 1], fx(h),
                start=(h == 0), stop=(h == 15)),
                waits=lw if hc == 0 else (), inc=s_t if hc == 15 else None)
        LCM = s_t.mark()
        P.op("scalar", lambda e, t=tt: e.activation(
            logc[0:1, t * 128:(t + 1) * 128], PS[2][0:1, 0:128], AF.Exp,
            bias=brow[0:1, 1280:1281]), waits=[LCM], inc=s_s)
        P.ps_done("2a", s_s.mark())
    XNT_OWN = xnT_marks[NT_OWN - 1]
    XNT_ALL = xnT_marks[NT_ALL - 1]
    LOGC = s_s.mark()
    XN_LAST = tile_done[-1]

    if "xnT" in DEBUG_TAPS:
        t_ = tap("xnT", (128, 16 * TKT), BF16)
        P.op("sync", lambda e, t2=t_: e.dma_start(out=t2, in_=xnT),
             waits=[XNT_ALL, LOGC], inc=s_st, inc_n=16)
    if "xn8" in DEBUG_TAPS:
        t_ = tap("xn8", (128, 16384), BF16)
        P.op("sync", lambda e, t2=t_: e.dma_start(out=t2, in_=aA[:, 0:16384]),
             waits=[XNT_ALL, LOGC], inc=s_st, inc_n=16)
        _m = s_st.mark()
        P.op("gpsimd", lambda e: e.nop(), waits=[_m])
    if "logc" in DEBUG_TAPS:
        t_ = tap("logc", (1, 2048))
        P.op("sync", lambda e, t2=t_: e.dma_start(out=t2, in_=logc[0:1, :]),
             waits=[LOGC], inc=s_st, inc_n=16)

    mmw = [XNT_OWN, W_PRE] + P.ps_take("0a", "0b")
    for n2 in range(2):
        for hc in range(16):
            P.op("tensor", lambda e, h=hc, n=n2: e.matmul(
                PS[0][0:8, n * 512:(n + 1) * 512], rw_sb[:, h * 8:(h + 1) * 8],
                xnT[:, h * TKT + n * 512: h * TKT + (n + 1) * 512],
                start=(h == 0), stop=(h == 15)),
                waits=mmw if (n2 == 0 and hc == 0) else (),
                inc=s_t if (n2 == 1 and hc == 15) else None)
    RT = s_t.mark()
    P.op("vector", lambda e: e.scalar_tensor_tensor(
        uexp[0:8, :], PS[0][0:8, :], rbc_sb, gumT[0:8, :], ALU.add, ALU.add),
        waits=[RT, CONSTS], inc=s_v)
    P.ps_done("0a", s_v.mark()); P.ps_done("0b", s_v.mark())
    RV = s_v.mark()
    P.op("scalar", lambda e: e.activation(uexp[0:8, :], uexp[0:8, :], AF.Exp),
         waits=[RV], inc=s_s)
    RE = s_s.mark()
    zw = [RE] + P.ps_take("1a", "1b")
    for n2 in range(2):
        P.op("tensor", lambda e, n=n2: e.matmul(
            PS[1][0:1, n * 512:(n + 1) * 512], ones8,
            uexp[0:8, n * 512:(n + 1) * 512], start=True, stop=True),
            waits=zw if n2 == 0 else (), inc=s_t if n2 == 1 else None)
    RZ = s_t.mark()
    P.op("vector", lambda e: e.reciprocal(zr[0:1, :], PS[1][0:1, :]),
         waits=[RZ], inc=s_v)
    P.ps_done("1a", s_v.mark()); P.ps_done("1b", s_v.mark())
    RR_ = s_v.mark()
    bw = [RR_] + P.ps_take("2a", "2b")
    for n2 in range(2):
        P.op("tensor", lambda e, n=n2: e.matmul(
            PS[2][0:8, n * 512:(n + 1) * 512], onesr[0:1, 0:8],
            zr[0:1, n * 512:(n + 1) * 512], start=True, stop=True),
            waits=bw if n2 == 0 else (), inc=s_t if n2 == 1 else None)
    RB2 = s_t.mark()
    P.op("vector", lambda e: e.tensor_tensor(
        smt_sb[0:8, :], uexp[0:8, :], PS[2][0:8, :], ALU.mult), waits=[RB2], inc=s_v)
    P.ps_done("2a", s_v.mark()); P.ps_done("2b", s_v.mark())
    SMT = s_v.mark()
    P.op("gpsimd", lambda e: e.dma_start(out=smt_d, in_=smt_sb[0:8, :]),
         waits=[SMT], inc=s_w, inc_n=16)
    SMT_ST = s_w.mark()
    P.op("gpsimd", lambda e: e.collective_compute(
        "AllGather", ALU.bypass, replica_groups=[list(range(8))],
        ins=[smt_d.opt()], outs=[gath_d.opt()]), waits=[SMT_ST], inc=s_cc)
    for gi in range(2):
        P.op("gpsimd", lambda e, g_=gi: e.dma_start(
            out=bis_sb[g_::2, :],
            in_=gath_d.rearrange("(r e) (g2 f) -> e r g2 f", e=8, g2=2)[:, :, g_, :]),
            waits=[(s_cc, 1)] if gi == 0 else (), inc=s_w, inc_n=16)
    BIS_LD = s_w.mark()
    if "smt" in DEBUG_TAPS:
        t_ = tap("smt", (NE, TOK))
        P.op("sync", lambda e, t2=t_: e.dma_start(out=t2, in_=smt_sb[0:8, :]),
             waits=[SMT], inc=s_st, inc_n=16)
    if "bis" in DEBUG_TAPS:
        t_ = tap("bis", (128, 512))
        P.op("sync", lambda e, t2=t_: e.dma_start(out=t2, in_=bis_sb),
             waits=[BIS_LD], inc=s_st, inc_n=16)

    ld3(wk_sb, Wk, 16, q="g", waits=[XN_LAST])
    ld3(wv_sb, Wv, 16, q="g", waits=[XN_LAST])
    WKV = s_w.mark()
    P.op("vector", lambda e: e.reduce_sum(
        sumw_sb.rearrange("p (a b) -> p a b", a=128),
        logc[0:1, :].rearrange("p (a b) -> p a b", a=128), axis=AX.X),
        waits=[LOGC], inc=s_v)
    SW = s_v.mark()
    P.op("vector", lambda e: e.reciprocal(rsumw_sb, sumw_sb), waits=[SW], inc=s_v)
    RSW = s_v.mark()
    P.op("tensor", lambda e: e.matmul(PS[2][:, 0:128], onesr[0:1, 0:128],
                                      rsumw_sb, start=True, stop=True),
         waits=[RSW] + P.ps_take("2a"), inc=s_t)
    P.op("vector", lambda e: e.tensor_copy(rsb_sb, PS[2][:, 0:128]),
         waits=[s_t.mark()], inc=s_v)
    P.ps_done("2a", s_v.mark())
    RSB = s_v.mark()
    exw = [LOGC] + P.ps_take("2b")
    for ttc in range(16):
        P.op("tensor", lambda e, t=ttc: e.matmul(
            PS[2][:, 512 + t:513 + t], logc[0:1, t * 128:(t + 1) * 128],
            onesr[0:1, 0:1], start=True, stop=True),
            waits=exw if ttc == 0 else (), inc=s_t if ttc == 15 else None)
    EXC = s_t.mark()
    P.op("vector", lambda e: e.tensor_copy(expw_sb, PS[2][:, 512:528]),
         waits=[EXC], inc=s_v)
    P.ps_done("2b", s_v.mark())
    EXS = s_v.mark()
    for ttc in range(16):
        P.op("vector", lambda e, t=ttc: e.tensor_scalar(
            wblkb[:, t * 8:(t + 1) * 8], m8_sb, expw_sb[:, t:t + 1], None, ALU.mult),
            waits=[EXS, CONSTS] if ttc == 0 else (), inc=s_v if ttc == 15 else None)
    WBLK = s_v.mark()
    enw = [WBLK, XN_LAST] + P.ps_take("0a", "0b", "1a", "1b")
    for ht in range(16):
        ps = PS[0] if ht < 8 else PS[1]
        col = (ht % 8) * 128
        for ttc in range(16):
            P.op("tensor", lambda e, t=ttc, h=ht, p=ps, c=col: e.matmul(
                p[:, c + t * 8: c + (t + 1) * 8],
                xn[:, t * 2048 + h * 128: t * 2048 + (h + 1) * 128],
                wblkb[:, t * 8:(t + 1) * 8], start=True, stop=True),
                waits=enw if (ht == 0 and ttc == 0) else (),
                inc=s_t if (ht == 15 and ttc == 15) else None)
    ENTM = s_t.mark()
    for ht in range(16):
        ps = PS[0] if ht < 8 else PS[1]
        col = (ht % 8) * 128
        P.op("vector", lambda e, h=ht, p=ps, c=col: e.tensor_tensor(
            entT[:, h * 128:(h + 1) * 128], p[:, c:c + 128], rsb_sb, ALU.mult),
            waits=[ENTM, RSB] if ht == 0 else (), inc=s_v)
    ENT = s_v.mark()
    for k in ("0a", "0b", "1a", "1b"):
        P.ps_done(k, ENT)
    if "entT" in DEBUG_TAPS:
        t_ = tap("entT", (128, 2048), BF16)
        P.op("sync", lambda e, t2=t_: e.dma_start(out=t2, in_=entT), waits=[ENT],
             inc=s_st, inc_n=16)
    ckw = [ENT, CONSTS, WKV] + P.ps_take("2a")
    P.op("tensor", lambda e: e.matmul(PS[2][:, 0:128], brow[0:1, 1024:1152],
                                      onesr[0:1, 0:128], start=True, stop=False),
         waits=ckw)
    for hc in range(16):
        P.op("tensor", lambda e, h=hc: e.matmul(
            PS[2][:, 0:128], wk_sb[:, h * 128:(h + 1) * 128],
            entT[:, h * 128:(h + 1) * 128], start=False, stop=(h == 15)),
            inc=s_t if hc == 15 else None)
    CKM = s_t.mark()
    P.op("scalar", lambda e: e.activation(ckn_sb, PS[2][:, 0:128], AF.Square),
         waits=[CKM], inc=s_s)
    CKSQ = s_s.mark()
    P.op("tensor", lambda e: e.matmul(PS[3][0:1, 0:128], onesc, ckn_sb,
                                      start=True, stop=True),
         waits=[CKSQ] + P.ps_take("3a"), inc=s_t)
    CKSS = s_t.mark()
    P.op("scalar", lambda e: e.activation(ckr_sb, PS[3][0:1, 0:128], AF.Ln,
                                          bias=EPS_RMS, scale=1.0 / 128),
         waits=[CKSS], inc=s_s)
    CKLN = s_s.mark()
    P.ps_done("3a", CKLN)
    P.op("scalar", lambda e: e.activation(ckr_sb, ckr_sb, AF.Exp, scale=-0.5),
         waits=[CKLN], inc=s_s)
    CKIV = s_s.mark()
    P.op("tensor", lambda e: e.matmul(PS[3][:, 128:256], onesr[0:1, 0:128],
                                      ckr_sb, start=True, stop=True),
         waits=[CKIV] + P.ps_take("3a"), inc=s_t)
    P.op("vector", lambda e: e.tensor_copy(invb_sb, PS[3][:, 128:256]),
         waits=[s_t.mark()], inc=s_v)
    P.ps_done("3a", s_v.mark())
    CKBC = s_v.mark()
    P.op("vector", lambda e: e.scalar_tensor_tensor(
        ckn_sb, PS[2][:, 0:128], knc_sb, invb_sb, ALU.mult, ALU.mult),
        waits=[CKBC], inc=s_v)
    P.ps_done("2a", s_v.mark())
    CKN = s_v.mark()
    P.op("vector", lambda e: e.tensor_copy(tmpR[:, 0:128], ckn_sb[32:64, :]),
         waits=[CKN], inc=s_v)
    C2 = s_v.mark()
    P.op("vector", lambda e: e.tensor_tensor(tmpR[:, 128:256], ckn_sb[0:32, :],
                                             cosb_sb, ALU.mult), waits=[CKN], inc=s_v)
    P.op("vector", lambda e: e.tensor_tensor(tmpR[:, 256:384], tmpR[:, 0:128],
                                             sinb_sb, ALU.mult), waits=[C2], inc=s_v)
    CKR1 = s_v.mark()
    P.op("vector", lambda e: e.tensor_tensor(
        ckT_sb[0:32, :], tmpR[:, 128:256], tmpR[:, 256:384], ALU.subtract),
        waits=[CKR1], inc=s_v)
    SUB1 = s_v.mark()
    P.op("vector", lambda e: e.tensor_tensor(tmpR[:, 128:256], ckn_sb[0:32, :],
                                             sinb_sb, ALU.mult), waits=[SUB1], inc=s_v)
    P.op("vector", lambda e: e.tensor_tensor(tmpR[:, 256:384], tmpR[:, 0:128],
                                             cosb_sb, ALU.mult), waits=[SUB1], inc=s_v)
    CKR2 = s_v.mark()
    P.op("vector", lambda e: e.tensor_tensor(
        tmpo, tmpR[:, 128:256], tmpR[:, 256:384], ALU.add), waits=[CKR2], inc=s_v)
    CKO2 = s_v.mark()
    P.op("vector", lambda e: e.tensor_copy(ckT_sb[32:64, :], tmpo),
         waits=[CKO2], inc=s_v)
    P.op("vector", lambda e: e.tensor_copy(ckT_sb[64:128, :], ckn_sb[64:128, :]),
         waits=[CKN], inc=s_v)
    CKT = s_v.mark()
    cvw = [ENT, WKV] + P.ps_take("2b")
    P.op("tensor", lambda e: e.matmul(PS[2][:, 512:640], onesr[0:1, 0:128],
                                      brow[0:1, 1152:1280], start=True, stop=False),
         waits=cvw)
    for hc in range(16):
        P.op("tensor", lambda e, h=hc: e.matmul(
            PS[2][:, 512:640], entT[:, h * 128:(h + 1) * 128],
            wv_sb[:, h * 128:(h + 1) * 128], start=False, stop=(h == 15)),
            inc=s_t if hc == 15 else None)
    CVM = s_t.mark()
    P.op("scalar", lambda e: e.activation(cv_sb, PS[2][:, 512:640], AF.Copy),
         waits=[CVM], inc=s_s)
    P.ps_done("2b", s_s.mark())
    CV = s_s.mark()
    if "ckT" in DEBUG_TAPS:
        t_ = tap("ckT", (128, 128), BF16)
        P.op("sync", lambda e, t2=t_: e.dma_start(out=t2, in_=ckT_sb), waits=[CKT],
             inc=s_st, inc_n=16)
    if "cv" in DEBUG_TAPS:
        t_ = tap("cv", (128, 128), BF16)
        P.op("sync", lambda e, t2=t_: e.dma_start(out=t2, in_=cv_sb), waits=[CV],
             inc=s_st, inc_n=16)

    qbw = [ENTM, CONSTS] + P.ps_take("3b")
    P.op("tensor", lambda e: e.matmul(PS[3][:, 512:640], onesr[0:1, 0:128], qnr_sb,
                                      start=True, stop=True), waits=qbw)
    P.op("tensor", lambda e: e.matmul(PS[3][:, 640:768], onesr[0:1, 0:128], knr_sb,
                                      start=True, stop=True), inc=s_t)
    QNB = s_t.mark()
    P.op("vector", lambda e: e.tensor_copy(qnwbc, PS[3][:, 512:640]), waits=[QNB],
         inc=s_v)
    P.op("vector", lambda e: e.tensor_copy(knwbc, PS[3][:, 640:768]), inc=s_v)
    P.ps_done("3b", s_v.mark())
    QNW = s_v.mark()
    P.op("vector", lambda e: e.memset(qtab2, 0.0),
         waits=[QNW, CONSTS, CKT, CV], inc=s_v)
    P.op("vector", lambda e: e.memset(ktab2, 0.0), inc=s_v)
    P.op("vector", lambda e: e.memset(ktab2t8, 0.0), inc=s_v)
    P.op("vector", lambda e: e.tensor_scalar(
        negq, qnwbc[:, 32:64], -1.0, None, ALU.mult), inc=s_v)
    P.op("vector", lambda e: e.tensor_scalar(
        negk, knwbc[:, 32:64], -1.0, None, ALU.mult), inc=s_v)
    Z0 = s_v.mark()
    for t in range(8):
        P.op("vector", lambda e, t_=t: e.tensor_tensor(
            qtab[:, t_ * 128:t_ * 128 + 32], cosq_sb[:, t_ * 32:(t_ + 1) * 32],
            qnwbc[:, 0:32], ALU.mult), waits=[Z0] if t == 0 else ())
        P.op("vector", lambda e, t_=t: e.tensor_tensor(
            qtab[:, t_ * 128 + 32:t_ * 128 + 64],
            cosq_sb[:, t_ * 32:(t_ + 1) * 32], qnwbc[:, 32:64], ALU.mult))
        P.op("vector", lambda e, t_=t: e.tensor_copy(
            qtab[:, t_ * 128 + 64:t_ * 128 + 128], qnwbc[:, 64:128]))
        P.op("vector", lambda e, t_=t: e.tensor_tensor(
            qtab2[:, t_ * 128:t_ * 128 + 32], sinq_sb[:, t_ * 32:(t_ + 1) * 32],
            negq, ALU.mult))
        P.op("vector", lambda e, t_=t: e.tensor_tensor(
            qtab2[:, t_ * 128 + 32:t_ * 128 + 64],
            sinq_sb[:, t_ * 32:(t_ + 1) * 32], qnwbc[:, 0:32], ALU.mult))
    for t in range(9):
        P.op("vector", lambda e, t_=t: e.tensor_tensor(
            ktab[:, t_ * 128:t_ * 128 + 32], cosk_sb[:, t_ * 32:(t_ + 1) * 32],
            knwbc[:, 0:32], ALU.mult))
        P.op("vector", lambda e, t_=t: e.tensor_tensor(
            ktab[:, t_ * 128 + 32:t_ * 128 + 64],
            cosk_sb[:, t_ * 32:(t_ + 1) * 32], knwbc[:, 32:64], ALU.mult))
        P.op("vector", lambda e, t_=t: e.tensor_copy(
            ktab[:, t_ * 128 + 64:t_ * 128 + 128], knwbc[:, 64:128]))
        P.op("vector", lambda e, t_=t: e.tensor_tensor(
            kt2tile(t_)[:, 0:32], sink_sb[:, t_ * 32:(t_ + 1) * 32],
            negk, ALU.mult))
        P.op("vector", lambda e, t_=t: e.tensor_tensor(
            kt2tile(t_)[:, 32:64],
            sink_sb[:, t_ * 32:(t_ + 1) * 32], knwbc[:, 0:32], ALU.mult),
            inc=s_v if t == 8 else None)
    TABS = s_v.mark()

    def kt_off(kt):
        return 1024 if kt == 0 else (kt - 1) * 128

    wbufs = [wbuf_bf, aC[:, 0:8192]]
    wbuf_free = {}
    bias_free = {}
    ATT0 = [TABS, CV, CKT, W_PRE]
    for g in range(4):
        WLDs = {}
        BLDs = {}
        for wi, (Wsrc, bsrc) in enumerate([(Wq, bq), (Wlk, blk)]):
            wslot = (wi + g) % 2
            wbuf = wbufs[wslot]
            wlw = list(ATT0) if (g == 0) else []
            if wslot in wbuf_free:
                wlw.append(wbuf_free[wslot])
            P.op("gpsimd", lambda e, W=Wsrc, g_=g, wb=wbuf: e.dma_start(
                out=wb.rearrange("p (c f) -> p c f", c=16),
                in_=W[:, g_ * 512:(g_ + 1) * 512].rearrange(
                    "(c p) f -> p c f", p=128)), waits=wlw, inc=s_w, inc_n=16)
            WLDs[wi] = s_w.mark()
            bld_w = [bias_free[wslot]] if wslot in bias_free else []
            ld(brow[0:1, wslot * 512:wslot * 512 + 512],
               bsrc[0:1, g * 512:(g + 1) * 512], waits=bld_w)
            BLDs[wi] = s_c.mark()

        jobs = [(0, t) for t in range(8)] + [(1, t) for t in range(9)]
        NJ = len(jobs)
        MM = {}; SQm = {}; SS = {}; INV = {}; APP = {}
        ROPE = {}; TR = {}
        for i in range(NJ + 2):
            if i < NJ:
                wi, t = jobs[i]
                xt0 = t * 128 if wi == 0 else kt_off(t)
                key = "0a" if i % 2 == 0 else "1a"
                ps = PS[0] if i % 2 == 0 else PS[1]
                wbuf = wbufs[(wi + g) % 2]
                br = brow[0:1, ((wi + g) % 2) * 512:((wi + g) % 2) * 512 + 512]
                mmw = [WLDs[wi], BLDs[wi]] + P.ps_take(key)
                P.op("tensor", lambda e, p=ps, b2=br: e.matmul(
                    p[:, 0:512], onesr[0:1, 0:128], b2,
                    start=True, stop=False), waits=mmw)
                for hc in range(16):
                    P.op("tensor", lambda e, h=hc, x0=xt0, p=ps, wb=wbuf:
                         e.matmul(p[:, 0:512],
                                  xnT[:, h * TKT + x0: h * TKT + x0 + 128],
                                  wb[:, h * 512:(h + 1) * 512],
                                  start=False, stop=(h == 15)),
                         inc=s_t if hc == 15 else None)
                MM[i] = s_t.mark()
                if i == 7:
                    wbuf_free[g % 2] = MM[i]
                    bias_free[g % 2] = MM[i]
                if i == NJ - 1:
                    wbuf_free[1 - g % 2] = MM[i]
                    bias_free[1 - g % 2] = MM[i]
            if i >= 2 and i - 2 < NJ:
                j = i - 2
                trw = [ROPE[j]] + P.ps_take("2t")
                for h4 in range(4):
                    P.op("tensor", lambda e, h=h4: e.transpose(
                        PSbf[2][:, h * 128:(h + 1) * 128],
                        q4r[:, h * 128:(h + 1) * 128], id_sb),
                        waits=trw if h4 == 0 else (),
                        inc=s_t if h4 == 3 else None)
                TR[j] = s_t.mark()
            if i < NJ:
                ps = PS[0] if i % 2 == 0 else PS[1]
                sqw = [MM[i]]
                if i >= 1:
                    sqw.append(SS[i - 1])
                P.op("scalar", lambda e, p=ps: e.activation(
                    sqb, p[:, 0:512], AF.Square), waits=sqw, inc=s_s)
                SQm[i] = s_s.mark()
            if i >= 1 and i - 1 < NJ:
                j = i - 1
                sc0 = (j % 2) * 16
                P.op("scalar", lambda e, c=sc0: e.activation(
                    ssq[:, c + 4:c + 8], ssq[:, c + 0:c + 4], AF.Ln,
                    bias=EPS_RMS, scale=1.0 / 128), waits=[SS[j]], inc=s_s)
                P.op("scalar", lambda e, c=sc0: e.activation(
                    ssq[:, c + 8:c + 12], ssq[:, c + 4:c + 8], AF.Exp,
                    scale=-0.5), inc=s_s)
                INV[j] = s_s.mark()
                psj = PS[0] if j % 2 == 0 else PS[1]
                apw = [INV[j]]
                if j >= 1:
                    apw.append(ROPE[j - 1])
                for h4 in range(4):
                    P.op("scalar", lambda e, h=h4, p=psj, c=sc0: e.activation(
                        qn2[:, h * 128:(h + 1) * 128],
                        p[:, h * 128:(h + 1) * 128],
                        AF.Copy, scale=ssq[:, c + 8 + h:c + 9 + h]),
                        waits=apw if h4 == 0 else (),
                        inc=s_s if h4 == 3 else None)
                APP[j] = s_s.mark()
                P.ps_done("0a" if j % 2 == 0 else "1a", APP[j])
            if i < NJ:
                sc0 = (i % 2) * 16
                P.op("vector", lambda e, c=sc0: e.reduce_sum(
                    ssq[:, c:c + 4].rearrange("p (a b) -> p a b", a=4),
                    sqb.rearrange("p (a b) -> p a b", a=4), axis=AX.X),
                    waits=[SQm[i]], inc=s_v)
                SS[i] = s_v.mark()
            if i >= 1 and i - 1 < NJ:
                j = i - 1
                wiJ, tJ = jobs[j]
                tabsel = qtab if wiJ == 0 else ktab
                t2tile = (qtab2[:, tJ * 128:(tJ + 1) * 128] if wiJ == 0
                          else kt2tile(tJ))
                tab0 = tJ * 128
                rpw = [APP[j]]
                if j >= 1:
                    rpw.append(TR[j - 1])
                qn3 = qn2.rearrange("p (h s q) -> p h s q", h=4, s=4)
                sw3 = swb.rearrange("p (h s q) -> p h s q", h=4, s=4)
                qn2h = qn2.rearrange("p (h s q) -> p h s q", h=4, s=2)
                sw2h = swb.rearrange("p (h s q) -> p h s q", h=4, s=2)
                P.op("vector", lambda e, a=sw3, b=qn3: e.tensor_copy(
                    a[:, :, 0, :], b[:, :, 1, :]), waits=rpw, inc=s_v)
                P.op("vector", lambda e, a=sw3, b=qn3: e.tensor_copy(
                    a[:, :, 1, :], b[:, :, 0, :]), inc=s_v)
                P.op("vector", lambda e, a=sw2h, b=qn2h: e.tensor_copy(
                    a[:, :, 1, :], b[:, :, 1, :]), inc=s_v)
                CPm = s_v.mark()
                for h4 in range(4):
                    c0 = h4 * 128
                    P.op("vector", lambda e, c=c0, t0=tab0, tb=tabsel:
                         e.tensor_tensor(rtmp1[:, c:c + 128], qn2[:, c:c + 128],
                                         tb[:, t0:t0 + 128], ALU.mult),
                         inc=s_v)
                for h4 in range(4):
                    c0 = h4 * 128
                    P.op("vector", lambda e, c=c0, tb=t2tile:
                         e.tensor_tensor(swb[:, c:c + 128], swb[:, c:c + 128],
                                         tb, ALU.mult),
                         waits=[CPm] if h4 == 0 else (), inc=s_v)
                Mm = s_v.mark()
                for h4 in range(4):
                    c0 = h4 * 128
                    P.op("vector", lambda e, c=c0: e.tensor_tensor(
                        q4r[:, c:c + 128], rtmp1[:, c:c + 128],
                        swb[:, c:c + 128], ALU.add),
                        waits=[Mm] if h4 == 0 else (), inc=s_v)
                ROPE[j] = s_v.mark()
            if i >= 2 and i - 2 < NJ:
                j = i - 2
                wiJ, tJ = jobs[j]
                dstT = qT4 if wiJ == 0 else lkT4
                dW = 1024 if wiJ == 0 else 1152
                P.op("vector", lambda e, it_=tJ, d=dstT, w_=dW: e.tensor_copy(
                    d.rearrange("p (h t) -> p h t", t=w_)[
                        :, :, it_ * 128:(it_ + 1) * 128],
                    PSbf[2][:, 0:512].rearrange("p (h t) -> p h t", h=4)),
                    waits=[TR[j]], inc=s_v)
                P.ps_done("2t", s_v.mark())

        sv_ = g % 2
        wlw = [wbuf_free[sv_]]
        P.op("gpsimd", lambda e, g_=g, wb=wbufs[sv_]: e.dma_start(
            out=wb.rearrange("p (c f) -> p c f", c=16),
            in_=Wlv[:, g_ * 512:(g_ + 1) * 512].rearrange(
                "(c p) f -> p c f", p=128)), waits=wlw, inc=s_w, inc_n=16)
        WLDv = s_w.mark()
        ld(brow[0:1, sv_ * 512:sv_ * 512 + 512],
           blv[0:1, g * 512:(g + 1) * 512], waits=[bias_free[sv_]])
        BLDv = s_c.mark()
        wbuf = wbufs[sv_]
        for it in range(9):
            xt0 = kt_off(it)
            key = "0a" if it % 2 == 0 else "1a"
            ps = PS[0] if it % 2 == 0 else PS[1]
            mmw = [WLDv, BLDv] + P.ps_take(key)
            P.op("tensor", lambda e, p=ps, b2=brow[0:1, sv_ * 512:sv_ * 512 + 512]:
                 e.matmul(p[:, 0:512], onesr[0:1, 0:128], b2,
                          start=True, stop=False), waits=mmw)
            for hc in range(16):
                P.op("tensor", lambda e, h=hc, x0=xt0, p=ps, wb=wbuf:
                     e.matmul(p[:, 0:512],
                              xnT[:, h * TKT + x0: h * TKT + x0 + 128],
                              wb[:, h * 512:(h + 1) * 512],
                              start=False, stop=(h == 15)),
                     inc=s_t if hc == 15 else None)
            MMv = s_t.mark()
            P.op("vector", lambda e, it_=it, p=ps: e.tensor_copy(
                lv4.rearrange("p (h i d) -> p h i d", h=4, i=9)[:, :, it_, :],
                p[:, 0:512].rearrange("p (h d) -> p h d", h=4)),
                waits=[MMv], inc=s_v)
            P.ps_done(key, s_v.mark())
        wbuf_free[sv_] = s_t.mark()
        bias_free[sv_] = wbuf_free[sv_]
        PROJ = s_v.mark()
        if g == TAP_GROUP and "qT4" in DEBUG_TAPS:
            t_ = tap("qT4", (128, 4096), BF16)
            P.op("sync", lambda e, t2=t_: e.dma_start(out=t2, in_=qT4),
                 waits=[PROJ], inc=s_st, inc_n=16)
            t_ = tap("lkT4", (128, 4608), BF16)
            P.op("sync", lambda e, t2=t_: e.dma_start(out=t2, in_=lkT4),
                 waits=[PROJ], inc=s_st, inc_n=16)
            t_ = tap("lv4", (128, 4608), BF16)
            P.op("sync", lambda e, t2=t_: e.dma_start(out=t2, in_=lv4),
                 waits=[PROJ], inc=s_st, inc_n=16)
            ATT0.append(s_st.mark())

        SCM = {}; MSKD = {}; EXD = {}; SUMS = {}; SCL = {}; MX = {}
        TRp = {}; EVp = {}; PVm = {}; CTXE = {}
        NPAIR = 16
        def pr_h(p):
            return p // 4
        def pr_qt(p):
            return (p % 4) * 2
        for i in range(NPAIR + 3):
            if i < NPAIR:
                h4 = pr_h(i); qt = pr_qt(i); h = g * 4 + h4
                psL = PS[0][:, 512:1024] if i % 2 == 0 else PS[3][:, 0:512]
                psC = PS[1][:, 512:1024] if i % 2 == 0 else PS[3][:, 512:1024]
                kL = "0b" if i % 2 == 0 else "3a"
                kC = "1b" if i % 2 == 0 else "3b"
                scw = [PROJ] + P.ps_take(kL, kC)
                for k2 in range(2):
                    P.op("tensor", lambda e, h_=h4, q_=qt + k2, p=psL, o=k2 * 256:
                         e.matmul(p[:, o:o + 256],
                                  qT4[:, h_ * 1024 + q_ * 128: h_ * 1024 + (q_ + 1) * 128],
                                  lkT4[:, h_ * 1152 + q_ * 128: h_ * 1152 + q_ * 128 + 256],
                                  start=True, stop=True),
                         waits=scw if k2 == 0 else ())
                    P.op("tensor", lambda e, h_=h4, q_=qt + k2, p=psC, o=k2 * 256:
                         e.matmul(p[:, o:o + 128],
                                  qT4[:, h_ * 1024 + q_ * 128: h_ * 1024 + (q_ + 1) * 128],
                                  ckT_sb, start=True, stop=False))
                    P.op("tensor", lambda e, hh=h, h_=h4, q_=qt + k2, p=psC, o=k2 * 256:
                         e.matmul(p[:, o + 128:o + 132],
                                  qT4[:, h_ * 1024 + q_ * 128: h_ * 1024 + (q_ + 1) * 128],
                                  sinkTk_sb[:, hh:hh + 4], start=False, stop=True),
                         inc=s_t if k2 == 1 else None)
                SCM[i] = s_t.mark()
            if i >= 1 and i - 1 < NPAIR:
                j = i - 1
                mb = 8 + (j % 2) * 12
                psL = PS[0][:, 512:1024] if j % 2 == 0 else PS[3][:, 0:512]
                psC = PS[1][:, 512:1024] if j % 2 == 0 else PS[3][:, 512:1024]
                mxw = [SCM[j]]
                if j >= 2:
                    mxw.append(EXD[j - 2])
                P.op("vector", lambda e, p=psL, b=mb: e.reduce_max(
                    rrp[:, b:b + 2].rearrange("p (a b2) -> p a b2", a=2),
                    p.rearrange("p (a b2) -> p a b2", a=2), axis=AX.X,
                    negate=True), waits=mxw, inc=s_v)
                P.op("vector", lambda e, p=psC, b=mb: e.reduce_max(
                    rrp[:, b + 2:b + 4].rearrange("p (a b2) -> p a b2", a=2),
                    p.rearrange("p (a b2) -> p a b2", a=2)[:, :, 0:132], axis=AX.X,
                    negate=True), inc=s_v)
                P.op("vector", lambda e, b=mb: e.tensor_scalar(
                    rrp[:, b + 4:b + 8], rrp[:, b:b + 4], SCALE, None, ALU.mult),
                    inc=s_v)
                MX[j] = s_v.mark()
            if i >= 3 and i - 3 < NPAIR:
                j = i - 3
                evw = [TRp[j]]
                if j >= 1:
                    evw.append(PVm[j - 1])
                P.op("scalar", lambda e: e.activation(pTl, PSbf[2][:, 0:512],
                                                      AF.Copy), waits=evw, inc=s_s)
                P.op("scalar", lambda e: e.activation(pTc, PSbf[2][:, 512:768],
                                                      AF.Copy), inc=s_s)
                P.op("scalar", lambda e: e.activation(pTs[0:1, :],
                                                      PSbf[2][0:1, 768:1024],
                                                      AF.Copy), inc=s_s)
                EVp[j] = s_s.mark()
                P.ps_done("2t", EVp[j])
            if i >= 2 and i - 2 < NPAIR:
                j = i - 2
                trw = [SCL[j]] + P.ps_take("2t")
                pljj = pl2[(j % 2)]
                pcjj = pc2[(j % 2)]
                for k2 in range(4):
                    P.op("tensor", lambda e, k=k2, pj=pljj: e.transpose(
                        PSbf[2][:, k * 128:(k + 1) * 128],
                        pj[:, k * 128:(k + 1) * 128], id_sb),
                        waits=trw if k2 == 0 else ())
                for k2 in range(2):
                    P.op("tensor", lambda e, k=k2, pj=pcjj: e.transpose(
                        PSbf[2][:, 512 + k * 128: 512 + (k + 1) * 128],
                        pj[:, k * 132:k * 132 + 128], id_sb))
                for k2 in range(2):
                    P.op("tensor", lambda e, k=k2, pj=pcjj: e.transpose(
                        PSbf[2][0:1, 768 + k * 128: 768 + (k + 1) * 128],
                        pj[:, k * 132 + 128:k * 132 + 129], id_sb),
                        inc=s_t if k2 == 1 else None)
                TRp[j] = s_t.mark()
            if i >= 3 and i - 3 < NPAIR:
                j = i - 3
                h4 = pr_h(j); qt = pr_qt(j); h = g * 4 + h4
                cx0 = 512 if j % 2 == 0 else 768
                pvw = [EVp[j]] + P.ps_take("ctx_e" if j % 2 == 0 else "ctx_o")
                for k2 in range(2):
                    q_ = qt + k2
                    pcx = PS[2][:, cx0 + k2 * 128: cx0 + (k2 + 1) * 128]
                    P.op("tensor", lambda e, h_=h4, q2=q_, p=pcx, k=k2: e.matmul(
                        p, lv4[:, h_ * 1152 + q2 * 128: h_ * 1152 + (q2 + 1) * 128],
                        pTl[:, k * 256:k * 256 + 128], start=True, stop=False),
                        waits=pvw if k2 == 0 else ())
                    P.op("tensor", lambda e, h_=h4, q2=q_, p=pcx, k=k2: e.matmul(
                        p, lv4[:, h_ * 1152 + (q2 + 1) * 128: h_ * 1152 + (q2 + 2) * 128],
                        pTl[:, k * 256 + 128:k * 256 + 256], start=False, stop=False))
                    P.op("tensor", lambda e, p=pcx, k=k2: e.matmul(
                        p, cv_sb, pTc[:, k * 128:(k + 1) * 128],
                        start=False, stop=False))
                    P.op("tensor", lambda e, hh=h, p=pcx, k=k2: e.matmul(
                        p, sinkv_sb[0:1, hh * 128:(hh + 1) * 128],
                        pTs[0:1, k * 128:(k + 1) * 128], start=False, stop=True),
                        inc=s_t if k2 == 1 else None)
                PVm[j] = s_t.mark()
            if i >= 1 and i - 1 < NPAIR:
                j = i - 1
                psL = PS[0][:, 512:1024] if j % 2 == 0 else PS[3][:, 0:512]
                psC = PS[1][:, 512:1024] if j % 2 == 0 else PS[3][:, 512:1024]
                plj = pl2[(j % 2)]
                pcj = pc2[(j % 2)]
                mb = 8 + (j % 2) * 12
                exw = [SCM[j], MX[j]]
                if j >= 2:
                    exw.append(SCL[j - 2])
                for k2 in range(2):
                    P.op("scalar", lambda e, p=psL, pj=plj, k=k2, b=mb: e.activation(
                        pj[:, k * 256:(k + 1) * 256], p[:, k * 256:(k + 1) * 256],
                        AF.Exp, scale=SCALE, bias=rrp[:, b + 4 + k:b + 5 + k]),
                        waits=exw if k2 == 0 else (), inc=s_s)
                    P.op("scalar", lambda e, p=psC, pj=pcj, k=k2, b=mb: e.activation(
                        pj[:, k * 132:(k + 1) * 132],
                        p[:, k * 256:k * 256 + 132],
                        AF.Exp, scale=SCALE, bias=rrp[:, b + 6 + k:b + 7 + k]),
                        inc=s_s)
                EXD[j] = s_s.mark()
                P.ps_done("0b" if j % 2 == 0 else "3a", EXD[j])
                P.ps_done("1b" if j % 2 == 0 else "3b", EXD[j])
            if i >= 1 and i - 1 < NPAIR:
                j = i - 1
                qt = pr_qt(j)
                plj = pl2[(j % 2)]
                pcj = pc2[(j % 2)]
                mk = mfirstM_sb if qt == 0 else mrestM_sb
                P.op("vector", lambda e, pj=plj, m=mk: e.tensor_tensor(
                    pj, pj, m, ALU.mult), waits=[EXD[j]], inc=s_v)
                P.op("vector", lambda e, pj=pcj, q_=qt: e.tensor_tensor(
                    pj, pj, cmaskM_sb[:, q_ * CW:(q_ + 2) * CW], ALU.mult), inc=s_v)
                MSKD[j] = s_v.mark()
                P.op("vector", lambda e, pj=plj: e.reduce_sum(
                    rrp[:, 0:2].rearrange("p (a b) -> p a b", a=2),
                    pj.rearrange("p (a b) -> p a b", a=2), axis=AX.X),
                    waits=[MSKD[j]], inc=s_v)
                P.op("vector", lambda e, pj=pcj: e.reduce_sum(
                    rrp[:, 2:4].rearrange("p (a b) -> p a b", a=2),
                    pj.rearrange("p (a b) -> p a b", a=2), axis=AX.X), inc=s_v)
                SUMS[j] = s_v.mark()
                P.op("vector", lambda e: e.reciprocal(rrp[:, 4:8], rrp[:, 0:4]),
                     waits=[SUMS[j]], inc=s_v)
                RCm = s_v.mark()
                for k2 in range(2):
                    P.op("vector", lambda e, pj=plj, k=k2: e.tensor_scalar(
                        pj[:, k * 256:(k + 1) * 256], pj[:, k * 256:(k + 1) * 256],
                        rrp[:, 4 + k:5 + k], 0.5, ALU.mult, ALU.mult),
                        waits=[RCm] if k2 == 0 else (), inc=s_v)
                    P.op("vector", lambda e, pj=pcj, k=k2: e.tensor_scalar(
                        pj[:, k * 132:(k + 1) * 132], pj[:, k * 132:(k + 1) * 132],
                        rrp[:, 6 + k:7 + k], 0.5, ALU.mult, ALU.mult), inc=s_v)
                SCL[j] = s_v.mark()
            if i >= 3 and i - 3 < NPAIR:
                j = i - 3
                h4 = pr_h(j); qt = pr_qt(j); h = g * 4 + h4
                cx0 = 512 if j % 2 == 0 else 768
                P.op("vector", lambda e, hh=h, q_=qt, c=cx0: e.tensor_copy(
                    ctxT[:, hh * 1024 + q_ * 128: hh * 1024 + (q_ + 2) * 128],
                    PS[2][:, c:c + 256]), waits=[PVm[j]], inc=s_v)
                CTXE[j] = s_v.mark()
                P.ps_done("ctx_e" if j % 2 == 0 else "ctx_o", CTXE[j])
    CTX = s_v.mark()
    if "ctxT" in DEBUG_TAPS:
        t_ = tap("ctxT", (128, 16 * 1024), BF16)
        P.op("sync", lambda e, t2=t_: e.dma_start(out=t2, in_=ctxT), waits=[CTX],
             inc=s_st, inc_n=16)

    bis = {"k": 0, "cur": None}

    def bis_init():
        P.op("vector", lambda e: e.memset(bst[0:8, 0:1], 0.0),
             waits=[BIS_LD, CTX], inc=s_v)
        P.op("vector", lambda e: e.memset(bst[0:8, 1:2], 1.1), inc=s_v)
        P.op("vector", lambda e: e.memset(bst[0:8, 2:3], 0.55), inc=s_v)
        P.op("vector", lambda e: e.memset(mid128[:, 0:1], 0.55), inc=s_v)
        bis["cur"] = s_v.mark()

    def bis_iter():
        it = bis["k"]
        bis["k"] += 1
        a = (it % 2) * 8
        b_ = 8 - a
        m128 = mid128[:, it % 2:it % 2 + 1]
        m128n = mid128[:, 1 - it % 2:2 - it % 2]
        P.op("vector", lambda e, m=m128: e.tensor_scalar(
            trashA, bis_sb, m, 0.0, ALU.is_ge, ALU.add, accum_out=cnt128),
            waits=[bis["cur"]], inc=s_v)
        CN = s_v.mark()
        P.op("tensor", lambda e: e.matmul(PS[3][0:8, 256:257], m8_sb, cnt128,
                                          start=True, stop=True),
             waits=[CN] + P.ps_take("3a"), inc=s_t)
        CM = s_t.mark()
        P.op("vector", lambda e, aa=a: e.tensor_scalar(
            bst[0:8, aa + 3:aa + 4], PS[3][0:8, 256:257], KCAP, None, ALU.is_ge),
            waits=[CM], inc=s_v)
        P.ps_done("3a", s_v.mark())
        P.op("vector", lambda e, aa=a: e.tensor_tensor(
            bst[0:8, aa + 4:aa + 5], bst[0:8, aa + 2:aa + 3],
            bst[0:8, aa + 0:aa + 1], ALU.subtract), waits=[s_v.mark()], inc=s_v)
        P.op("vector", lambda e, aa=a: e.tensor_tensor(
            bst[0:8, aa + 5:aa + 6], bst[0:8, aa + 1:aa + 2],
            bst[0:8, aa + 2:aa + 3], ALU.subtract), inc=s_v)
        DD = s_v.mark()
        P.op("vector", lambda e, aa=a, bb=b_: e.scalar_tensor_tensor(
            bst[0:8, bb + 0:bb + 1], bst[0:8, aa + 3:aa + 4],
            bst[0:8, aa + 4:aa + 5], bst[0:8, aa + 0:aa + 1], ALU.mult, ALU.add),
            waits=[DD], inc=s_v)
        P.op("vector", lambda e, aa=a, bb=b_: e.scalar_tensor_tensor(
            bst[0:8, bb + 1:bb + 2], bst[0:8, aa + 3:aa + 4],
            bst[0:8, aa + 5:aa + 6], bst[0:8, aa + 2:aa + 3], ALU.mult, ALU.add),
            inc=s_v)
        LH = s_v.mark()
        P.op("vector", lambda e, bb=b_: e.tensor_tensor(
            bst[0:8, bb + 6:bb + 7], bst[0:8, bb + 1:bb + 2],
            bst[0:8, bb + 0:bb + 1], ALU.subtract), waits=[LH], inc=s_v)
        DD2 = s_v.mark()
        P.op("vector", lambda e, bb=b_: e.scalar_tensor_tensor(
            bst[0:8, bb + 2:bb + 3], bst[0:8, bb + 6:bb + 7], 0.5,
            bst[0:8, bb + 0:bb + 1], ALU.mult, ALU.add), waits=[DD2], inc=s_v)
        MD = s_v.mark()
        if it + 1 < BISECT_ITERS:
            P.op("tensor", lambda e, bb=b_: e.matmul(
                PS[3][:, 384:385], seg16T_sb, bst[0:8, bb + 2:bb + 3],
                start=True, stop=True), waits=[MD] + P.ps_take("3a"), inc=s_t)
            MB = s_t.mark()
            P.op("vector", lambda e, m=m128n: e.tensor_copy(m, PS[3][:, 384:385]),
                 waits=[MB], inc=s_v)
            P.ps_done("3a", s_v.mark())
            bis["cur"] = s_v.mark()
        else:
            bis["cur"] = MD

    bis_init()
    prev_ev = {}
    prev_mmdone = None
    mmdone = {}
    chunk_i = 0
    for oc in range(4):
        woslot = oc % 2
        wosb = wbufs[woslot]
        wol = [wbuf_free[woslot]] if oc < 2 else [mmdone[oc - 2]]
        P.op("gpsimd", lambda e, o=oc, wb=wosb: e.dma_start(
            out=wb.rearrange("p (c f) -> p c f", c=16),
            in_=Wo[:, o * 512:(o + 1) * 512].rearrange(
                "(c p) f -> p c f", p=128)), waits=wol, inc=s_w, inc_n=16)
        WOL = s_w.mark()
        if oc == 0:
            ld(borow, bo)
        BOL = s_c.mark()
        for tt in range(8):
            sl4 = tt % 4
            key = f"{sl4 // 2}{'a' if sl4 % 2 == 0 else 'b'}"
            ps8 = PS[sl4 // 2][:, (sl4 % 2) * 512:(sl4 % 2) * 512 + 512]
            bmw = [WOL, BOL, CTX] + P.ps_take(key)
            P.op("tensor", lambda e, p=ps8, o=oc: e.matmul(
                p, onesr[0:1, 0:128], borow[0:1, o * 512:(o + 1) * 512],
                start=True, stop=False), waits=bmw)
            for dc in range(16):
                P.op("tensor", lambda e, d=dc, t=tt, p=ps8, wb=wosb: e.matmul(
                    p, ctxT[:, d * 1024 + t * 128: d * 1024 + (t + 1) * 128],
                    wb[:, d * 512:(d + 1) * 512], start=False, stop=(d == 15)),
                    inc=s_t if dc == 15 else None)
            WM = s_t.mark()
            evw = [WM]
            if (tt % 2) in prev_ev:
                evw.append(prev_ev[tt % 2])
            P.op("scalar", lambda e, p=ps8, sl_=tt % 2: e.activation(
                wo_stage[:, sl_ * 512:(sl_ + 1) * 512], p, AF.Copy),
                waits=evw, inc=s_s)
            EV = s_s.mark()
            P.ps_done(key, EV)
            P.op("sync", lambda e, t=tt, o=oc, sl_=tt % 2: e.dma_start(
                out=ao_d[t * 128:(t + 1) * 128, o * 512:(o + 1) * 512],
                in_=wo_stage[:, sl_ * 512:(sl_ + 1) * 512]), waits=[EV],
                inc=s_st, inc_n=16)
            prev_ev[tt % 2] = s_st.mark()
            chunk_i += 1
            while bis["k"] < min(BISECT_ITERS, int(chunk_i * 1.4)):
                bis_iter()
        mmdone[oc] = s_t.mark()
        prev_mmdone = mmdone[oc]
    while bis["k"] < BISECT_ITERS:
        bis_iter()
    AO_ST = s_st.mark()
    if "ao" in DEBUG_TAPS:
        t_ = tap("ao", (TOK, HID))
        P.op("sync", lambda e, t2=t_: e.dma_start(out=t2, in_=ao_d),
             waits=[AO_ST], inc=s_st, inc_n=16)

    fin = (BISECT_ITERS % 2) * 8
    P.op("vector", lambda e, ff=fin: e.scalar_tensor_tensor(
        sT_row[0:8, :], smt_sb[0:8, :], bst[0:8, ff:ff + 1], smt_sb[0:8, :],
        ALU.is_ge, ALU.mult), waits=[bis["cur"]], inc=s_v)
    STR = s_v.mark()
    sclw = [STR] + P.ps_take("3b")
    for tt in range(8):
        P.op("tensor", lambda e, t=tt: e.matmul(
            PS[3][:, 512 + t:513 + t], sT_row[0:8, t * 128:(t + 1) * 128],
            ones8, start=True, stop=True),
            waits=sclw if tt == 0 else (), inc=s_t if tt == 7 else None)
    SCM2 = s_t.mark()
    P.op("vector", lambda e: e.tensor_copy(s_col, PS[3][:, 512:520]),
         waits=[SCM2], inc=s_v)
    P.ps_done("3b", s_v.mark())
    SCOL = s_v.mark()
    if "scol" in DEBUG_TAPS:
        t_ = tap("scol", (128, 8))
        P.op("sync", lambda e, t2=t_: e.dma_start(out=t2, in_=s_col), waits=[SCOL],
             inc=s_st, inc_n=16)
    if "thr" in DEBUG_TAPS:
        t_ = tap("thr", (8, 16))
        P.op("sync", lambda e, t2=t_: e.dma_start(out=t2, in_=bst), waits=[SCOL],
             inc=s_st, inc_n=16)

    prev_sg = {}
    prev_hT = {}
    wff_free = {}
    for f in range(NFT):
        wgb = wff[(f % 2) * 2]
        wub = wff[(f % 2) * 2 + 1]
        wwl = [prev_mmdone, AO_ST] if f < 2 else [wff_free[f % 2]]
        P.op("gpsimd", lambda e, ff_=f, wb=wgb: e.dma_start(
            out=wb.rearrange("p (c f) -> p c f", c=16),
            in_=wg[:, ff_ * 128:(ff_ + 1) * 128].rearrange(
                "(c p) f -> p c f", p=128)), waits=wwl, inc=s_w, inc_n=16)
        P.op("gpsimd", lambda e, ff_=f, wb=wub: e.dma_start(
            out=wb.rearrange("p (c f) -> p c f", c=16),
            in_=wu[:, ff_ * 128:(ff_ + 1) * 128].rearrange(
                "(c p) f -> p c f", p=128)), inc=s_w, inc_n=16)
        WF = s_w.mark()
        kg = "0a0b" if f % 2 == 0 else "2a2b"
        psg = PS[0] if f % 2 == 0 else PS[2]
        psu = PS[1] if f % 2 == 0 else PS[3]
        kga, kgb = (("0a", "0b") if f % 2 == 0 else ("2a", "2b"))
        kua, kub = (("1a", "1b") if f % 2 == 0 else ("3a", "3b"))
        gw = [WF, XNT_OWN] + P.ps_take(kga, kgb)
        for n2 in range(2):
            for hc in range(16):
                P.op("tensor", lambda e, h=hc, n=n2, p=psg, wb=wgb: e.matmul(
                    p[:, n * 512:(n + 1) * 512], wb[:, h * 128:(h + 1) * 128],
                    xnT[:, h * TKT + n * 512: h * TKT + (n + 1) * 512],
                    start=(h == 0), stop=(h == 15)),
                    waits=gw if (n2 == 0 and hc == 0) else ())
        uw = P.ps_take(kua, kub)
        for n2 in range(2):
            for hc in range(16):
                P.op("tensor", lambda e, h=hc, n=n2, p=psu, wb=wub: e.matmul(
                    p[:, n * 512:(n + 1) * 512], wb[:, h * 128:(h + 1) * 128],
                    xnT[:, h * TKT + n * 512: h * TKT + (n + 1) * 512],
                    start=(h == 0), stop=(h == 15)),
                    waits=uw if (n2 == 0 and hc == 0) else (),
                    inc=s_t if (n2 == 1 and hc == 15) else None)
        GU = s_t.mark()
        wff_free[f % 2] = GU
        sgb = sg_sb[:, (f % 2) * 1024:(f % 2 + 1) * 1024]
        sgw = [GU, CONSTS]
        if (f % 2) in prev_sg:
            sgw.append(prev_sg[f % 2])
        P.op("scalar", lambda e, ff_=f, p=psg, sg_=sgb: e.activation(
            sg_, p, AF.Silu, bias=bgc_sb[:, ff_:ff_ + 1]), waits=sgw, inc=s_s)
        SG = s_s.mark()
        P.ps_done(kga, SG); P.ps_done(kgb, SG)
        P.op("vector", lambda e, ff_=f, p=psu, sg_=sgb: e.scalar_tensor_tensor(
            hT_tile(ff_), p, buc_sb[:, ff_:ff_ + 1], sg_, ALU.add, ALU.mult),
            waits=[SG], inc=s_v)
        HTm = s_v.mark()
        P.ps_done(kua, HTm); P.ps_done(kub, HTm)
        prev_sg[f % 2] = HTm
        prev_hT[f % 2] = HTm
    HT = s_v.mark()
    if "hT0" in DEBUG_TAPS:
        t_ = tap("hT0", (128, 1024), BF16)
        P.op("sync", lambda e, t2=t_: e.dma_start(out=t2, in_=hT_tile(0)),
             waits=[HT], inc=s_st, inc_n=16)
    if "hT42" in DEBUG_TAPS:
        t_ = tap("hT42", (128, 1024), BF16)
        P.op("sync", lambda e, t2=t_: e.dma_start(out=t2, in_=hT_tile(42)),
             waits=[HT], inc=s_st, inc_n=16)

    FBLK = [list(range(i, min(i + 4, NFT))) for i in range(0, NFT, 4)]
    xnTf = xnT.bitcast(F32)
    ystage4 = [xnTf[:, k * 512:(k + 1) * 512] for k in range(8)]
    aostage4 = [xnTf[:, 4096 + k * 512: 4096 + (k + 1) * 512] for k in range(8)]
    assert xnTf.shape[1] >= 8192
    wdb4 = [aC[:, k * 2048:(k + 1) * 2048] for k in range(4)]
    wd_free = {}
    ystore = {}
    for oc in range(4):
        if oc == 0:
            ld(bdrow, bd)
        BDL = s_c.mark()
        for tt in range(8):
            key = f"{tt // 2}{'a' if tt % 2 == 0 else 'b'}"
            ps8 = PS[tt // 2][:, (tt % 2) * 512:(tt % 2) * 512 + 512]
            wts = [BDL, HT] + P.ps_take(key)
            P.op("tensor", lambda e, p=ps8, o=oc: e.matmul(
                p, onesr[0:1, 0:128], bdrow[0:1, o * 512:(o + 1) * 512],
                start=True, stop=False), waits=wts)
        for bi, fb in enumerate(FBLK):
            gbi = oc * len(FBLK) + bi
            wdb = wdb4[gbi % 4]
            nf = len(fb)
            wdw = [HT] if gbi < 4 else [wd_free[gbi % 4]]
            P.op("gpsimd", lambda e, o=oc, f0=fb[0], n=nf, wb=wdb: e.dma_start(
                out=wb[:, 0:n * 512].rearrange("p (c f) -> p c f", c=n),
                in_=wd[f0 * 128:(f0 + n) * 128, o * 512:(o + 1) * 512].rearrange(
                    "(c p) f -> p c f", p=128)), waits=wdw, inc=s_w, inc_n=16)
            WDL = s_w.mark()
            for fi, f in enumerate(fb):
                for tt in range(8):
                    ps8 = PS[tt // 2][:, (tt % 2) * 512:(tt % 2) * 512 + 512]
                    last = (f == NFT - 1)
                    blk_last = (fi == nf - 1 and tt == 7)
                    P.op("tensor", lambda e, p=ps8, ff_=f, t=tt, fi_=fi, wb=wdb: e.matmul(
                        p, hT_tile(ff_)[:, t * 128:(t + 1) * 128],
                        wb[:, fi_ * 512: fi_ * 512 + 512],
                        start=False, stop=last),
                        waits=[WDL] if (fi == 0 and tt == 0) else (),
                        inc=s_t if blk_last else None)
            wd_free[gbi % 4] = s_t.mark()
        DN = s_t.mark()
        for tt in range(8):
            key = f"{tt // 2}{'a' if tt % 2 == 0 else 'b'}"
            sl4 = tt
            ps8 = PS[tt // 2][:, (tt % 2) * 512:(tt % 2) * 512 + 512]
            ysl = ystage4[sl4]
            aosl = aostage4[sl4]
            xw_ = [SCOL, AO_ST, HT]
            if sl4 in ystore:
                xw_.append(ystore[sl4])
            P.op("sync", lambda e, t=tt, o=oc, y_=ysl: e.dma_start(
                out=y_, in_=x_own[t * 128:(t + 1) * 128, o * 512:(o + 1) * 512]),
                waits=xw_, inc=s_x, inc_n=16)
            P.op("sync", lambda e, t=tt, o=oc, a_=aosl: e.dma_start(
                out=a_, in_=ao_d[t * 128:(t + 1) * 128, o * 512:(o + 1) * 512]),
                inc=s_x, inc_n=16)
            XT = s_x.mark()
            P.op("vector", lambda e, p=ps8, t=tt, y_=ysl: e.scalar_tensor_tensor(
                y_, p, s_col[:, t:t + 1], y_, ALU.mult, ALU.add),
                waits=[XT, DN, SCOL], inc=s_v)
            P.op("vector", lambda e, y_=ysl, a_=aosl: e.tensor_tensor(
                y_, y_, a_, ALU.add), inc=s_v)
            YV = s_v.mark()
            P.ps_done(key, YV)
            if oc == 0 and tt == 0 and "yraw" in DEBUG_TAPS:
                t_ = tap("yraw", (128, 512))
                P.op("sync", lambda e, t2=t_, y_=ysl: e.dma_start(out=t2, in_=y_),
                     waits=[YV], inc=s_st, inc_n=16)
            P.op("sync", lambda e, t=tt, o=oc, y_=ysl: e.dma_start(
                out=y[t * 128:(t + 1) * 128, o * 512:(o + 1) * 512], in_=y_),
                waits=[YV], inc=s_st, inc_n=16)
            ystore[sl4] = s_st.mark()
    P.op("sync", lambda e: e.nop(), waits=[s_st.mark()])
    P.op("gpsimd", lambda e: e.nop(), waits=[s_st.mark()])

    finals = {s_x: s_x.n, s_x2: s_x2.n, s_c: s_c.n, s_w: s_w.n, s_t: s_t.n,
              s_v: s_v.n, s_s: s_s.n, s_cc: s_cc.n, s_st: s_st.n}
    for eng, lw in P.last_wait.items():
        for sem, v in lw.items():
            assert v <= finals[sem], (eng, v, finals[sem])

    with nc.Block() as block:
        P.emit(block)
    return nc, taps



def run_raw(inputs):
    nc, taps = build_program()
    in_maps = [host_inputs(c, inputs) for c in range(8)]
    res = run_bass_kernel_spmd(nc, in_maps, core_ids=list(range(8)))
    return res


def kernel(**inputs):
    res = run_raw(inputs)
    out = np.zeros((B, S, HID), np.float32)
    for c in range(8):
        b, half = c // 2, c % 2
        out[b, half * TOK:(half + 1) * TOK] = res.results[c]["y"]
    return out


# version 10 (final)
# speedup vs baseline: 1.0254x; 1.0254x over previous
import numpy as np
import ml_dtypes
import concourse.bass as bass
import concourse.mybir as mybir
from concourse.bass_utils import run_bass_kernel_spmd

F32 = mybir.dt.float32
BF16 = mybir.dt.bfloat16
AF = mybir.ActivationFunctionType
ALU = mybir.AluOpType
AX = mybir.AxisListType

B, S, HID = 4, 2048, 2048
NH, HD = 16, 128
R = 16
DFF = 5461
NE = 8
KCAP = 1024.0
TOK = 1024
NT_OWN = 8
NT_ALL = 16
TKT = 1152
CW = 132
NFT = 43
EPS_LN = 1e-5
EPS_RMS = 1e-6
NEG = -1e9
SCALE = float(1.0 / np.sqrt(128.0))
BISECT_ITERS = 44
EBIAS = -30.0

DEBUG_TAPS = ()
TAP_GROUP = 0



def _rope_tables(pos):
    inv_freq = 1.0 / 10000.0 ** (np.arange(32, dtype=np.float64) * 2.0 / 64.0)
    ang = pos.astype(np.float64)[:, None] * inv_freq[None, :]
    return np.cos(ang).astype(np.float32), np.sin(ang).astype(np.float32)


def _pad_cols(a, n):
    out = np.zeros((a.shape[0], n), a.dtype)
    out[:, :a.shape[1]] = a
    return out


def host_inputs(core, inp):
    b, half = core // 2, core % 2
    d = {}
    x = np.asarray(inp["L0_attn_hidden_states"], np.float32)
    d["x_own"] = np.ascontiguousarray(x[b, half * TOK:(half + 1) * TOK])
    d["x_oth"] = np.ascontiguousarray(x[b, (1 - half) * TOK:(2 - half) * TOK])

    own_start = half * TOK
    pos_q = own_start + np.arange(TOK)
    cq, sq = _rope_tables(pos_q)
    d["cos_q"], d["sin_q"] = cq, sq
    pos_k = np.concatenate([own_start - 128 + np.arange(128), pos_q])
    ck_, sk_ = _rope_tables(pos_k)
    d["cos_k"], d["sin_k"] = ck_, sk_
    be = np.concatenate([own_start + np.arange(64) * R + (R - 1),
                         (1 - half) * TOK + np.arange(64) * R + (R - 1)])
    cb, sbt_ = _rope_tables(be)
    d["cosT_b"] = np.ascontiguousarray(cb.T)
    d["sinT_b"] = np.ascontiguousarray(sbt_.T)

    cm = np.zeros((TOK, CW), np.float32)
    cm[:, :128] = (be[None, :] <= pos_q[:, None]).astype(np.float32)
    cm[:, 128] = 1.0
    d["cmaskM"] = np.ascontiguousarray(
        cm.reshape(8, 128, CW).transpose(1, 0, 2).reshape(128, 8 * CW)
    ).astype(ml_dtypes.bfloat16)

    i = np.arange(128)[:, None]
    j = np.arange(256)[None, :]
    rest = (((j < 128) & (j >= i)) | ((j >= 128) & (j - 128 <= i))).astype(np.float32)
    first = rest.copy()
    if half == 0:
        first[:, :128] = 0.0
    d["mrestM2"] = np.concatenate([rest, rest], 1).astype(ml_dtypes.bfloat16)
    d["mfirstM2"] = np.concatenate([first, rest], 1).astype(ml_dtypes.bfloat16)

    import jax
    _old_part = jax.config.jax_threefry_partitionable
    jax.config.update("jax_threefry_partitionable", True)
    with jax.default_device(jax.devices("cpu")[0]):
        u = np.asarray(jax.random.uniform(
            jax.random.key(42, impl="threefry2x32"), (B * S, NE),
            dtype=np.float32))
    jax.config.update("jax_threefry_partitionable", _old_part)
    gum = (-np.log(-np.log(u + 1e-10) + 1e-10)).astype(np.float32)
    d["gumbelT"] = np.ascontiguousarray(
        gum[b * S + own_start: b * S + own_start + TOK].T)

    for k, srck in [("Wq", "L0_attn_Wq"), ("Wlk", "L0_attn_Wlk"),
                    ("Wlv", "L0_attn_Wlv"), ("Wo", "L0_attn_Wo"),
                    ("Wk", "L0_attn_Wk"), ("Wv", "L0_attn_Wv"),
                    ("rw", "L0_ffn_router_w")]:
        d[k] = np.asarray(inp[srck], np.float32)
    d["Wc"] = np.ascontiguousarray(
        np.asarray(inp["L0_attn_Wc"], np.float32).reshape(16, 128).T)
    for k, srck in [("bq", "L0_attn_bq"), ("blk", "L0_attn_blk"),
                    ("blv", "L0_attn_blv"), ("bo", "L0_attn_bo"),
                    ("bk", "L0_attn_bk"), ("bv", "L0_attn_bv"),
                    ("bc", "L0_attn_bc"), ("bd", "L0_ffn_b_down")]:
        d[k] = np.asarray(inp[srck], np.float32)[None, :]
    d["qn_row"] = np.asarray(inp["L0_attn_qn_w"], np.float32)[None, :]
    d["kn_row"] = np.asarray(inp["L0_attn_kn_w"], np.float32)[None, :]
    d["kn_col"] = np.asarray(inp["L0_attn_kn_w"], np.float32)[:, None]
    d["sinkTk"] = _pad_cols(np.ascontiguousarray(
        np.asarray(inp["L0_attn_sink_k"], np.float32).T), 20)
    d["sinkv_row"] = np.ascontiguousarray(
        np.asarray(inp["L0_attn_sink_v"], np.float32).reshape(1, NH * HD))
    d["rb_col"] = np.asarray(inp["L0_ffn_router_b"], np.float32)[:, None]
    d["wg"] = _pad_cols(np.asarray(inp["L0_ffn_w_gate"], np.float32), NFT * 128)
    d["wu"] = _pad_cols(np.asarray(inp["L0_ffn_w_up"], np.float32), NFT * 128)
    wdp = np.zeros((NFT * 128, HID), np.float32)
    wdp[:DFF] = np.asarray(inp["L0_ffn_w_down"], np.float32)
    d["wd"] = wdp
    d["bg_col"] = np.ascontiguousarray(_pad_cols(
        np.asarray(inp["L0_ffn_b_gate"], np.float32)[None, :], NFT * 128)
        .reshape(NFT, 128).T)
    d["bu_col"] = np.ascontiguousarray(_pad_cols(
        np.asarray(inp["L0_ffn_b_up"], np.float32)[None, :], NFT * 128)
        .reshape(NFT, 128).T)
    return d



class Sem:
    def __init__(self, nc, name):
        self.h = nc.alloc_semaphore(name)
        self.n = 0

    def mark(self):
        return (self, self.n)


class Prog:
    ENGINES = ("tensor", "vector", "scalar", "gpsimd", "sync")

    def __init__(self, nc):
        self.nc = nc
        self.thunks = {e: [] for e in self.ENGINES}
        self.last_wait = {e: {} for e in self.ENGINES}
        self.ps_reader = {}

    def op(self, eng, f, waits=(), inc=None, inc_n=1):
        wl = []
        lw = self.last_wait[eng]
        for w in waits:
            if w is None:
                continue
            sem, v = w
            if v > 0 and lw.get(sem, -1) < v:
                wl.append((sem.h, v))
                lw[sem] = v
        if inc is not None:
            inc.n += inc_n
        ih = inc.h if inc is not None else None

        def thunk(e, wl=wl, f=f, ih=ih, inc_n=inc_n):
            for h, v in wl:
                e.wait_ge(h, v)
            inst = f(e)
            if ih is not None:
                inst.then_inc(ih, inc_n)
        self.thunks[eng].append(thunk)

    def ps_take(self, *keys):
        return [m for k in keys if (m := self.ps_reader.get(k)) is not None]

    def ps_done(self, key, mark):
        self.ps_reader[key] = mark

    def emit(self, block):
        for eng in self.ENGINES:
            ths = self.thunks[eng]
            getattr(block, eng)(lambda e, ths=ths: [t(e) for t in ths])



def build_program():
    nc = bass.Bass("TRN2", target_bir_lowering=False, debug=False)
    P = Prog(nc)

    def inp(name, shape, dtype=F32):
        return nc.dram_tensor(name, list(shape), dtype, kind="ExternalInput").ap()

    x_own = inp("x_own", (TOK, HID))
    x_oth = inp("x_oth", (TOK, HID))
    cos_q = inp("cos_q", (TOK, 32)); sin_q = inp("sin_q", (TOK, 32))
    cos_k = inp("cos_k", (TKT, 32)); sin_k = inp("sin_k", (TKT, 32))
    cosT_b = inp("cosT_b", (32, 128)); sinT_b = inp("sinT_b", (32, 128))
    cmaskM = inp("cmaskM", (128, 8 * CW), BF16)
    mfirstM2 = inp("mfirstM2", (128, 512), BF16)
    mrestM2 = inp("mrestM2", (128, 512), BF16)
    gumbelT = inp("gumbelT", (NE, TOK))
    Wq = inp("Wq", (HID, HID)); Wlk = inp("Wlk", (HID, HID)); Wlv = inp("Wlv", (HID, HID))
    Wo = inp("Wo", (HID, HID))
    Wk = inp("Wk", (HID, HD)); Wv = inp("Wv", (HID, HD)); Wc = inp("Wc", (128, 16))
    bq = inp("bq", (1, HID)); blk = inp("blk", (1, HID)); blv = inp("blv", (1, HID))
    bo = inp("bo", (1, HID)); bk = inp("bk", (1, HD)); bv = inp("bv", (1, HD))
    bc = inp("bc", (1, 1)); bd = inp("bd", (1, HID))
    qn_row = inp("qn_row", (1, HD)); kn_row = inp("kn_row", (1, HD))
    kn_col = inp("kn_col", (HD, 1))
    sinkTk = inp("sinkTk", (HD, 20)); sinkv_row = inp("sinkv_row", (1, NH * HD))
    rw = inp("rw", (HID, NE)); rb_col = inp("rb_col", (NE, 1))
    wg = inp("wg", (HID, NFT * 128)); wu = inp("wu", (HID, NFT * 128))
    wd = inp("wd", (NFT * 128, HID))
    bg_col = inp("bg_col", (128, NFT)); bu_col = inp("bu_col", (128, NFT))

    y = nc.dram_tensor("y", [TOK, HID], F32, kind="ExternalOutput").ap()

    id_bf = nc.inline_tensor(np.eye(128, dtype=ml_dtypes.bfloat16), "id_bf").ap()
    ones_row_c = nc.inline_tensor(np.ones((1, 512), np.float32), "ones_row").ap()
    ones_col_c = nc.inline_tensor(np.ones((128, 1), np.float32), "ones_col").ap()
    m8np = np.zeros((128, 8), np.float32)
    m8np[np.arange(128), np.arange(128) // 16] = 1.0
    M8c = nc.inline_tensor(m8np, "M8c").ap()
    seg16T_c = nc.inline_tensor(np.ascontiguousarray(m8np.T), "seg16Tc").ap()
    ones8_c = nc.inline_tensor(np.ones((8, 1), np.float32), "ones8c").ap()
    eps_c = nc.inline_tensor(
        np.broadcast_to(np.array([EPS_LN, EPS_RMS, EBIAS], np.float32),
                        (128, 3)).copy(), "eps_c").ap()

    smt_d = nc.dram_tensor("smt_d", [NE, TOK], F32, kind="Internal").ap()
    gath_d = nc.dram_tensor("gath_d", [NE * 8, TOK], F32, kind="Internal").ap()
    ao_d = nc.dram_tensor("ao_d", [TOK, HID], F32, kind="Internal").ap()

    taps = {}

    def tap(name, shape, dtype=F32):
        taps[name] = nc.dram_tensor("tap_" + name, list(shape), dtype,
                                    kind="ExternalOutput").ap()
        return taps[name]

    sbt = nc.alloc_sbuf_tensor
    xnT = sbt("xnT", [128, 16 * TKT], BF16).ap()
    arenaA = sbt("arenaA", [128, 16384], F32).ap()
    arenaB = sbt("arenaB", [128, 8192], F32).ap()
    arenaC = sbt("arenaC", [128, 4096], F32).ap()
    aA = arenaA.bitcast(BF16)
    aB = arenaB.bitcast(BF16)
    aC = arenaC.bitcast(BF16)

    xn = aA
    O_W0 = 0
    O_QTAB = 4096
    O_KTAB = O_QTAB + 1024
    O_QNW = O_KTAB + 1152
    O_KNW = O_QNW + 128
    O_QN2 = O_KNW + 128
    O_SQB = O_QN2 + 512
    O_SSQ = O_SQB + 512
    O_RTMP = O_SSQ + 32
    O_Q4R = O_RTMP + 256
    O_QT4 = O_Q4R + 256
    O_LKT4 = O_QT4 + 2048
    O_LV4 = O_LKT4 + 2304
    O_PL2 = O_LV4 + 2304
    O_PC2 = O_PL2 + 512
    O_PTL = O_PC2 + 272
    O_PTC = O_PTL + 256
    O_PTS = O_PTC + 128
    O_RR = O_PTS + 128
    O_TRASH = O_PL2
    O_WOST = O_QN2
    assert O_RR + 32 <= 16384

    wbuf_bf = aA[:, 2 * O_W0: 2 * O_W0 + 8192]
    qtab = arenaA[:, O_QTAB:O_QTAB + 1024]
    ktab = arenaA[:, O_KTAB:O_KTAB + 1152]
    qnwbc = arenaA[:, O_QNW:O_QNW + 128]
    knwbc = arenaA[:, O_KNW:O_KNW + 128]
    qn2 = arenaA[:, O_QN2:O_QN2 + 512]
    sqb = arenaA[:, O_SQB:O_SQB + 512]
    ssq = arenaA[:, O_SSQ:O_SSQ + 32]
    rtmp = arenaA[:, O_RTMP:O_RTMP + 256]
    q4r = aA[:, 2 * O_Q4R: 2 * O_Q4R + 512]
    qT4 = aA[:, 2 * O_QT4: 2 * O_QT4 + 4096]
    lkT4 = aA[:, 2 * O_LKT4: 2 * O_LKT4 + 4608]
    lv4 = aA[:, 2 * O_LV4: 2 * O_LV4 + 4608]
    pl2 = [aA[:, 2 * O_PL2: 2 * O_PL2 + 512],
           aA[:, 2 * O_PL2 + 512: 2 * O_PL2 + 1024]]
    pc2 = [aA[:, 2 * O_PC2: 2 * O_PC2 + 264],
           aA[:, 2 * O_PC2 + 272: 2 * O_PC2 + 536]]
    pTl = aA[:, 2 * O_PTL: 2 * O_PTL + 512]
    pTc = aA[:, 2 * O_PTC: 2 * O_PTC + 256]
    pTs = aA[:, 2 * O_PTS: 2 * O_PTS + 256]
    rrp = arenaA[:, O_RR:O_RR + 32]
    trashA = arenaA[:, O_TRASH:O_TRASH + 512]
    swb = arenaA[:, O_PL2:O_PL2 + 512]
    rtmp1 = arenaA[:, O_PC2:O_PC2 + 512]
    negq = arenaA[:, O_RR + 32:O_RR + 64]
    negk = arenaA[:, O_RR + 64:O_RR + 96]
    wo_stage = arenaA[:, O_WOST:O_WOST + 1024]

    xnTt = aB[:, 0:2048]
    zr = arenaB[:, 1024:2048]
    uexp = arenaB[:, 2048:3072]
    gumT = arenaB[:, 3072:4096]
    logc = arenaB[:, 4096:6144]
    trashB = arenaB[:, 6144:8192]
    entT = aB[:, 2 * 6144: 2 * 6144 + 2048]
    wblkb = aB[:, 2 * 7296: 2 * 7296 + 128]
    ctxT = aB[:, 0:16384]
    O_SG = 5632
    O_YST = 6656
    sg_sb = aB[:, 2 * O_SG: 2 * O_SG + 2048]
    ystage = arenaB[:, O_YST:O_YST + 1024]
    aostage = arenaB[:, O_SG:O_SG + 1024]

    def hT_tile(f):
        if f < 32:
            return aA[:, f * 1024:(f + 1) * 1024]
        return aB[:, (f - 32) * 1024:(f - 32 + 1) * 1024]

    xst = [arenaC[:, 0:2048], arenaC[:, 2048:4096]]
    wo_sb = aC[:, 0:8192]
    wff = [aC[:, 0:2048], aC[:, 2048:4096], aC[:, 4096:6144], aC[:, 6144:8192]]
    wdb_sl = [aC[:, 0:4096], aC[:, 4096:8192]]

    lnA_t = sbt("lnA", [128, 16], F32).ap()
    eps_sb = sbt("eps_sb", [128, 3], F32).ap()
    nc.const_aps.register(F32, EPS_LN, eps_sb[:, 0:1])
    nc.const_aps.register(F32, EPS_RMS, eps_sb[:, 1:2])
    nc.const_aps.register(F32, EBIAS, eps_sb[:, 2:3])
    cmaskM_sb = sbt("cmaskM_sb", [128, 8 * CW], BF16).ap()
    mfirstM_sb = sbt("mfirstM_sb", [128, 512], BF16).ap()
    mrestM_sb = sbt("mrestM_sb", [128, 512], BF16).ap()
    cosq_sb = sbt("cosq_sb", [128, 8 * 32], F32).ap()
    sinq_sb = sbt("sinq_sb", [128, 8 * 32], F32).ap()
    cosk_sb = sbt("cosk_sb", [128, 9 * 32], F32).ap()
    sink_sb = sbt("sink_sb", [128, 9 * 32], F32).ap()
    cosb_sb = sbt("cosb_sb", [32, 128], F32).ap()
    sinb_sb = sbt("sinb_sb", [32, 128], F32).ap()
    id_sb = sbt("id_sb", [128, 128], BF16).ap()
    onesr = sbt("onesr", [1, 512], F32).ap()
    onesc = sbt("onesc", [128, 1], F32).ap()
    m8_sb = sbt("m8_sb", [128, 8], F32).ap()
    seg16T_sb = sbt("seg16T_sb", [8, 128], F32).ap()
    ones8 = sbt("ones8", [8, 1], F32).ap()
    qnr_sb = sbt("qnr_sb", [1, HD], F32).ap()
    knr_sb = sbt("knr_sb", [1, HD], F32).ap()
    knc_sb = sbt("knc_sb", [HD, 1], F32).ap()
    rbc_sb = sbt("rbc_sb", [NE, 1], F32).ap()
    sinkTk_sb = sbt("sinkTk_sb", [HD, 20], BF16).ap()
    sinkv_sb = sbt("sinkv_sb", [1, NH * HD], BF16).ap()
    bgc_sb = sbt("bgc_sb", [128, NFT], F32).ap()
    buc_sb = sbt("buc_sb", [128, NFT], F32).ap()
    brow = sbt("brow", [1, 1536], F32).ap()
    borow = sbt("borow", [1, 2048], F32).ap()
    bdrow = sbt("bdrow", [1, 2048], F32).ap()
    smt_sb = sbt("smt_sb", [NE, TOK], F32).ap()
    bis_sb = sbt("bis_sb", [128, 512], F32).ap()
    bst = sbt("bst", [8, 16], F32).ap()
    mid128 = sbt("mid128", [128, 2], F32).ap()
    cnt128 = sbt("cnt128", [128, 1], F32).ap()
    sT_row = sbt("sT_row", [NE, TOK], F32).ap()
    s_col = sbt("s_col", [128, 8], F32).ap()
    ckT_sb = sbt("ckT_sb", [128, 128], BF16).ap()
    cv_sb = sbt("cv_sb", [128, 128], BF16).ap()
    ckn_sb = sbt("ckn_sb", [128, 128], F32).ap()
    tmpR = sbt("tmpR", [32, 512], F32).ap()
    tmpo = sbt("tmpo", [32, 128], BF16).ap()
    expw_sb = sbt("expw_sb", [128, 16], F32).ap()
    sumw_sb = sbt("sumw_sb", [1, 128], F32).ap()
    rsumw_sb = sbt("rsumw_sb", [1, 128], F32).ap()
    rsb_sb = sbt("rsb_sb", [128, 128], F32).ap()
    invb_sb = sbt("invb_sb", [128, 128], F32).ap()
    ckr_sb = sbt("ckr_sb", [1, 128], F32).ap()
    wc_sb = sbt("wc_sb", [128, 16], BF16).ap()
    rw_sb = sbt("rw_sb", [128, 16 * NE], BF16).ap()
    qtab2 = arenaB[:, 6144:7168]
    ktab2 = arenaB[:, 7168:8192]
    ktab2t8 = arenaA[:, 16144:16272]

    def kt2tile(t):
        return ktab2[:, t * 128:(t + 1) * 128] if t < 8 else ktab2t8
    wk_sb = aC[:, 0:2048]
    wv_sb = aC[:, 2048:4096]

    s_x = Sem(nc, "s_x")
    s_x2 = Sem(nc, "s_x2")
    s_c = Sem(nc, "s_c")
    s_w = Sem(nc, "s_w")
    s_t = Sem(nc, "s_t")
    s_v = Sem(nc, "s_v")
    s_s = Sem(nc, "s_s")
    s_cc = Sem(nc, "s_cc")
    s_st = Sem(nc, "s_st")

    PS = [nc.alloc_psum_tensor(f"ps{i}", [128, 1024], F32).ap() for i in range(4)]
    PSbf = [p.bitcast(BF16) for p in PS]

    def ld(dst, src, waits=()):
        P.op("sync", lambda e, d=dst, s=src: e.dma_start(out=d, in_=s),
             waits=waits, inc=s_c, inc_n=16)

    def wld(dst, src, waits=()):
        P.op("gpsimd", lambda e, d=dst, s=src: e.dma_start(out=d, in_=s),
             waits=waits, inc=s_w, inc_n=16)

    def ld3(dst2d, src, n, q="sync", waits=()):
        s3 = src.rearrange("(c p) f -> p c f", p=128)
        d3 = dst2d.rearrange("p (c f) -> p c f", c=n)
        if q == "sync":
            P.op("sync", lambda e, d=d3, s_=s3: e.dma_start(out=d, in_=s_),
                 waits=waits, inc=s_c, inc_n=16)
        else:
            P.op("gpsimd", lambda e, d=d3, s_=s3: e.dma_start(out=d, in_=s_),
                 waits=waits, inc=s_w, inc_n=16)

    ld(id_sb, id_bf); ld(onesr, ones_row_c); ld(onesc, ones_col_c)
    ld(m8_sb, M8c); ld(seg16T_sb, seg16T_c); ld(ones8, ones8_c)
    ld(mfirstM_sb, mfirstM2); ld(mrestM_sb, mrestM2)
    ld(cmaskM_sb, cmaskM)
    ld3(cosq_sb, cos_q, 8)
    ld3(sinq_sb, sin_q, 8)
    ld3(cosk_sb, cos_k, 9)
    ld3(sink_sb, sin_k, 9)
    ld(cosb_sb, cosT_b); ld(sinb_sb, sinT_b)
    ld(qnr_sb, qn_row); ld(knr_sb, kn_row); ld(knc_sb, kn_col)
    ld(rbc_sb, rb_col); ld(bgc_sb, bg_col); ld(buc_sb, bu_col)
    ld(eps_sb, eps_c)
    ld(gumT[0:8, :], gumbelT)
    ld(brow[0:1, 1024:1152], bk)
    ld(brow[0:1, 1152:1280], bv)
    ld(brow[0:1, 1280:1281], bc)
    CONSTS = s_c.mark()


    wld(sinkTk_sb, sinkTk)
    wld(sinkv_sb, sinkv_row)
    wld(wc_sb, Wc)
    ld3(rw_sb, rw, 16, q="g")
    W_PRE = s_w.mark()

    tile_done = []
    xnT_marks = []
    for tt in range(NT_ALL):
        buf = xst[tt % 2]
        src = x_own if tt < NT_OWN else x_oth
        r0 = (tt % NT_OWN) * 128
        w = [] if tt < 2 else [tile_done[tt - 2]]
        sx_p = s_x if tt % 2 == 0 else s_x2
        P.op("sync", lambda e, bu_=buf, s=src, r=r0: e.dma_start(
            out=bu_, in_=s[r:r + 128, :]), waits=w, inc=sx_p, inc_n=16)
        xw = sx_p.mark()
        lnA = lnA_t[:, 0:8]
        P.op("vector", lambda e, bu_=buf, L=lnA: e.reduce_sum(L[:, 0:1], bu_, axis=AX.X),
             waits=[xw], inc=s_v)
        VS = s_v.mark()
        P.op("scalar", lambda e, bu_=buf: e.activation(
            trashB, bu_, AF.Square, accum_out=lnA[:, 1:2]), waits=[xw], inc=s_s)
        SQm = s_s.mark()
        P.op("vector", lambda e: e.tensor_scalar(
            lnA[:, 2:3], lnA[:, 0:1], 1.0 / HID, None, ALU.mult), waits=[VS], inc=s_v)
        P.op("vector", lambda e: e.tensor_scalar(
            lnA[:, 3:4], lnA[:, 1:2], 1.0 / HID, None, ALU.mult), waits=[SQm], inc=s_v)
        vB = s_v.mark()
        P.op("vector", lambda e: e.scalar_tensor_tensor(
            lnA[:, 4:5], lnA[:, 2:3], lnA[:, 2:3], lnA[:, 3:4], ALU.mult, ALU.subtract),
            waits=[vB], inc=s_v)
        vC = s_v.mark()
        P.op("scalar", lambda e: e.activation(
            lnA[:, 5:6], lnA[:, 4:5], AF.Ln, bias=EPS_LN, scale=-1.0),
            waits=[vC, CONSTS], inc=s_s)
        sA = s_s.mark()
        P.op("scalar", lambda e: e.activation(
            lnA[:, 6:7], lnA[:, 5:6], AF.Exp, scale=-0.5), waits=[sA], inc=s_s)
        sB = s_s.mark()
        P.op("vector", lambda e: e.tensor_scalar(
            lnA[:, 7:8], lnA[:, 2:3], lnA[:, 6:7], -1.0, ALU.mult, ALU.mult),
            waits=[sB], inc=s_v)
        vD = s_v.mark()
        P.op("scalar", lambda e, bu_=buf, t=tt: e.activation(
            xn[:, t * 2048:(t + 1) * 2048], bu_, AF.Identity,
            bias=lnA[:, 7:8], scale=lnA[:, 6:7]), waits=[vD], inc=s_s)
        sAPP = s_s.mark()
        tile_done.append(sAPP)

        for grp in range(4):
            hf = grp % 2
            key = "3a" if hf == 0 else "3b"
            tw = [sAPP, CONSTS] + P.ps_take(key)
            for k in range(4):
                hc = grp * 4 + k
                P.op("tensor", lambda e, t=tt, h=hc, k_=k, hf_=hf: e.transpose(
                    PSbf[3][:, hf_ * 1024 + k_ * 128: hf_ * 1024 + (k_ + 1) * 128],
                    xn[:, t * 2048 + h * 128: t * 2048 + (h + 1) * 128], id_sb),
                    waits=tw if k == 0 else (), inc=s_t if k == 3 else None)
            tE = s_t.mark()
            if tt < NT_OWN:
                dst = xnT.rearrange("p (h t) -> p h t", h=16)[
                    :, grp * 4:(grp + 1) * 4, tt * 128:(tt + 1) * 128]
            else:
                dst = xnTt.rearrange("p (h t) -> p h t", h=16)[
                    :, grp * 4:(grp + 1) * 4, 0:128]
            P.op("vector", lambda e, d=dst, hf_=hf: e.tensor_copy(
                d, PSbf[3][:, hf_ * 1024:hf_ * 1024 + 512].rearrange(
                    "p (h t) -> p h t", h=4)), waits=[tE], inc=s_v)
            P.ps_done(key, s_v.mark())
            if tt == NT_ALL - 1:
                P.op("vector", lambda e, g=grp, hf_=hf: e.tensor_copy(
                    xnT.rearrange("p (h t) -> p h t", h=16)[
                        :, g * 4:(g + 1) * 4, 1024:1152],
                    PSbf[3][:, hf_ * 1024:hf_ * 1024 + 512].rearrange(
                        "p (h t) -> p h t", h=4)), inc=s_v)
                P.ps_done(key, s_v.mark())
        TILE_T = s_v.mark()
        xnT_marks.append(TILE_T)

        if tt < NT_OWN:
            def src_xnT(h, t=tt):
                return xnT[:, h * TKT + t * 128: h * TKT + (t + 1) * 128]
        else:
            def src_xnT(h):
                return xnTt[:, h * 128:(h + 1) * 128]
        lw = [TILE_T, W_PRE, CONSTS] + P.ps_take("2a")
        for hc in range(16):
            P.op("tensor", lambda e, h=hc, fx=src_xnT: e.matmul(
                PS[2][0:1, 0:128], wc_sb[:, h:h + 1], fx(h),
                start=(h == 0), stop=(h == 15)),
                waits=lw if hc == 0 else (), inc=s_t if hc == 15 else None)
        LCM = s_t.mark()
        P.op("scalar", lambda e, t=tt: e.activation(
            logc[0:1, t * 128:(t + 1) * 128], PS[2][0:1, 0:128], AF.Exp,
            bias=brow[0:1, 1280:1281]), waits=[LCM], inc=s_s)
        P.ps_done("2a", s_s.mark())
    XNT_OWN = xnT_marks[NT_OWN - 1]
    XNT_ALL = xnT_marks[NT_ALL - 1]
    LOGC = s_s.mark()
    XN_LAST = tile_done[-1]

    if "xnT" in DEBUG_TAPS:
        t_ = tap("xnT", (128, 16 * TKT), BF16)
        P.op("sync", lambda e, t2=t_: e.dma_start(out=t2, in_=xnT),
             waits=[XNT_ALL, LOGC], inc=s_st, inc_n=16)
    if "xn8" in DEBUG_TAPS:
        t_ = tap("xn8", (128, 16384), BF16)
        P.op("sync", lambda e, t2=t_: e.dma_start(out=t2, in_=aA[:, 0:16384]),
             waits=[XNT_ALL, LOGC], inc=s_st, inc_n=16)
        _m = s_st.mark()
        P.op("gpsimd", lambda e: e.nop(), waits=[_m])
    if "logc" in DEBUG_TAPS:
        t_ = tap("logc", (1, 2048))
        P.op("sync", lambda e, t2=t_: e.dma_start(out=t2, in_=logc[0:1, :]),
             waits=[LOGC], inc=s_st, inc_n=16)

    mmw = [XNT_OWN, W_PRE] + P.ps_take("0a", "0b")
    for n2 in range(2):
        for hc in range(16):
            P.op("tensor", lambda e, h=hc, n=n2: e.matmul(
                PS[0][0:8, n * 512:(n + 1) * 512], rw_sb[:, h * 8:(h + 1) * 8],
                xnT[:, h * TKT + n * 512: h * TKT + (n + 1) * 512],
                start=(h == 0), stop=(h == 15)),
                waits=mmw if (n2 == 0 and hc == 0) else (),
                inc=s_t if (n2 == 1 and hc == 15) else None)
    RT = s_t.mark()
    P.op("vector", lambda e: e.scalar_tensor_tensor(
        uexp[0:8, :], PS[0][0:8, :], rbc_sb, gumT[0:8, :], ALU.add, ALU.add),
        waits=[RT, CONSTS], inc=s_v)
    P.ps_done("0a", s_v.mark()); P.ps_done("0b", s_v.mark())
    RV = s_v.mark()
    P.op("scalar", lambda e: e.activation(uexp[0:8, :], uexp[0:8, :], AF.Exp),
         waits=[RV], inc=s_s)
    RE = s_s.mark()
    zw = [RE] + P.ps_take("1a", "1b")
    for n2 in range(2):
        P.op("tensor", lambda e, n=n2: e.matmul(
            PS[1][0:1, n * 512:(n + 1) * 512], ones8,
            uexp[0:8, n * 512:(n + 1) * 512], start=True, stop=True),
            waits=zw if n2 == 0 else (), inc=s_t if n2 == 1 else None)
    RZ = s_t.mark()
    P.op("vector", lambda e: e.reciprocal(zr[0:1, :], PS[1][0:1, :]),
         waits=[RZ], inc=s_v)
    P.ps_done("1a", s_v.mark()); P.ps_done("1b", s_v.mark())
    RR_ = s_v.mark()
    bw = [RR_] + P.ps_take("2a", "2b")
    for n2 in range(2):
        P.op("tensor", lambda e, n=n2: e.matmul(
            PS[2][0:8, n * 512:(n + 1) * 512], onesr[0:1, 0:8],
            zr[0:1, n * 512:(n + 1) * 512], start=True, stop=True),
            waits=bw if n2 == 0 else (), inc=s_t if n2 == 1 else None)
    RB2 = s_t.mark()
    P.op("vector", lambda e: e.tensor_tensor(
        smt_sb[0:8, :], uexp[0:8, :], PS[2][0:8, :], ALU.mult), waits=[RB2], inc=s_v)
    P.ps_done("2a", s_v.mark()); P.ps_done("2b", s_v.mark())
    SMT = s_v.mark()
    if "smt" in DEBUG_TAPS:
        t_ = tap("smt", (NE, TOK))
        P.op("sync", lambda e, t2=t_: e.dma_start(out=t2, in_=smt_sb[0:8, :]),
             waits=[SMT], inc=s_st, inc_n=16)

    ld3(wk_sb, Wk, 16, q="g", waits=[XN_LAST])
    ld3(wv_sb, Wv, 16, q="g", waits=[XN_LAST])
    WKV = s_w.mark()
    P.op("vector", lambda e: e.reduce_sum(
        sumw_sb.rearrange("p (a b) -> p a b", a=128),
        logc[0:1, :].rearrange("p (a b) -> p a b", a=128), axis=AX.X),
        waits=[LOGC], inc=s_v)
    SW = s_v.mark()
    P.op("vector", lambda e: e.reciprocal(rsumw_sb, sumw_sb), waits=[SW], inc=s_v)
    RSW = s_v.mark()
    P.op("tensor", lambda e: e.matmul(PS[2][:, 0:128], onesr[0:1, 0:128],
                                      rsumw_sb, start=True, stop=True),
         waits=[RSW] + P.ps_take("2a"), inc=s_t)
    P.op("vector", lambda e: e.tensor_copy(rsb_sb, PS[2][:, 0:128]),
         waits=[s_t.mark()], inc=s_v)
    P.ps_done("2a", s_v.mark())
    RSB = s_v.mark()
    exw = [LOGC] + P.ps_take("2b")
    for ttc in range(16):
        P.op("tensor", lambda e, t=ttc: e.matmul(
            PS[2][:, 512 + t:513 + t], logc[0:1, t * 128:(t + 1) * 128],
            onesr[0:1, 0:1], start=True, stop=True),
            waits=exw if ttc == 0 else (), inc=s_t if ttc == 15 else None)
    EXC = s_t.mark()
    P.op("vector", lambda e: e.tensor_copy(expw_sb, PS[2][:, 512:528]),
         waits=[EXC], inc=s_v)
    P.ps_done("2b", s_v.mark())
    EXS = s_v.mark()
    for ttc in range(16):
        P.op("vector", lambda e, t=ttc: e.tensor_scalar(
            wblkb[:, t * 8:(t + 1) * 8], m8_sb, expw_sb[:, t:t + 1], None, ALU.mult),
            waits=[EXS, CONSTS] if ttc == 0 else (), inc=s_v if ttc == 15 else None)
    WBLK = s_v.mark()
    enw = [WBLK, XN_LAST] + P.ps_take("0a", "0b", "1a", "1b")
    for ht in range(16):
        ps = PS[0] if ht < 8 else PS[1]
        col = (ht % 8) * 128
        for ttc in range(16):
            P.op("tensor", lambda e, t=ttc, h=ht, p=ps, c=col: e.matmul(
                p[:, c + t * 8: c + (t + 1) * 8],
                xn[:, t * 2048 + h * 128: t * 2048 + (h + 1) * 128],
                wblkb[:, t * 8:(t + 1) * 8], start=True, stop=True),
                waits=enw if (ht == 0 and ttc == 0) else (),
                inc=s_t if (ht == 15 and ttc == 15) else None)
    ENTM = s_t.mark()
    for ht in range(16):
        ps = PS[0] if ht < 8 else PS[1]
        col = (ht % 8) * 128
        P.op("vector", lambda e, h=ht, p=ps, c=col: e.tensor_tensor(
            entT[:, h * 128:(h + 1) * 128], p[:, c:c + 128], rsb_sb, ALU.mult),
            waits=[ENTM, RSB] if ht == 0 else (), inc=s_v)
    ENT = s_v.mark()
    for k in ("0a", "0b", "1a", "1b"):
        P.ps_done(k, ENT)
    if "entT" in DEBUG_TAPS:
        t_ = tap("entT", (128, 2048), BF16)
        P.op("sync", lambda e, t2=t_: e.dma_start(out=t2, in_=entT), waits=[ENT],
             inc=s_st, inc_n=16)
    ckw = [ENT, CONSTS, WKV] + P.ps_take("2a")
    P.op("tensor", lambda e: e.matmul(PS[2][:, 0:128], brow[0:1, 1024:1152],
                                      onesr[0:1, 0:128], start=True, stop=False),
         waits=ckw)
    for hc in range(16):
        P.op("tensor", lambda e, h=hc: e.matmul(
            PS[2][:, 0:128], wk_sb[:, h * 128:(h + 1) * 128],
            entT[:, h * 128:(h + 1) * 128], start=False, stop=(h == 15)),
            inc=s_t if hc == 15 else None)
    CKM = s_t.mark()
    P.op("scalar", lambda e: e.activation(ckn_sb, PS[2][:, 0:128], AF.Square),
         waits=[CKM], inc=s_s)
    CKSQ = s_s.mark()
    P.op("tensor", lambda e: e.matmul(PS[3][0:1, 0:128], onesc, ckn_sb,
                                      start=True, stop=True),
         waits=[CKSQ] + P.ps_take("3a"), inc=s_t)
    CKSS = s_t.mark()
    P.op("scalar", lambda e: e.activation(ckr_sb, PS[3][0:1, 0:128], AF.Ln,
                                          bias=EPS_RMS, scale=1.0 / 128),
         waits=[CKSS], inc=s_s)
    CKLN = s_s.mark()
    P.ps_done("3a", CKLN)
    P.op("scalar", lambda e: e.activation(ckr_sb, ckr_sb, AF.Exp, scale=-0.5),
         waits=[CKLN], inc=s_s)
    CKIV = s_s.mark()
    P.op("tensor", lambda e: e.matmul(PS[3][:, 128:256], onesr[0:1, 0:128],
                                      ckr_sb, start=True, stop=True),
         waits=[CKIV] + P.ps_take("3a"), inc=s_t)
    P.op("vector", lambda e: e.tensor_copy(invb_sb, PS[3][:, 128:256]),
         waits=[s_t.mark()], inc=s_v)
    P.ps_done("3a", s_v.mark())
    CKBC = s_v.mark()
    P.op("vector", lambda e: e.scalar_tensor_tensor(
        ckn_sb, PS[2][:, 0:128], knc_sb, invb_sb, ALU.mult, ALU.mult),
        waits=[CKBC], inc=s_v)
    P.ps_done("2a", s_v.mark())
    CKN = s_v.mark()
    P.op("vector", lambda e: e.tensor_copy(tmpR[:, 0:128], ckn_sb[32:64, :]),
         waits=[CKN], inc=s_v)
    C2 = s_v.mark()
    P.op("vector", lambda e: e.tensor_tensor(tmpR[:, 128:256], ckn_sb[0:32, :],
                                             cosb_sb, ALU.mult), waits=[CKN], inc=s_v)
    P.op("vector", lambda e: e.tensor_tensor(tmpR[:, 256:384], tmpR[:, 0:128],
                                             sinb_sb, ALU.mult), waits=[C2], inc=s_v)
    CKR1 = s_v.mark()
    P.op("vector", lambda e: e.tensor_tensor(
        ckT_sb[0:32, :], tmpR[:, 128:256], tmpR[:, 256:384], ALU.subtract),
        waits=[CKR1], inc=s_v)
    SUB1 = s_v.mark()
    P.op("vector", lambda e: e.tensor_tensor(tmpR[:, 128:256], ckn_sb[0:32, :],
                                             sinb_sb, ALU.mult), waits=[SUB1], inc=s_v)
    P.op("vector", lambda e: e.tensor_tensor(tmpR[:, 256:384], tmpR[:, 0:128],
                                             cosb_sb, ALU.mult), waits=[SUB1], inc=s_v)
    CKR2 = s_v.mark()
    P.op("vector", lambda e: e.tensor_tensor(
        tmpo, tmpR[:, 128:256], tmpR[:, 256:384], ALU.add), waits=[CKR2], inc=s_v)
    CKO2 = s_v.mark()
    P.op("vector", lambda e: e.tensor_copy(ckT_sb[32:64, :], tmpo),
         waits=[CKO2], inc=s_v)
    P.op("vector", lambda e: e.tensor_copy(ckT_sb[64:128, :], ckn_sb[64:128, :]),
         waits=[CKN], inc=s_v)
    CKT = s_v.mark()
    cvw = [ENT, WKV] + P.ps_take("2b")
    P.op("tensor", lambda e: e.matmul(PS[2][:, 512:640], onesr[0:1, 0:128],
                                      brow[0:1, 1152:1280], start=True, stop=False),
         waits=cvw)
    for hc in range(16):
        P.op("tensor", lambda e, h=hc: e.matmul(
            PS[2][:, 512:640], entT[:, h * 128:(h + 1) * 128],
            wv_sb[:, h * 128:(h + 1) * 128], start=False, stop=(h == 15)),
            inc=s_t if hc == 15 else None)
    CVM = s_t.mark()
    P.op("scalar", lambda e: e.activation(cv_sb, PS[2][:, 512:640], AF.Copy),
         waits=[CVM], inc=s_s)
    P.ps_done("2b", s_s.mark())
    CV = s_s.mark()
    if "ckT" in DEBUG_TAPS:
        t_ = tap("ckT", (128, 128), BF16)
        P.op("sync", lambda e, t2=t_: e.dma_start(out=t2, in_=ckT_sb), waits=[CKT],
             inc=s_st, inc_n=16)
    if "cv" in DEBUG_TAPS:
        t_ = tap("cv", (128, 128), BF16)
        P.op("sync", lambda e, t2=t_: e.dma_start(out=t2, in_=cv_sb), waits=[CV],
             inc=s_st, inc_n=16)

    qbw = [ENTM, CONSTS] + P.ps_take("3b")
    P.op("tensor", lambda e: e.matmul(PS[3][:, 512:640], onesr[0:1, 0:128], qnr_sb,
                                      start=True, stop=True), waits=qbw)
    P.op("tensor", lambda e: e.matmul(PS[3][:, 640:768], onesr[0:1, 0:128], knr_sb,
                                      start=True, stop=True), inc=s_t)
    QNB = s_t.mark()
    P.op("vector", lambda e: e.tensor_copy(qnwbc, PS[3][:, 512:640]), waits=[QNB],
         inc=s_v)
    P.op("vector", lambda e: e.tensor_copy(knwbc, PS[3][:, 640:768]), inc=s_v)
    P.ps_done("3b", s_v.mark())
    QNW = s_v.mark()
    P.op("vector", lambda e: e.memset(qtab2, 0.0),
         waits=[QNW, CONSTS, CKT, CV], inc=s_v)
    P.op("vector", lambda e: e.memset(ktab2, 0.0), inc=s_v)
    P.op("vector", lambda e: e.memset(ktab2t8, 0.0), inc=s_v)
    P.op("vector", lambda e: e.tensor_scalar(
        negq, qnwbc[:, 32:64], -1.0, None, ALU.mult), inc=s_v)
    P.op("vector", lambda e: e.tensor_scalar(
        negk, knwbc[:, 32:64], -1.0, None, ALU.mult), inc=s_v)
    Z0 = s_v.mark()
    for t in range(8):
        P.op("vector", lambda e, t_=t: e.tensor_tensor(
            qtab[:, t_ * 128:t_ * 128 + 32], cosq_sb[:, t_ * 32:(t_ + 1) * 32],
            qnwbc[:, 0:32], ALU.mult), waits=[Z0] if t == 0 else ())
        P.op("vector", lambda e, t_=t: e.tensor_tensor(
            qtab[:, t_ * 128 + 32:t_ * 128 + 64],
            cosq_sb[:, t_ * 32:(t_ + 1) * 32], qnwbc[:, 32:64], ALU.mult))
        P.op("vector", lambda e, t_=t: e.tensor_copy(
            qtab[:, t_ * 128 + 64:t_ * 128 + 128], qnwbc[:, 64:128]))
        P.op("vector", lambda e, t_=t: e.tensor_tensor(
            qtab2[:, t_ * 128:t_ * 128 + 32], sinq_sb[:, t_ * 32:(t_ + 1) * 32],
            negq, ALU.mult))
        P.op("vector", lambda e, t_=t: e.tensor_tensor(
            qtab2[:, t_ * 128 + 32:t_ * 128 + 64],
            sinq_sb[:, t_ * 32:(t_ + 1) * 32], qnwbc[:, 0:32], ALU.mult))
    for t in range(9):
        P.op("vector", lambda e, t_=t: e.tensor_tensor(
            ktab[:, t_ * 128:t_ * 128 + 32], cosk_sb[:, t_ * 32:(t_ + 1) * 32],
            knwbc[:, 0:32], ALU.mult))
        P.op("vector", lambda e, t_=t: e.tensor_tensor(
            ktab[:, t_ * 128 + 32:t_ * 128 + 64],
            cosk_sb[:, t_ * 32:(t_ + 1) * 32], knwbc[:, 32:64], ALU.mult))
        P.op("vector", lambda e, t_=t: e.tensor_copy(
            ktab[:, t_ * 128 + 64:t_ * 128 + 128], knwbc[:, 64:128]))
        P.op("vector", lambda e, t_=t: e.tensor_tensor(
            kt2tile(t_)[:, 0:32], sink_sb[:, t_ * 32:(t_ + 1) * 32],
            negk, ALU.mult))
        P.op("vector", lambda e, t_=t: e.tensor_tensor(
            kt2tile(t_)[:, 32:64],
            sink_sb[:, t_ * 32:(t_ + 1) * 32], knwbc[:, 0:32], ALU.mult),
            inc=s_v if t == 8 else None)
    TABS = s_v.mark()

    def kt_off(kt):
        return 1024 if kt == 0 else (kt - 1) * 128

    wbufs = [wbuf_bf, aC[:, 0:8192]]
    wbuf_free = {}
    bias_free = {}
    ATT0 = [TABS, CV, CKT, W_PRE]
    for g in range(4):
        WLDs = {}
        BLDs = {}
        for wi, (Wsrc, bsrc) in enumerate([(Wq, bq), (Wlk, blk)]):
            wslot = (wi + g) % 2
            wbuf = wbufs[wslot]
            wlw = list(ATT0) if (g == 0) else []
            if wslot in wbuf_free:
                wlw.append(wbuf_free[wslot])
            P.op("gpsimd", lambda e, W=Wsrc, g_=g, wb=wbuf: e.dma_start(
                out=wb.rearrange("p (c f) -> p c f", c=16),
                in_=W[:, g_ * 512:(g_ + 1) * 512].rearrange(
                    "(c p) f -> p c f", p=128)), waits=wlw, inc=s_w, inc_n=16)
            WLDs[wi] = s_w.mark()
            bld_w = [bias_free[wslot]] if wslot in bias_free else []
            ld(brow[0:1, wslot * 512:wslot * 512 + 512],
               bsrc[0:1, g * 512:(g + 1) * 512], waits=bld_w)
            BLDs[wi] = s_c.mark()

        jobs = [(0, t) for t in range(8)] + [(1, t) for t in range(9)]
        NJ = len(jobs)
        MM = {}; SQm = {}; SS = {}; INV = {}; APP = {}
        ROPE = {}; TR = {}
        for i in range(NJ + 2):
            if i < NJ:
                wi, t = jobs[i]
                xt0 = t * 128 if wi == 0 else kt_off(t)
                key = "0a" if i % 2 == 0 else "1a"
                ps = PS[0] if i % 2 == 0 else PS[1]
                wbuf = wbufs[(wi + g) % 2]
                br = brow[0:1, ((wi + g) % 2) * 512:((wi + g) % 2) * 512 + 512]
                mmw = [WLDs[wi], BLDs[wi]] + P.ps_take(key)
                P.op("tensor", lambda e, p=ps, b2=br: e.matmul(
                    p[:, 0:512], onesr[0:1, 0:128], b2,
                    start=True, stop=False), waits=mmw)
                for hc in range(16):
                    P.op("tensor", lambda e, h=hc, x0=xt0, p=ps, wb=wbuf:
                         e.matmul(p[:, 0:512],
                                  xnT[:, h * TKT + x0: h * TKT + x0 + 128],
                                  wb[:, h * 512:(h + 1) * 512],
                                  start=False, stop=(h == 15)),
                         inc=s_t if hc == 15 else None)
                MM[i] = s_t.mark()
                if i == 7:
                    wbuf_free[g % 2] = MM[i]
                    bias_free[g % 2] = MM[i]
                if i == NJ - 1:
                    wbuf_free[1 - g % 2] = MM[i]
                    bias_free[1 - g % 2] = MM[i]
            if i >= 2 and i - 2 < NJ:
                j = i - 2
                trw = [ROPE[j]] + P.ps_take("2t")
                for h4 in range(4):
                    P.op("tensor", lambda e, h=h4: e.transpose(
                        PSbf[2][:, h * 128:(h + 1) * 128],
                        q4r[:, h * 128:(h + 1) * 128], id_sb),
                        waits=trw if h4 == 0 else (),
                        inc=s_t if h4 == 3 else None)
                TR[j] = s_t.mark()
            if i < NJ:
                ps = PS[0] if i % 2 == 0 else PS[1]
                sqw = [MM[i]]
                if i >= 1:
                    sqw.append(SS[i - 1])
                P.op("scalar", lambda e, p=ps: e.activation(
                    sqb, p[:, 0:512], AF.Square), waits=sqw, inc=s_s)
                SQm[i] = s_s.mark()
            if i >= 1 and i - 1 < NJ:
                j = i - 1
                sc0 = (j % 2) * 16
                P.op("scalar", lambda e, c=sc0: e.activation(
                    ssq[:, c + 4:c + 8], ssq[:, c + 0:c + 4], AF.Ln,
                    bias=EPS_RMS, scale=1.0 / 128), waits=[SS[j]], inc=s_s)
                P.op("scalar", lambda e, c=sc0: e.activation(
                    ssq[:, c + 8:c + 12], ssq[:, c + 4:c + 8], AF.Exp,
                    scale=-0.5), inc=s_s)
                INV[j] = s_s.mark()
                psj = PS[0] if j % 2 == 0 else PS[1]
                apw = [INV[j]]
                if j >= 1:
                    apw.append(ROPE[j - 1])
                for h4 in range(4):
                    P.op("scalar", lambda e, h=h4, p=psj, c=sc0: e.activation(
                        qn2[:, h * 128:(h + 1) * 128],
                        p[:, h * 128:(h + 1) * 128],
                        AF.Copy, scale=ssq[:, c + 8 + h:c + 9 + h]),
                        waits=apw if h4 == 0 else (),
                        inc=s_s if h4 == 3 else None)
                APP[j] = s_s.mark()
                P.ps_done("0a" if j % 2 == 0 else "1a", APP[j])
            if i < NJ:
                sc0 = (i % 2) * 16
                P.op("vector", lambda e, c=sc0: e.reduce_sum(
                    ssq[:, c:c + 4].rearrange("p (a b) -> p a b", a=4),
                    sqb.rearrange("p (a b) -> p a b", a=4), axis=AX.X),
                    waits=[SQm[i]], inc=s_v)
                SS[i] = s_v.mark()
            if i >= 1 and i - 1 < NJ:
                j = i - 1
                wiJ, tJ = jobs[j]
                tabsel = qtab if wiJ == 0 else ktab
                t2tile = (qtab2[:, tJ * 128:(tJ + 1) * 128] if wiJ == 0
                          else kt2tile(tJ))
                tab0 = tJ * 128
                rpw = [APP[j]]
                if j >= 1:
                    rpw.append(TR[j - 1])
                qn3 = qn2.rearrange("p (h s q) -> p h s q", h=4, s=4)
                sw3 = swb.rearrange("p (h s q) -> p h s q", h=4, s=4)
                qn2h = qn2.rearrange("p (h s q) -> p h s q", h=4, s=2)
                sw2h = swb.rearrange("p (h s q) -> p h s q", h=4, s=2)
                P.op("vector", lambda e, a=sw3, b=qn3: e.tensor_copy(
                    a[:, :, 0, :], b[:, :, 1, :]), waits=rpw, inc=s_v)
                P.op("vector", lambda e, a=sw3, b=qn3: e.tensor_copy(
                    a[:, :, 1, :], b[:, :, 0, :]), inc=s_v)
                P.op("vector", lambda e, a=sw2h, b=qn2h: e.tensor_copy(
                    a[:, :, 1, :], b[:, :, 1, :]), inc=s_v)
                CPm = s_v.mark()
                for h4 in range(4):
                    c0 = h4 * 128
                    P.op("vector", lambda e, c=c0, t0=tab0, tb=tabsel:
                         e.tensor_tensor(rtmp1[:, c:c + 128], qn2[:, c:c + 128],
                                         tb[:, t0:t0 + 128], ALU.mult),
                         inc=s_v)
                for h4 in range(4):
                    c0 = h4 * 128
                    P.op("vector", lambda e, c=c0, tb=t2tile:
                         e.tensor_tensor(swb[:, c:c + 128], swb[:, c:c + 128],
                                         tb, ALU.mult),
                         waits=[CPm] if h4 == 0 else (), inc=s_v)
                Mm = s_v.mark()
                for h4 in range(4):
                    c0 = h4 * 128
                    P.op("vector", lambda e, c=c0: e.tensor_tensor(
                        q4r[:, c:c + 128], rtmp1[:, c:c + 128],
                        swb[:, c:c + 128], ALU.add),
                        waits=[Mm] if h4 == 0 else (), inc=s_v)
                ROPE[j] = s_v.mark()
            if i >= 2 and i - 2 < NJ:
                j = i - 2
                wiJ, tJ = jobs[j]
                dstT = qT4 if wiJ == 0 else lkT4
                dW = 1024 if wiJ == 0 else 1152
                P.op("vector", lambda e, it_=tJ, d=dstT, w_=dW: e.tensor_copy(
                    d.rearrange("p (h t) -> p h t", t=w_)[
                        :, :, it_ * 128:(it_ + 1) * 128],
                    PSbf[2][:, 0:512].rearrange("p (h t) -> p h t", h=4)),
                    waits=[TR[j]], inc=s_v)
                P.ps_done("2t", s_v.mark())

        sv_ = g % 2
        wlw = [wbuf_free[sv_]]
        P.op("gpsimd", lambda e, g_=g, wb=wbufs[sv_]: e.dma_start(
            out=wb.rearrange("p (c f) -> p c f", c=16),
            in_=Wlv[:, g_ * 512:(g_ + 1) * 512].rearrange(
                "(c p) f -> p c f", p=128)), waits=wlw, inc=s_w, inc_n=16)
        WLDv = s_w.mark()
        ld(brow[0:1, sv_ * 512:sv_ * 512 + 512],
           blv[0:1, g * 512:(g + 1) * 512], waits=[bias_free[sv_]])
        BLDv = s_c.mark()
        wbuf = wbufs[sv_]
        for it in range(9):
            xt0 = kt_off(it)
            key = "0a" if it % 2 == 0 else "1a"
            ps = PS[0] if it % 2 == 0 else PS[1]
            mmw = [WLDv, BLDv] + P.ps_take(key)
            P.op("tensor", lambda e, p=ps, b2=brow[0:1, sv_ * 512:sv_ * 512 + 512]:
                 e.matmul(p[:, 0:512], onesr[0:1, 0:128], b2,
                          start=True, stop=False), waits=mmw)
            for hc in range(16):
                P.op("tensor", lambda e, h=hc, x0=xt0, p=ps, wb=wbuf:
                     e.matmul(p[:, 0:512],
                              xnT[:, h * TKT + x0: h * TKT + x0 + 128],
                              wb[:, h * 512:(h + 1) * 512],
                              start=False, stop=(h == 15)),
                     inc=s_t if hc == 15 else None)
            MMv = s_t.mark()
            P.op("vector", lambda e, it_=it, p=ps: e.tensor_copy(
                lv4.rearrange("p (h i d) -> p h i d", h=4, i=9)[:, :, it_, :],
                p[:, 0:512].rearrange("p (h d) -> p h d", h=4)),
                waits=[MMv], inc=s_v)
            P.ps_done(key, s_v.mark())
        wbuf_free[sv_] = s_t.mark()
        bias_free[sv_] = wbuf_free[sv_]
        PROJ = s_v.mark()
        if g == TAP_GROUP and "qT4" in DEBUG_TAPS:
            t_ = tap("qT4", (128, 4096), BF16)
            P.op("sync", lambda e, t2=t_: e.dma_start(out=t2, in_=qT4),
                 waits=[PROJ], inc=s_st, inc_n=16)
            t_ = tap("lkT4", (128, 4608), BF16)
            P.op("sync", lambda e, t2=t_: e.dma_start(out=t2, in_=lkT4),
                 waits=[PROJ], inc=s_st, inc_n=16)
            t_ = tap("lv4", (128, 4608), BF16)
            P.op("sync", lambda e, t2=t_: e.dma_start(out=t2, in_=lv4),
                 waits=[PROJ], inc=s_st, inc_n=16)
            ATT0.append(s_st.mark())

        SCM = {}; MSKD = {}; EXD = {}; SUMS = {}; SCL = {}; MX = {}
        TRp = {}; EVp = {}; PVm = {}; CTXE = {}
        NPAIR = 16
        def pr_h(p):
            return p // 4
        def pr_qt(p):
            return (p % 4) * 2
        for i in range(NPAIR + 3):
            if i < NPAIR:
                h4 = pr_h(i); qt = pr_qt(i); h = g * 4 + h4
                psL = PS[0][:, 512:1024] if i % 2 == 0 else PS[3][:, 0:512]
                psC = PS[1][:, 512:1024] if i % 2 == 0 else PS[3][:, 512:1024]
                kL = "0b" if i % 2 == 0 else "3a"
                kC = "1b" if i % 2 == 0 else "3b"
                scw = [PROJ] + P.ps_take(kL, kC)
                for k2 in range(2):
                    P.op("tensor", lambda e, h_=h4, q_=qt + k2, p=psL, o=k2 * 256:
                         e.matmul(p[:, o:o + 256],
                                  qT4[:, h_ * 1024 + q_ * 128: h_ * 1024 + (q_ + 1) * 128],
                                  lkT4[:, h_ * 1152 + q_ * 128: h_ * 1152 + q_ * 128 + 256],
                                  start=True, stop=True),
                         waits=scw if k2 == 0 else ())
                    P.op("tensor", lambda e, h_=h4, q_=qt + k2, p=psC, o=k2 * 256:
                         e.matmul(p[:, o:o + 128],
                                  qT4[:, h_ * 1024 + q_ * 128: h_ * 1024 + (q_ + 1) * 128],
                                  ckT_sb, start=True, stop=False))
                    P.op("tensor", lambda e, hh=h, h_=h4, q_=qt + k2, p=psC, o=k2 * 256:
                         e.matmul(p[:, o + 128:o + 132],
                                  qT4[:, h_ * 1024 + q_ * 128: h_ * 1024 + (q_ + 1) * 128],
                                  sinkTk_sb[:, hh:hh + 4], start=False, stop=True),
                         inc=s_t if k2 == 1 else None)
                SCM[i] = s_t.mark()
            if i >= 1 and i - 1 < NPAIR:
                j = i - 1
                mb = 8 + (j % 2) * 12
                psL = PS[0][:, 512:1024] if j % 2 == 0 else PS[3][:, 0:512]
                psC = PS[1][:, 512:1024] if j % 2 == 0 else PS[3][:, 512:1024]
                mxw = [SCM[j]]
                if j >= 2:
                    mxw.append(EXD[j - 2])
                P.op("vector", lambda e, p=psL, b=mb: e.reduce_max(
                    rrp[:, b:b + 2].rearrange("p (a b2) -> p a b2", a=2),
                    p.rearrange("p (a b2) -> p a b2", a=2), axis=AX.X,
                    negate=True), waits=mxw, inc=s_v)
                P.op("vector", lambda e, p=psC, b=mb: e.reduce_max(
                    rrp[:, b + 2:b + 4].rearrange("p (a b2) -> p a b2", a=2),
                    p.rearrange("p (a b2) -> p a b2", a=2)[:, :, 0:132], axis=AX.X,
                    negate=True), inc=s_v)
                P.op("vector", lambda e, b=mb: e.tensor_scalar(
                    rrp[:, b + 4:b + 8], rrp[:, b:b + 4], SCALE, None, ALU.mult),
                    inc=s_v)
                MX[j] = s_v.mark()
            if i >= 3 and i - 3 < NPAIR:
                j = i - 3
                evw = [TRp[j]]
                if j >= 1:
                    evw.append(PVm[j - 1])
                P.op("scalar", lambda e: e.activation(pTl, PSbf[2][:, 0:512],
                                                      AF.Copy), waits=evw, inc=s_s)
                P.op("scalar", lambda e: e.activation(pTc, PSbf[2][:, 512:768],
                                                      AF.Copy), inc=s_s)
                P.op("scalar", lambda e: e.activation(pTs[0:1, :],
                                                      PSbf[2][0:1, 768:1024],
                                                      AF.Copy), inc=s_s)
                EVp[j] = s_s.mark()
                P.ps_done("2t", EVp[j])
            if i >= 2 and i - 2 < NPAIR:
                j = i - 2
                trw = [SCL[j]] + P.ps_take("2t")
                pljj = pl2[(j % 2)]
                pcjj = pc2[(j % 2)]
                for k2 in range(4):
                    P.op("tensor", lambda e, k=k2, pj=pljj: e.transpose(
                        PSbf[2][:, k * 128:(k + 1) * 128],
                        pj[:, k * 128:(k + 1) * 128], id_sb),
                        waits=trw if k2 == 0 else ())
                for k2 in range(2):
                    P.op("tensor", lambda e, k=k2, pj=pcjj: e.transpose(
                        PSbf[2][:, 512 + k * 128: 512 + (k + 1) * 128],
                        pj[:, k * 132:k * 132 + 128], id_sb))
                for k2 in range(2):
                    P.op("tensor", lambda e, k=k2, pj=pcjj: e.transpose(
                        PSbf[2][0:1, 768 + k * 128: 768 + (k + 1) * 128],
                        pj[:, k * 132 + 128:k * 132 + 129], id_sb),
                        inc=s_t if k2 == 1 else None)
                TRp[j] = s_t.mark()
            if i >= 3 and i - 3 < NPAIR:
                j = i - 3
                h4 = pr_h(j); qt = pr_qt(j); h = g * 4 + h4
                cx0 = 512 if j % 2 == 0 else 768
                pvw = [EVp[j]] + P.ps_take("ctx_e" if j % 2 == 0 else "ctx_o")
                for k2 in range(2):
                    q_ = qt + k2
                    pcx = PS[2][:, cx0 + k2 * 128: cx0 + (k2 + 1) * 128]
                    P.op("tensor", lambda e, h_=h4, q2=q_, p=pcx, k=k2: e.matmul(
                        p, lv4[:, h_ * 1152 + q2 * 128: h_ * 1152 + (q2 + 1) * 128],
                        pTl[:, k * 256:k * 256 + 128], start=True, stop=False),
                        waits=pvw if k2 == 0 else ())
                    P.op("tensor", lambda e, h_=h4, q2=q_, p=pcx, k=k2: e.matmul(
                        p, lv4[:, h_ * 1152 + (q2 + 1) * 128: h_ * 1152 + (q2 + 2) * 128],
                        pTl[:, k * 256 + 128:k * 256 + 256], start=False, stop=False))
                    P.op("tensor", lambda e, p=pcx, k=k2: e.matmul(
                        p, cv_sb, pTc[:, k * 128:(k + 1) * 128],
                        start=False, stop=False))
                    P.op("tensor", lambda e, hh=h, p=pcx, k=k2: e.matmul(
                        p, sinkv_sb[0:1, hh * 128:(hh + 1) * 128],
                        pTs[0:1, k * 128:(k + 1) * 128], start=False, stop=True),
                        inc=s_t if k2 == 1 else None)
                PVm[j] = s_t.mark()
            if i >= 1 and i - 1 < NPAIR:
                j = i - 1
                psL = PS[0][:, 512:1024] if j % 2 == 0 else PS[3][:, 0:512]
                psC = PS[1][:, 512:1024] if j % 2 == 0 else PS[3][:, 512:1024]
                plj = pl2[(j % 2)]
                pcj = pc2[(j % 2)]
                mb = 8 + (j % 2) * 12
                exw = [SCM[j], MX[j]]
                if j >= 2:
                    exw.append(SCL[j - 2])
                for k2 in range(2):
                    P.op("scalar", lambda e, p=psL, pj=plj, k=k2, b=mb: e.activation(
                        pj[:, k * 256:(k + 1) * 256], p[:, k * 256:(k + 1) * 256],
                        AF.Exp, scale=SCALE, bias=rrp[:, b + 4 + k:b + 5 + k]),
                        waits=exw if k2 == 0 else (), inc=s_s)
                    P.op("scalar", lambda e, p=psC, pj=pcj, k=k2, b=mb: e.activation(
                        pj[:, k * 132:(k + 1) * 132],
                        p[:, k * 256:k * 256 + 132],
                        AF.Exp, scale=SCALE, bias=rrp[:, b + 6 + k:b + 7 + k]),
                        inc=s_s)
                EXD[j] = s_s.mark()
                P.ps_done("0b" if j % 2 == 0 else "3a", EXD[j])
                P.ps_done("1b" if j % 2 == 0 else "3b", EXD[j])
            if i >= 1 and i - 1 < NPAIR:
                j = i - 1
                qt = pr_qt(j)
                plj = pl2[(j % 2)]
                pcj = pc2[(j % 2)]
                mk = mfirstM_sb if qt == 0 else mrestM_sb
                P.op("vector", lambda e, pj=plj, m=mk: e.tensor_tensor(
                    pj, pj, m, ALU.mult), waits=[EXD[j]], inc=s_v)
                P.op("vector", lambda e, pj=pcj, q_=qt: e.tensor_tensor(
                    pj, pj, cmaskM_sb[:, q_ * CW:(q_ + 2) * CW], ALU.mult), inc=s_v)
                MSKD[j] = s_v.mark()
                P.op("vector", lambda e, pj=plj: e.reduce_sum(
                    rrp[:, 0:2].rearrange("p (a b) -> p a b", a=2),
                    pj.rearrange("p (a b) -> p a b", a=2), axis=AX.X),
                    waits=[MSKD[j]], inc=s_v)
                P.op("vector", lambda e, pj=pcj: e.reduce_sum(
                    rrp[:, 2:4].rearrange("p (a b) -> p a b", a=2),
                    pj.rearrange("p (a b) -> p a b", a=2), axis=AX.X), inc=s_v)
                SUMS[j] = s_v.mark()
                P.op("vector", lambda e: e.reciprocal(rrp[:, 4:8], rrp[:, 0:4]),
                     waits=[SUMS[j]], inc=s_v)
                RCm = s_v.mark()
                for k2 in range(2):
                    P.op("vector", lambda e, pj=plj, k=k2: e.tensor_scalar(
                        pj[:, k * 256:(k + 1) * 256], pj[:, k * 256:(k + 1) * 256],
                        rrp[:, 4 + k:5 + k], 0.5, ALU.mult, ALU.mult),
                        waits=[RCm] if k2 == 0 else (), inc=s_v)
                    P.op("vector", lambda e, pj=pcj, k=k2: e.tensor_scalar(
                        pj[:, k * 132:(k + 1) * 132], pj[:, k * 132:(k + 1) * 132],
                        rrp[:, 6 + k:7 + k], 0.5, ALU.mult, ALU.mult), inc=s_v)
                SCL[j] = s_v.mark()
            if i >= 3 and i - 3 < NPAIR:
                j = i - 3
                h4 = pr_h(j); qt = pr_qt(j); h = g * 4 + h4
                cx0 = 512 if j % 2 == 0 else 768
                P.op("vector", lambda e, hh=h, q_=qt, c=cx0: e.tensor_copy(
                    ctxT[:, hh * 1024 + q_ * 128: hh * 1024 + (q_ + 2) * 128],
                    PS[2][:, c:c + 256]), waits=[PVm[j]], inc=s_v)
                CTXE[j] = s_v.mark()
                P.ps_done("ctx_e" if j % 2 == 0 else "ctx_o", CTXE[j])
    CTX = s_v.mark()
    if "ctxT" in DEBUG_TAPS:
        t_ = tap("ctxT", (128, 16 * 1024), BF16)
        P.op("sync", lambda e, t2=t_: e.dma_start(out=t2, in_=ctxT), waits=[CTX],
             inc=s_st, inc_n=16)

    P.op("gpsimd", lambda e: e.dma_start(out=smt_d, in_=smt_sb[0:8, :]),
         waits=[SMT], inc=s_w, inc_n=16)
    SMT_ST = s_w.mark()
    P.op("gpsimd", lambda e: e.collective_compute(
        "AllGather", ALU.bypass, replica_groups=[list(range(8))],
        ins=[smt_d.opt()], outs=[gath_d.opt()]), waits=[SMT_ST], inc=s_cc)
    for gi in range(2):
        P.op("gpsimd", lambda e, g_=gi: e.dma_start(
            out=bis_sb[g_::2, :],
            in_=gath_d.rearrange("(r e) (g2 f) -> e r g2 f", e=8, g2=2)[:, :, g_, :]),
            waits=[(s_cc, 1)] if gi == 0 else (), inc=s_w, inc_n=16)
    BIS_LD = s_w.mark()
    if "bis" in DEBUG_TAPS:
        t_ = tap("bis", (128, 512))
        P.op("sync", lambda e, t2=t_: e.dma_start(out=t2, in_=bis_sb),
             waits=[BIS_LD], inc=s_st, inc_n=16)

    bis = {"k": 0, "cur": None}

    def bis_init():
        P.op("vector", lambda e: e.memset(bst[0:8, 0:1], 0.0),
             waits=[BIS_LD, CTX], inc=s_v)
        P.op("vector", lambda e: e.memset(bst[0:8, 1:2], 1.1), inc=s_v)
        P.op("vector", lambda e: e.memset(bst[0:8, 2:3], 0.55), inc=s_v)
        P.op("vector", lambda e: e.memset(mid128[:, 0:1], 0.55), inc=s_v)
        bis["cur"] = s_v.mark()

    def bis_iter():
        it = bis["k"]
        bis["k"] += 1
        a = (it % 2) * 8
        b_ = 8 - a
        m128 = mid128[:, it % 2:it % 2 + 1]
        m128n = mid128[:, 1 - it % 2:2 - it % 2]
        P.op("vector", lambda e, m=m128: e.tensor_scalar(
            trashA, bis_sb, m, 0.0, ALU.is_ge, ALU.add, accum_out=cnt128),
            waits=[bis["cur"]], inc=s_v)
        CN = s_v.mark()
        P.op("tensor", lambda e: e.matmul(PS[3][0:8, 256:257], m8_sb, cnt128,
                                          start=True, stop=True),
             waits=[CN] + P.ps_take("3a"), inc=s_t)
        CM = s_t.mark()
        P.op("vector", lambda e, aa=a: e.tensor_scalar(
            bst[0:8, aa + 3:aa + 4], PS[3][0:8, 256:257], KCAP, None, ALU.is_ge),
            waits=[CM], inc=s_v)
        P.ps_done("3a", s_v.mark())
        P.op("vector", lambda e, aa=a: e.tensor_tensor(
            bst[0:8, aa + 4:aa + 5], bst[0:8, aa + 2:aa + 3],
            bst[0:8, aa + 0:aa + 1], ALU.subtract), waits=[s_v.mark()], inc=s_v)
        P.op("vector", lambda e, aa=a: e.tensor_tensor(
            bst[0:8, aa + 5:aa + 6], bst[0:8, aa + 1:aa + 2],
            bst[0:8, aa + 2:aa + 3], ALU.subtract), inc=s_v)
        DD = s_v.mark()
        P.op("vector", lambda e, aa=a, bb=b_: e.scalar_tensor_tensor(
            bst[0:8, bb + 0:bb + 1], bst[0:8, aa + 3:aa + 4],
            bst[0:8, aa + 4:aa + 5], bst[0:8, aa + 0:aa + 1], ALU.mult, ALU.add),
            waits=[DD], inc=s_v)
        P.op("vector", lambda e, aa=a, bb=b_: e.scalar_tensor_tensor(
            bst[0:8, bb + 1:bb + 2], bst[0:8, aa + 3:aa + 4],
            bst[0:8, aa + 5:aa + 6], bst[0:8, aa + 2:aa + 3], ALU.mult, ALU.add),
            inc=s_v)
        LH = s_v.mark()
        P.op("vector", lambda e, bb=b_: e.tensor_tensor(
            bst[0:8, bb + 6:bb + 7], bst[0:8, bb + 1:bb + 2],
            bst[0:8, bb + 0:bb + 1], ALU.subtract), waits=[LH], inc=s_v)
        DD2 = s_v.mark()
        P.op("vector", lambda e, bb=b_: e.scalar_tensor_tensor(
            bst[0:8, bb + 2:bb + 3], bst[0:8, bb + 6:bb + 7], 0.5,
            bst[0:8, bb + 0:bb + 1], ALU.mult, ALU.add), waits=[DD2], inc=s_v)
        MD = s_v.mark()
        if it + 1 < BISECT_ITERS:
            P.op("tensor", lambda e, bb=b_: e.matmul(
                PS[3][:, 384:385], seg16T_sb, bst[0:8, bb + 2:bb + 3],
                start=True, stop=True), waits=[MD] + P.ps_take("3a"), inc=s_t)
            MB = s_t.mark()
            P.op("vector", lambda e, m=m128n: e.tensor_copy(m, PS[3][:, 384:385]),
                 waits=[MB], inc=s_v)
            P.ps_done("3a", s_v.mark())
            bis["cur"] = s_v.mark()
        else:
            bis["cur"] = MD

    bis_init()
    prev_ev = {}
    prev_mmdone = None
    mmdone = {}
    chunk_i = 0
    for oc in range(4):
        woslot = oc % 2
        wosb = wbufs[woslot]
        wol = [wbuf_free[woslot]] if oc < 2 else [mmdone[oc - 2]]
        P.op("gpsimd", lambda e, o=oc, wb=wosb: e.dma_start(
            out=wb.rearrange("p (c f) -> p c f", c=16),
            in_=Wo[:, o * 512:(o + 1) * 512].rearrange(
                "(c p) f -> p c f", p=128)), waits=wol, inc=s_w, inc_n=16)
        WOL = s_w.mark()
        if oc == 0:
            ld(borow, bo)
        BOL = s_c.mark()
        for tt in range(8):
            sl4 = tt % 4
            key = f"{sl4 // 2}{'a' if sl4 % 2 == 0 else 'b'}"
            ps8 = PS[sl4 // 2][:, (sl4 % 2) * 512:(sl4 % 2) * 512 + 512]
            bmw = [WOL, BOL, CTX] + P.ps_take(key)
            P.op("tensor", lambda e, p=ps8, o=oc: e.matmul(
                p, onesr[0:1, 0:128], borow[0:1, o * 512:(o + 1) * 512],
                start=True, stop=False), waits=bmw)
            for dc in range(16):
                P.op("tensor", lambda e, d=dc, t=tt, p=ps8, wb=wosb: e.matmul(
                    p, ctxT[:, d * 1024 + t * 128: d * 1024 + (t + 1) * 128],
                    wb[:, d * 512:(d + 1) * 512], start=False, stop=(d == 15)),
                    inc=s_t if dc == 15 else None)
            WM = s_t.mark()
            evw = [WM]
            if (tt % 2) in prev_ev:
                evw.append(prev_ev[tt % 2])
            P.op("scalar", lambda e, p=ps8, sl_=tt % 2: e.activation(
                wo_stage[:, sl_ * 512:(sl_ + 1) * 512], p, AF.Copy),
                waits=evw, inc=s_s)
            EV = s_s.mark()
            P.ps_done(key, EV)
            P.op("sync", lambda e, t=tt, o=oc, sl_=tt % 2: e.dma_start(
                out=ao_d[t * 128:(t + 1) * 128, o * 512:(o + 1) * 512],
                in_=wo_stage[:, sl_ * 512:(sl_ + 1) * 512]), waits=[EV],
                inc=s_st, inc_n=16)
            prev_ev[tt % 2] = s_st.mark()
            chunk_i += 1
            while bis["k"] < min(BISECT_ITERS, int(chunk_i * 1.4)):
                bis_iter()
        mmdone[oc] = s_t.mark()
        prev_mmdone = mmdone[oc]
    while bis["k"] < BISECT_ITERS:
        bis_iter()
    AO_ST = s_st.mark()
    if "ao" in DEBUG_TAPS:
        t_ = tap("ao", (TOK, HID))
        P.op("sync", lambda e, t2=t_: e.dma_start(out=t2, in_=ao_d),
             waits=[AO_ST], inc=s_st, inc_n=16)

    fin = (BISECT_ITERS % 2) * 8
    P.op("vector", lambda e, ff=fin: e.scalar_tensor_tensor(
        sT_row[0:8, :], smt_sb[0:8, :], bst[0:8, ff:ff + 1], smt_sb[0:8, :],
        ALU.is_ge, ALU.mult), waits=[bis["cur"]], inc=s_v)
    STR = s_v.mark()
    sclw = [STR] + P.ps_take("3b")
    for tt in range(8):
        P.op("tensor", lambda e, t=tt: e.matmul(
            PS[3][:, 512 + t:513 + t], sT_row[0:8, t * 128:(t + 1) * 128],
            ones8, start=True, stop=True),
            waits=sclw if tt == 0 else (), inc=s_t if tt == 7 else None)
    SCM2 = s_t.mark()
    P.op("vector", lambda e: e.tensor_copy(s_col, PS[3][:, 512:520]),
         waits=[SCM2], inc=s_v)
    P.ps_done("3b", s_v.mark())
    SCOL = s_v.mark()
    if "scol" in DEBUG_TAPS:
        t_ = tap("scol", (128, 8))
        P.op("sync", lambda e, t2=t_: e.dma_start(out=t2, in_=s_col), waits=[SCOL],
             inc=s_st, inc_n=16)
    if "thr" in DEBUG_TAPS:
        t_ = tap("thr", (8, 16))
        P.op("sync", lambda e, t2=t_: e.dma_start(out=t2, in_=bst), waits=[SCOL],
             inc=s_st, inc_n=16)

    prev_sg = {}
    prev_hT = {}
    wff_free = {}
    for f in range(NFT):
        wgb = wff[(f % 2) * 2]
        wub = wff[(f % 2) * 2 + 1]
        wwl = [prev_mmdone, AO_ST] if f < 2 else [wff_free[f % 2]]
        P.op("gpsimd", lambda e, ff_=f, wb=wgb: e.dma_start(
            out=wb.rearrange("p (c f) -> p c f", c=16),
            in_=wg[:, ff_ * 128:(ff_ + 1) * 128].rearrange(
                "(c p) f -> p c f", p=128)), waits=wwl, inc=s_w, inc_n=16)
        P.op("gpsimd", lambda e, ff_=f, wb=wub: e.dma_start(
            out=wb.rearrange("p (c f) -> p c f", c=16),
            in_=wu[:, ff_ * 128:(ff_ + 1) * 128].rearrange(
                "(c p) f -> p c f", p=128)), inc=s_w, inc_n=16)
        WF = s_w.mark()
        kg = "0a0b" if f % 2 == 0 else "2a2b"
        psg = PS[0] if f % 2 == 0 else PS[2]
        psu = PS[1] if f % 2 == 0 else PS[3]
        kga, kgb = (("0a", "0b") if f % 2 == 0 else ("2a", "2b"))
        kua, kub = (("1a", "1b") if f % 2 == 0 else ("3a", "3b"))
        gw = [WF, XNT_OWN] + P.ps_take(kga, kgb)
        for n2 in range(2):
            for hc in range(16):
                P.op("tensor", lambda e, h=hc, n=n2, p=psg, wb=wgb: e.matmul(
                    p[:, n * 512:(n + 1) * 512], wb[:, h * 128:(h + 1) * 128],
                    xnT[:, h * TKT + n * 512: h * TKT + (n + 1) * 512],
                    start=(h == 0), stop=(h == 15)),
                    waits=gw if (n2 == 0 and hc == 0) else ())
        uw = P.ps_take(kua, kub)
        for n2 in range(2):
            for hc in range(16):
                P.op("tensor", lambda e, h=hc, n=n2, p=psu, wb=wub: e.matmul(
                    p[:, n * 512:(n + 1) * 512], wb[:, h * 128:(h + 1) * 128],
                    xnT[:, h * TKT + n * 512: h * TKT + (n + 1) * 512],
                    start=(h == 0), stop=(h == 15)),
                    waits=uw if (n2 == 0 and hc == 0) else (),
                    inc=s_t if (n2 == 1 and hc == 15) else None)
        GU = s_t.mark()
        wff_free[f % 2] = GU
        sgb = sg_sb[:, (f % 2) * 1024:(f % 2 + 1) * 1024]
        sgw = [GU, CONSTS]
        if (f % 2) in prev_sg:
            sgw.append(prev_sg[f % 2])
        P.op("scalar", lambda e, ff_=f, p=psg, sg_=sgb: e.activation(
            sg_, p, AF.Silu, bias=bgc_sb[:, ff_:ff_ + 1]), waits=sgw, inc=s_s)
        SG = s_s.mark()
        P.ps_done(kga, SG); P.ps_done(kgb, SG)
        P.op("vector", lambda e, ff_=f, p=psu, sg_=sgb: e.scalar_tensor_tensor(
            hT_tile(ff_), p, buc_sb[:, ff_:ff_ + 1], sg_, ALU.add, ALU.mult),
            waits=[SG], inc=s_v)
        HTm = s_v.mark()
        P.ps_done(kua, HTm); P.ps_done(kub, HTm)
        prev_sg[f % 2] = HTm
        prev_hT[f % 2] = HTm
    HT = s_v.mark()
    if "hT0" in DEBUG_TAPS:
        t_ = tap("hT0", (128, 1024), BF16)
        P.op("sync", lambda e, t2=t_: e.dma_start(out=t2, in_=hT_tile(0)),
             waits=[HT], inc=s_st, inc_n=16)
    if "hT42" in DEBUG_TAPS:
        t_ = tap("hT42", (128, 1024), BF16)
        P.op("sync", lambda e, t2=t_: e.dma_start(out=t2, in_=hT_tile(42)),
             waits=[HT], inc=s_st, inc_n=16)

    FBLK = [list(range(i, min(i + 4, NFT))) for i in range(0, NFT, 4)]
    xnTf = xnT.bitcast(F32)
    ystage4 = [xnTf[:, k * 512:(k + 1) * 512] for k in range(8)]
    aostage4 = [xnTf[:, 4096 + k * 512: 4096 + (k + 1) * 512] for k in range(8)]
    assert xnTf.shape[1] >= 8192
    wdb4 = [aC[:, k * 2048:(k + 1) * 2048] for k in range(4)]
    wd_free = {}
    ystore = {}
    for oc in range(4):
        if oc == 0:
            ld(bdrow, bd)
        BDL = s_c.mark()
        for tt in range(8):
            key = f"{tt // 2}{'a' if tt % 2 == 0 else 'b'}"
            ps8 = PS[tt // 2][:, (tt % 2) * 512:(tt % 2) * 512 + 512]
            wts = [BDL, HT] + P.ps_take(key)
            P.op("tensor", lambda e, p=ps8, o=oc: e.matmul(
                p, onesr[0:1, 0:128], bdrow[0:1, o * 512:(o + 1) * 512],
                start=True, stop=False), waits=wts)
        for bi, fb in enumerate(FBLK):
            gbi = oc * len(FBLK) + bi
            wdb = wdb4[gbi % 4]
            nf = len(fb)
            wdw = [HT] if gbi < 4 else [wd_free[gbi % 4]]
            P.op("gpsimd", lambda e, o=oc, f0=fb[0], n=nf, wb=wdb: e.dma_start(
                out=wb[:, 0:n * 512].rearrange("p (c f) -> p c f", c=n),
                in_=wd[f0 * 128:(f0 + n) * 128, o * 512:(o + 1) * 512].rearrange(
                    "(c p) f -> p c f", p=128)), waits=wdw, inc=s_w, inc_n=16)
            WDL = s_w.mark()
            for fi, f in enumerate(fb):
                for tt in range(8):
                    ps8 = PS[tt // 2][:, (tt % 2) * 512:(tt % 2) * 512 + 512]
                    last = (f == NFT - 1)
                    blk_last = (fi == nf - 1 and tt == 7)
                    P.op("tensor", lambda e, p=ps8, ff_=f, t=tt, fi_=fi, wb=wdb: e.matmul(
                        p, hT_tile(ff_)[:, t * 128:(t + 1) * 128],
                        wb[:, fi_ * 512: fi_ * 512 + 512],
                        start=False, stop=last),
                        waits=[WDL] if (fi == 0 and tt == 0) else (),
                        inc=s_t if blk_last else None)
            wd_free[gbi % 4] = s_t.mark()
        DN = s_t.mark()
        for tt in range(8):
            key = f"{tt // 2}{'a' if tt % 2 == 0 else 'b'}"
            sl4 = tt
            ps8 = PS[tt // 2][:, (tt % 2) * 512:(tt % 2) * 512 + 512]
            ysl = ystage4[sl4]
            aosl = aostage4[sl4]
            xw_ = [SCOL, AO_ST, HT]
            if sl4 in ystore:
                xw_.append(ystore[sl4])
            P.op("sync", lambda e, t=tt, o=oc, y_=ysl: e.dma_start(
                out=y_, in_=x_own[t * 128:(t + 1) * 128, o * 512:(o + 1) * 512]),
                waits=xw_, inc=s_x, inc_n=16)
            P.op("sync", lambda e, t=tt, o=oc, a_=aosl: e.dma_start(
                out=a_, in_=ao_d[t * 128:(t + 1) * 128, o * 512:(o + 1) * 512]),
                inc=s_x, inc_n=16)
            XT = s_x.mark()
            P.op("vector", lambda e, p=ps8, t=tt, y_=ysl: e.scalar_tensor_tensor(
                y_, p, s_col[:, t:t + 1], y_, ALU.mult, ALU.add),
                waits=[XT, DN, SCOL], inc=s_v)
            P.op("vector", lambda e, y_=ysl, a_=aosl: e.tensor_tensor(
                y_, y_, a_, ALU.add), inc=s_v)
            YV = s_v.mark()
            P.ps_done(key, YV)
            if oc == 0 and tt == 0 and "yraw" in DEBUG_TAPS:
                t_ = tap("yraw", (128, 512))
                P.op("sync", lambda e, t2=t_, y_=ysl: e.dma_start(out=t2, in_=y_),
                     waits=[YV], inc=s_st, inc_n=16)
            P.op("sync", lambda e, t=tt, o=oc, y_=ysl: e.dma_start(
                out=y[t * 128:(t + 1) * 128, o * 512:(o + 1) * 512], in_=y_),
                waits=[YV], inc=s_st, inc_n=16)
            ystore[sl4] = s_st.mark()
    P.op("sync", lambda e: e.nop(), waits=[s_st.mark()])
    P.op("gpsimd", lambda e: e.nop(), waits=[s_st.mark()])

    finals = {s_x: s_x.n, s_x2: s_x2.n, s_c: s_c.n, s_w: s_w.n, s_t: s_t.n,
              s_v: s_v.n, s_s: s_s.n, s_cc: s_cc.n, s_st: s_st.n}
    for eng, lw in P.last_wait.items():
        for sem, v in lw.items():
            assert v <= finals[sem], (eng, v, finals[sem])

    with nc.Block() as block:
        P.emit(block)
    return nc, taps



def run_raw(inputs):
    nc, taps = build_program()
    in_maps = [host_inputs(c, inputs) for c in range(8)]
    res = run_bass_kernel_spmd(nc, in_maps, core_ids=list(range(8)))
    return res


def kernel(**inputs):
    res = run_raw(inputs)
    out = np.zeros((B, S, HID), np.float32)
    for c in range(8):
        b, half = c // 2, c % 2
        out[b, half * TOK:(half + 1) * TOK] = res.results[c]["y"]
    return out


# version 11
# speedup vs baseline: 1.0541x; 1.0280x over previous
import numpy as np
import ml_dtypes
import concourse.bass as bass
import concourse.mybir as mybir
from concourse.bass_utils import run_bass_kernel_spmd

F32 = mybir.dt.float32
BF16 = mybir.dt.bfloat16
AF = mybir.ActivationFunctionType
ALU = mybir.AluOpType
AX = mybir.AxisListType

B, S, HID = 4, 2048, 2048
NH, HD = 16, 128
R = 16
DFF = 5461
NE = 8
KCAP = 1024.0
TOK = 1024
NT_OWN = 8
NT_ALL = 16
TKT = 1152
CW = 132
NFT = 43
EPS_LN = 1e-5
EPS_RMS = 1e-6
NEG = -1e9
SCALE = float(1.0 / np.sqrt(128.0))
BISECT_ITERS = 44
EBIAS = -30.0

DEBUG_TAPS = ()
TAP_GROUP = 0



def _rope_tables(pos):
    inv_freq = 1.0 / 10000.0 ** (np.arange(32, dtype=np.float64) * 2.0 / 64.0)
    ang = pos.astype(np.float64)[:, None] * inv_freq[None, :]
    return np.cos(ang).astype(np.float32), np.sin(ang).astype(np.float32)


def _pad_cols(a, n):
    out = np.zeros((a.shape[0], n), a.dtype)
    out[:, :a.shape[1]] = a
    return out


def host_inputs(core, inp):
    b, half = core // 2, core % 2
    d = {}
    x = np.asarray(inp["L0_attn_hidden_states"], np.float32)
    d["x_own"] = np.ascontiguousarray(x[b, half * TOK:(half + 1) * TOK])
    d["x_oth"] = np.ascontiguousarray(x[b, (1 - half) * TOK:(2 - half) * TOK])

    own_start = half * TOK
    pos_q = own_start + np.arange(TOK)
    cq, sq = _rope_tables(pos_q)
    d["cos_q"], d["sin_q"] = cq, sq
    pos_k = np.concatenate([own_start - 128 + np.arange(128), pos_q])
    ck_, sk_ = _rope_tables(pos_k)
    d["cos_k"], d["sin_k"] = ck_, sk_
    be = np.concatenate([own_start + np.arange(64) * R + (R - 1),
                         (1 - half) * TOK + np.arange(64) * R + (R - 1)])
    cb, sbt_ = _rope_tables(be)
    d["cosT_b"] = np.ascontiguousarray(cb.T)
    d["sinT_b"] = np.ascontiguousarray(sbt_.T)

    cm = np.zeros((TOK, CW), np.float32)
    cm[:, :128] = (be[None, :] <= pos_q[:, None]).astype(np.float32)
    cm[:, 128] = 1.0
    d["cmaskM"] = np.ascontiguousarray(
        cm.reshape(8, 128, CW).transpose(1, 0, 2).reshape(128, 8 * CW)
    ).astype(ml_dtypes.bfloat16)

    i = np.arange(128)[:, None]
    j = np.arange(256)[None, :]
    rest = (((j < 128) & (j >= i)) | ((j >= 128) & (j - 128 <= i))).astype(np.float32)
    first = rest.copy()
    if half == 0:
        first[:, :128] = 0.0
    d["mrestM2"] = np.concatenate([rest, rest], 1).astype(ml_dtypes.bfloat16)
    d["mfirstM2"] = np.concatenate([first, rest], 1).astype(ml_dtypes.bfloat16)

    import jax
    _old_part = jax.config.jax_threefry_partitionable
    jax.config.update("jax_threefry_partitionable", True)
    with jax.default_device(jax.devices("cpu")[0]):
        u = np.asarray(jax.random.uniform(
            jax.random.key(42, impl="threefry2x32"), (B * S, NE),
            dtype=np.float32))
    jax.config.update("jax_threefry_partitionable", _old_part)
    gum = (-np.log(-np.log(u + 1e-10) + 1e-10)).astype(np.float32)
    d["gumbelT"] = np.ascontiguousarray(
        gum[b * S + own_start: b * S + own_start + TOK].T)

    for k, srck in [("Wq", "L0_attn_Wq"), ("Wlk", "L0_attn_Wlk"),
                    ("Wlv", "L0_attn_Wlv"), ("Wo", "L0_attn_Wo"),
                    ("Wk", "L0_attn_Wk"), ("Wv", "L0_attn_Wv"),
                    ("rw", "L0_ffn_router_w")]:
        d[k] = np.asarray(inp[srck], np.float32)
    d["Wc"] = np.ascontiguousarray(
        np.asarray(inp["L0_attn_Wc"], np.float32).reshape(16, 128).T)
    for k, srck in [("bq", "L0_attn_bq"), ("blk", "L0_attn_blk"),
                    ("blv", "L0_attn_blv"), ("bo", "L0_attn_bo"),
                    ("bk", "L0_attn_bk"), ("bv", "L0_attn_bv"),
                    ("bc", "L0_attn_bc"), ("bd", "L0_ffn_b_down")]:
        d[k] = np.asarray(inp[srck], np.float32)[None, :]
    d["qn_row"] = np.asarray(inp["L0_attn_qn_w"], np.float32)[None, :]
    d["kn_row"] = np.asarray(inp["L0_attn_kn_w"], np.float32)[None, :]
    d["kn_col"] = np.asarray(inp["L0_attn_kn_w"], np.float32)[:, None]
    d["sinkTk"] = _pad_cols(np.ascontiguousarray(
        np.asarray(inp["L0_attn_sink_k"], np.float32).T), 20)
    d["sinkv_row"] = np.ascontiguousarray(
        np.asarray(inp["L0_attn_sink_v"], np.float32).reshape(1, NH * HD))
    d["rb_col"] = np.asarray(inp["L0_ffn_router_b"], np.float32)[:, None]
    d["wg"] = _pad_cols(np.asarray(inp["L0_ffn_w_gate"], np.float32), NFT * 128)
    d["wu"] = _pad_cols(np.asarray(inp["L0_ffn_w_up"], np.float32), NFT * 128)
    wdp = np.zeros((NFT * 128, HID), np.float32)
    wdp[:DFF] = np.asarray(inp["L0_ffn_w_down"], np.float32)
    d["wd"] = wdp
    d["bg_col"] = np.ascontiguousarray(_pad_cols(
        np.asarray(inp["L0_ffn_b_gate"], np.float32)[None, :], NFT * 128)
        .reshape(NFT, 128).T)
    d["bu_col"] = np.ascontiguousarray(_pad_cols(
        np.asarray(inp["L0_ffn_b_up"], np.float32)[None, :], NFT * 128)
        .reshape(NFT, 128).T)
    return d



class Sem:
    def __init__(self, nc, name):
        self.h = nc.alloc_semaphore(name)
        self.n = 0

    def mark(self):
        return (self, self.n)


class Prog:
    ENGINES = ("tensor", "vector", "scalar", "gpsimd", "sync")

    def __init__(self, nc):
        self.nc = nc
        self.thunks = {e: [] for e in self.ENGINES}
        self.last_wait = {e: {} for e in self.ENGINES}
        self.ps_reader = {}

    def op(self, eng, f, waits=(), inc=None, inc_n=1):
        wl = []
        lw = self.last_wait[eng]
        for w in waits:
            if w is None:
                continue
            sem, v = w
            if v > 0 and lw.get(sem, -1) < v:
                wl.append((sem.h, v))
                lw[sem] = v
        if inc is not None:
            inc.n += inc_n
        ih = inc.h if inc is not None else None

        def thunk(e, wl=wl, f=f, ih=ih, inc_n=inc_n):
            for h, v in wl:
                e.wait_ge(h, v)
            inst = f(e)
            if ih is not None:
                inst.then_inc(ih, inc_n)
        self.thunks[eng].append(thunk)

    def ps_take(self, *keys):
        return [m for k in keys if (m := self.ps_reader.get(k)) is not None]

    def ps_done(self, key, mark):
        self.ps_reader[key] = mark

    def emit(self, block):
        for eng in self.ENGINES:
            ths = self.thunks[eng]
            getattr(block, eng)(lambda e, ths=ths: [t(e) for t in ths])



def build_program():
    nc = bass.Bass("TRN2", target_bir_lowering=False, debug=False)
    P = Prog(nc)

    def inp(name, shape, dtype=F32):
        return nc.dram_tensor(name, list(shape), dtype, kind="ExternalInput").ap()

    x_own = inp("x_own", (TOK, HID))
    x_oth = inp("x_oth", (TOK, HID))
    cos_q = inp("cos_q", (TOK, 32)); sin_q = inp("sin_q", (TOK, 32))
    cos_k = inp("cos_k", (TKT, 32)); sin_k = inp("sin_k", (TKT, 32))
    cosT_b = inp("cosT_b", (32, 128)); sinT_b = inp("sinT_b", (32, 128))
    cmaskM = inp("cmaskM", (128, 8 * CW), BF16)
    mfirstM2 = inp("mfirstM2", (128, 512), BF16)
    mrestM2 = inp("mrestM2", (128, 512), BF16)
    gumbelT = inp("gumbelT", (NE, TOK))
    Wq = inp("Wq", (HID, HID)); Wlk = inp("Wlk", (HID, HID)); Wlv = inp("Wlv", (HID, HID))
    Wo = inp("Wo", (HID, HID))
    Wk = inp("Wk", (HID, HD)); Wv = inp("Wv", (HID, HD)); Wc = inp("Wc", (128, 16))
    bq = inp("bq", (1, HID)); blk = inp("blk", (1, HID)); blv = inp("blv", (1, HID))
    bo = inp("bo", (1, HID)); bk = inp("bk", (1, HD)); bv = inp("bv", (1, HD))
    bc = inp("bc", (1, 1)); bd = inp("bd", (1, HID))
    qn_row = inp("qn_row", (1, HD)); kn_row = inp("kn_row", (1, HD))
    kn_col = inp("kn_col", (HD, 1))
    sinkTk = inp("sinkTk", (HD, 20)); sinkv_row = inp("sinkv_row", (1, NH * HD))
    rw = inp("rw", (HID, NE)); rb_col = inp("rb_col", (NE, 1))
    wg = inp("wg", (HID, NFT * 128)); wu = inp("wu", (HID, NFT * 128))
    wd = inp("wd", (NFT * 128, HID))
    bg_col = inp("bg_col", (128, NFT)); bu_col = inp("bu_col", (128, NFT))

    y = nc.dram_tensor("y", [TOK, HID], F32, kind="ExternalOutput").ap()

    id_bf = nc.inline_tensor(np.eye(128, dtype=ml_dtypes.bfloat16), "id_bf").ap()
    ones_row_c = nc.inline_tensor(np.ones((1, 512), np.float32), "ones_row").ap()
    ones_col_c = nc.inline_tensor(np.ones((128, 1), np.float32), "ones_col").ap()
    m8np = np.zeros((128, 8), np.float32)
    m8np[np.arange(128), np.arange(128) // 16] = 1.0
    M8c = nc.inline_tensor(m8np, "M8c").ap()
    seg16T_c = nc.inline_tensor(np.ascontiguousarray(m8np.T), "seg16Tc").ap()
    ones8_c = nc.inline_tensor(np.ones((8, 1), np.float32), "ones8c").ap()
    eps_c = nc.inline_tensor(
        np.broadcast_to(np.array([EPS_LN, EPS_RMS, EBIAS], np.float32),
                        (128, 3)).copy(), "eps_c").ap()

    smt_d = nc.dram_tensor("smt_d", [NE, TOK], F32, kind="Internal").ap()
    gath_d = nc.dram_tensor("gath_d", [NE * 8, TOK], F32, kind="Internal").ap()
    ao_d = nc.dram_tensor("ao_d", [TOK, HID], F32, kind="Internal").ap()

    taps = {}

    def tap(name, shape, dtype=F32):
        taps[name] = nc.dram_tensor("tap_" + name, list(shape), dtype,
                                    kind="ExternalOutput").ap()
        return taps[name]

    sbt = nc.alloc_sbuf_tensor
    xnT = sbt("xnT", [128, 16 * TKT], BF16).ap()
    arenaA = sbt("arenaA", [128, 16384], F32).ap()
    arenaB = sbt("arenaB", [128, 8192], F32).ap()
    arenaC = sbt("arenaC", [128, 4096], F32).ap()
    aA = arenaA.bitcast(BF16)
    aB = arenaB.bitcast(BF16)
    aC = arenaC.bitcast(BF16)

    xn = aA
    O_W0 = 0
    O_QTAB = 4096
    O_KTAB = O_QTAB + 1024
    O_QNW = O_KTAB + 1152
    O_KNW = O_QNW + 128
    O_QN2 = O_KNW + 128
    O_SQB = O_QN2 + 512
    O_SSQ = O_SQB + 512
    O_RTMP = O_SSQ + 32
    O_Q4R = O_RTMP + 256
    O_QT4 = O_Q4R + 256
    O_LKT4 = O_QT4 + 2048
    O_LV4 = O_LKT4 + 2304
    O_PL2 = O_LV4 + 2304
    O_PC2 = O_PL2 + 512
    O_PTL = O_PC2 + 272
    O_PTC = O_PTL + 256
    O_PTS = O_PTC + 128
    O_RR = O_PTS + 128
    O_TRASH = O_PL2
    O_WOST = O_QN2
    assert O_RR + 32 <= 16384

    wbuf_bf = aA[:, 2 * O_W0: 2 * O_W0 + 8192]
    qtab = arenaA[:, O_QTAB:O_QTAB + 1024]
    ktab = arenaA[:, O_KTAB:O_KTAB + 1152]
    qnwbc = arenaA[:, O_QNW:O_QNW + 128]
    knwbc = arenaA[:, O_KNW:O_KNW + 128]
    qn2 = arenaA[:, O_QN2:O_QN2 + 512]
    sqb = arenaA[:, O_SQB:O_SQB + 512]
    ssq = arenaA[:, O_SSQ:O_SSQ + 32]
    rtmp = arenaA[:, O_RTMP:O_RTMP + 256]
    q4r = aA[:, 2 * O_Q4R: 2 * O_Q4R + 512]
    qT4 = aA[:, 2 * O_QT4: 2 * O_QT4 + 4096]
    lkT4 = aA[:, 2 * O_LKT4: 2 * O_LKT4 + 4608]
    lv4 = aA[:, 2 * O_LV4: 2 * O_LV4 + 4608]
    pl2 = [aA[:, 2 * O_PL2: 2 * O_PL2 + 512],
           aA[:, 2 * O_PL2 + 512: 2 * O_PL2 + 1024]]
    pc2 = [aA[:, 2 * O_PC2: 2 * O_PC2 + 264],
           aA[:, 2 * O_PC2 + 272: 2 * O_PC2 + 536]]
    pTl = aA[:, 2 * O_PTL: 2 * O_PTL + 512]
    pTc = aA[:, 2 * O_PTC: 2 * O_PTC + 256]
    pTs = aA[:, 2 * O_PTS: 2 * O_PTS + 256]
    rrp = arenaA[:, O_RR:O_RR + 32]
    trashA = arenaA[:, O_TRASH:O_TRASH + 512]
    swb = arenaA[:, O_PL2:O_PL2 + 512]
    rtmp1 = arenaA[:, O_PC2:O_PC2 + 512]
    negq = arenaA[:, O_RR + 32:O_RR + 64]
    negk = arenaA[:, O_RR + 64:O_RR + 96]
    wo_stage = arenaA[:, O_WOST:O_WOST + 1024]

    xnTt = aB[:, 0:2048]
    zr = arenaB[:, 1024:2048]
    uexp = arenaB[:, 2048:3072]
    gumT = arenaB[:, 3072:4096]
    logc = arenaB[:, 4096:6144]
    trashB = arenaB[:, 6144:8192]
    entT = aB[:, 2 * 6144: 2 * 6144 + 2048]
    wblkb = aB[:, 2 * 7296: 2 * 7296 + 128]
    ctxT = aB[:, 0:16384]
    O_SG = 5632
    O_YST = 6656
    sg_sb = aB[:, 2 * O_SG: 2 * O_SG + 2048]
    ystage = arenaB[:, O_YST:O_YST + 1024]
    aostage = arenaB[:, O_SG:O_SG + 1024]

    def hT_tile(f):
        if f < 32:
            return aA[:, f * 1024:(f + 1) * 1024]
        return aB[:, (f - 32) * 1024:(f - 32 + 1) * 1024]

    xst = [arenaC[:, 0:2048], arenaC[:, 2048:4096]]
    wo_sb = aC[:, 0:8192]
    wff = [aC[:, 0:2048], aC[:, 2048:4096], aC[:, 4096:6144], aC[:, 6144:8192]]
    wdb_sl = [aC[:, 0:4096], aC[:, 4096:8192]]

    lnA_t = sbt("lnA", [128, 16], F32).ap()
    eps_sb = sbt("eps_sb", [128, 3], F32).ap()
    nc.const_aps.register(F32, EPS_LN, eps_sb[:, 0:1])
    nc.const_aps.register(F32, EPS_RMS, eps_sb[:, 1:2])
    nc.const_aps.register(F32, EBIAS, eps_sb[:, 2:3])
    cmaskM_sb = sbt("cmaskM_sb", [128, 8 * CW], BF16).ap()
    mfirstM_sb = sbt("mfirstM_sb", [128, 512], BF16).ap()
    mrestM_sb = sbt("mrestM_sb", [128, 512], BF16).ap()
    cosq_sb = sbt("cosq_sb", [128, 8 * 32], F32).ap()
    sinq_sb = sbt("sinq_sb", [128, 8 * 32], F32).ap()
    cosk_sb = sbt("cosk_sb", [128, 9 * 32], F32).ap()
    sink_sb = sbt("sink_sb", [128, 9 * 32], F32).ap()
    cosb_sb = sbt("cosb_sb", [32, 128], F32).ap()
    sinb_sb = sbt("sinb_sb", [32, 128], F32).ap()
    id_sb = sbt("id_sb", [128, 128], BF16).ap()
    onesr = sbt("onesr", [1, 512], F32).ap()
    onesc = sbt("onesc", [128, 1], F32).ap()
    m8_sb = sbt("m8_sb", [128, 8], F32).ap()
    seg16T_sb = sbt("seg16T_sb", [8, 128], F32).ap()
    ones8 = sbt("ones8", [8, 1], F32).ap()
    qnr_sb = sbt("qnr_sb", [1, HD], F32).ap()
    knr_sb = sbt("knr_sb", [1, HD], F32).ap()
    knc_sb = sbt("knc_sb", [HD, 1], F32).ap()
    rbc_sb = sbt("rbc_sb", [NE, 1], F32).ap()
    sinkTk_sb = sbt("sinkTk_sb", [HD, 20], BF16).ap()
    sinkv_sb = sbt("sinkv_sb", [1, NH * HD], BF16).ap()
    bgc_sb = sbt("bgc_sb", [128, NFT], F32).ap()
    buc_sb = sbt("buc_sb", [128, NFT], F32).ap()
    brow = sbt("brow", [1, 1536], F32).ap()
    borow = sbt("borow", [1, 2048], F32).ap()
    bdrow = sbt("bdrow", [1, 2048], F32).ap()
    smt_sb = sbt("smt_sb", [NE, TOK], F32).ap()
    bis_sb = sbt("bis_sb", [128, 512], F32).ap()
    bst = sbt("bst", [8, 16], F32).ap()
    mid128 = sbt("mid128", [128, 2], F32).ap()
    cnt128 = sbt("cnt128", [128, 1], F32).ap()
    sT_row = sbt("sT_row", [NE, TOK], F32).ap()
    s_col = sbt("s_col", [128, 8], F32).ap()
    ckT_sb = sbt("ckT_sb", [128, 128], BF16).ap()
    cv_sb = sbt("cv_sb", [128, 128], BF16).ap()
    ckn_sb = sbt("ckn_sb", [128, 128], F32).ap()
    tmpR = sbt("tmpR", [32, 512], F32).ap()
    tmpo = sbt("tmpo", [32, 128], BF16).ap()
    expw_sb = sbt("expw_sb", [128, 16], F32).ap()
    sumw_sb = sbt("sumw_sb", [1, 128], F32).ap()
    rsumw_sb = sbt("rsumw_sb", [1, 128], F32).ap()
    rsb_sb = sbt("rsb_sb", [128, 128], F32).ap()
    invb_sb = sbt("invb_sb", [128, 128], F32).ap()
    ckr_sb = sbt("ckr_sb", [1, 128], F32).ap()
    wc_sb = sbt("wc_sb", [128, 16], BF16).ap()
    rw_sb = sbt("rw_sb", [128, 16 * NE], BF16).ap()
    qtab2 = arenaB[:, 6144:7168]
    ktab2 = arenaB[:, 7168:8192]
    ktab2t8 = arenaA[:, 16144:16272]

    def kt2tile(t):
        return ktab2[:, t * 128:(t + 1) * 128] if t < 8 else ktab2t8
    wk_sb = aC[:, 0:2048]
    wv_sb = aC[:, 2048:4096]

    s_x = Sem(nc, "s_x")
    s_x2 = Sem(nc, "s_x2")
    s_c = Sem(nc, "s_c")
    s_w = Sem(nc, "s_w")
    s_t = Sem(nc, "s_t")
    s_v = Sem(nc, "s_v")
    s_s = Sem(nc, "s_s")
    s_cc = Sem(nc, "s_cc")
    s_st = Sem(nc, "s_st")

    PS = [nc.alloc_psum_tensor(f"ps{i}", [128, 1024], F32).ap() for i in range(4)]
    PSbf = [p.bitcast(BF16) for p in PS]

    def ld(dst, src, waits=()):
        P.op("sync", lambda e, d=dst, s=src: e.dma_start(out=d, in_=s),
             waits=waits, inc=s_c, inc_n=16)

    def wld(dst, src, waits=()):
        P.op("gpsimd", lambda e, d=dst, s=src: e.dma_start(out=d, in_=s),
             waits=waits, inc=s_w, inc_n=16)

    def ld3(dst2d, src, n, q="sync", waits=()):
        s3 = src.rearrange("(c p) f -> p c f", p=128)
        d3 = dst2d.rearrange("p (c f) -> p c f", c=n)
        if q == "sync":
            P.op("sync", lambda e, d=d3, s_=s3: e.dma_start(out=d, in_=s_),
                 waits=waits, inc=s_c, inc_n=16)
        else:
            P.op("gpsimd", lambda e, d=d3, s_=s3: e.dma_start(out=d, in_=s_),
                 waits=waits, inc=s_w, inc_n=16)

    ld(id_sb, id_bf); ld(onesr, ones_row_c); ld(onesc, ones_col_c)
    ld(m8_sb, M8c); ld(seg16T_sb, seg16T_c); ld(ones8, ones8_c)
    ld(mfirstM_sb, mfirstM2); ld(mrestM_sb, mrestM2)
    ld(cmaskM_sb, cmaskM)
    ld3(cosq_sb, cos_q, 8)
    ld3(sinq_sb, sin_q, 8)
    ld3(cosk_sb, cos_k, 9)
    ld3(sink_sb, sin_k, 9)
    ld(cosb_sb, cosT_b); ld(sinb_sb, sinT_b)
    ld(qnr_sb, qn_row); ld(knr_sb, kn_row); ld(knc_sb, kn_col)
    ld(rbc_sb, rb_col); ld(bgc_sb, bg_col); ld(buc_sb, bu_col)
    ld(eps_sb, eps_c)
    ld(gumT[0:8, :], gumbelT)
    ld(brow[0:1, 1024:1152], bk)
    ld(brow[0:1, 1152:1280], bv)
    ld(brow[0:1, 1280:1281], bc)
    CONSTS = s_c.mark()


    wld(sinkTk_sb, sinkTk)
    wld(sinkv_sb, sinkv_row)
    wld(wc_sb, Wc)
    ld3(rw_sb, rw, 16, q="g")
    W_PRE = s_w.mark()

    tile_done = []
    xnT_marks = []
    for tt in range(NT_ALL):
        buf = xst[tt % 2]
        src = x_own if tt < NT_OWN else x_oth
        r0 = (tt % NT_OWN) * 128
        w = [] if tt < 2 else [tile_done[tt - 2]]
        sx_p = s_x if tt % 2 == 0 else s_x2
        P.op("sync", lambda e, bu_=buf, s=src, r=r0: e.dma_start(
            out=bu_, in_=s[r:r + 128, :]), waits=w, inc=sx_p, inc_n=16)
        xw = sx_p.mark()
        lnA = lnA_t[:, 0:8]
        P.op("vector", lambda e, bu_=buf, L=lnA: e.reduce_sum(L[:, 0:1], bu_, axis=AX.X),
             waits=[xw], inc=s_v)
        VS = s_v.mark()
        P.op("scalar", lambda e, bu_=buf: e.activation(
            trashB, bu_, AF.Square, accum_out=lnA[:, 1:2]), waits=[xw], inc=s_s)
        SQm = s_s.mark()
        P.op("vector", lambda e: e.tensor_scalar(
            lnA[:, 2:3], lnA[:, 0:1], 1.0 / HID, None, ALU.mult), waits=[VS], inc=s_v)
        P.op("vector", lambda e: e.tensor_scalar(
            lnA[:, 3:4], lnA[:, 1:2], 1.0 / HID, None, ALU.mult), waits=[SQm], inc=s_v)
        vB = s_v.mark()
        P.op("vector", lambda e: e.scalar_tensor_tensor(
            lnA[:, 4:5], lnA[:, 2:3], lnA[:, 2:3], lnA[:, 3:4], ALU.mult, ALU.subtract),
            waits=[vB], inc=s_v)
        vC = s_v.mark()
        P.op("scalar", lambda e: e.activation(
            lnA[:, 5:6], lnA[:, 4:5], AF.Ln, bias=EPS_LN, scale=-1.0),
            waits=[vC, CONSTS], inc=s_s)
        sA = s_s.mark()
        P.op("scalar", lambda e: e.activation(
            lnA[:, 6:7], lnA[:, 5:6], AF.Exp, scale=-0.5), waits=[sA], inc=s_s)
        sB = s_s.mark()
        P.op("vector", lambda e: e.tensor_scalar(
            lnA[:, 7:8], lnA[:, 2:3], lnA[:, 6:7], -1.0, ALU.mult, ALU.mult),
            waits=[sB], inc=s_v)
        vD = s_v.mark()
        P.op("scalar", lambda e, bu_=buf, t=tt: e.activation(
            xn[:, t * 2048:(t + 1) * 2048], bu_, AF.Identity,
            bias=lnA[:, 7:8], scale=lnA[:, 6:7]), waits=[vD], inc=s_s)
        sAPP = s_s.mark()
        tile_done.append(sAPP)

        for grp in range(4):
            hf = grp % 2
            key = "3a" if hf == 0 else "3b"
            tw = [sAPP, CONSTS] + P.ps_take(key)
            for k in range(4):
                hc = grp * 4 + k
                P.op("tensor", lambda e, t=tt, h=hc, k_=k, hf_=hf: e.transpose(
                    PSbf[3][:, hf_ * 1024 + k_ * 128: hf_ * 1024 + (k_ + 1) * 128],
                    xn[:, t * 2048 + h * 128: t * 2048 + (h + 1) * 128], id_sb),
                    waits=tw if k == 0 else (), inc=s_t if k == 3 else None)
            tE = s_t.mark()
            if tt < NT_OWN:
                dst = xnT.rearrange("p (h t) -> p h t", h=16)[
                    :, grp * 4:(grp + 1) * 4, tt * 128:(tt + 1) * 128]
            else:
                dst = xnTt.rearrange("p (h t) -> p h t", h=16)[
                    :, grp * 4:(grp + 1) * 4, 0:128]
            P.op("vector", lambda e, d=dst, hf_=hf: e.tensor_copy(
                d, PSbf[3][:, hf_ * 1024:hf_ * 1024 + 512].rearrange(
                    "p (h t) -> p h t", h=4)), waits=[tE], inc=s_v)
            P.ps_done(key, s_v.mark())
            if tt == NT_ALL - 1:
                P.op("vector", lambda e, g=grp, hf_=hf: e.tensor_copy(
                    xnT.rearrange("p (h t) -> p h t", h=16)[
                        :, g * 4:(g + 1) * 4, 1024:1152],
                    PSbf[3][:, hf_ * 1024:hf_ * 1024 + 512].rearrange(
                        "p (h t) -> p h t", h=4)), inc=s_v)
                P.ps_done(key, s_v.mark())
        TILE_T = s_v.mark()
        xnT_marks.append(TILE_T)

        if tt < NT_OWN:
            def src_xnT(h, t=tt):
                return xnT[:, h * TKT + t * 128: h * TKT + (t + 1) * 128]
        else:
            def src_xnT(h):
                return xnTt[:, h * 128:(h + 1) * 128]
        lw = [TILE_T, W_PRE, CONSTS] + P.ps_take("2a")
        for hc in range(16):
            P.op("tensor", lambda e, h=hc, fx=src_xnT: e.matmul(
                PS[2][0:1, 0:128], wc_sb[:, h:h + 1], fx(h),
                start=(h == 0), stop=(h == 15)),
                waits=lw if hc == 0 else (), inc=s_t if hc == 15 else None)
        LCM = s_t.mark()
        P.op("scalar", lambda e, t=tt: e.activation(
            logc[0:1, t * 128:(t + 1) * 128], PS[2][0:1, 0:128], AF.Exp,
            bias=brow[0:1, 1280:1281]), waits=[LCM], inc=s_s)
        P.ps_done("2a", s_s.mark())
    XNT_OWN = xnT_marks[NT_OWN - 1]
    XNT_ALL = xnT_marks[NT_ALL - 1]
    LOGC = s_s.mark()
    XN_LAST = tile_done[-1]

    if "xnT" in DEBUG_TAPS:
        t_ = tap("xnT", (128, 16 * TKT), BF16)
        P.op("sync", lambda e, t2=t_: e.dma_start(out=t2, in_=xnT),
             waits=[XNT_ALL, LOGC], inc=s_st, inc_n=16)
    if "xn8" in DEBUG_TAPS:
        t_ = tap("xn8", (128, 16384), BF16)
        P.op("sync", lambda e, t2=t_: e.dma_start(out=t2, in_=aA[:, 0:16384]),
             waits=[XNT_ALL, LOGC], inc=s_st, inc_n=16)
        _m = s_st.mark()
        P.op("gpsimd", lambda e: e.nop(), waits=[_m])
    if "logc" in DEBUG_TAPS:
        t_ = tap("logc", (1, 2048))
        P.op("sync", lambda e, t2=t_: e.dma_start(out=t2, in_=logc[0:1, :]),
             waits=[LOGC], inc=s_st, inc_n=16)

    mmw = [XNT_OWN, W_PRE] + P.ps_take("0a", "0b")
    for n2 in range(2):
        for hc in range(16):
            P.op("tensor", lambda e, h=hc, n=n2: e.matmul(
                PS[0][0:8, n * 512:(n + 1) * 512], rw_sb[:, h * 8:(h + 1) * 8],
                xnT[:, h * TKT + n * 512: h * TKT + (n + 1) * 512],
                start=(h == 0), stop=(h == 15)),
                waits=mmw if (n2 == 0 and hc == 0) else (),
                inc=s_t if (n2 == 1 and hc == 15) else None)
    RT = s_t.mark()
    P.op("vector", lambda e: e.scalar_tensor_tensor(
        uexp[0:8, :], PS[0][0:8, :], rbc_sb, gumT[0:8, :], ALU.add, ALU.add),
        waits=[RT, CONSTS], inc=s_v)
    P.ps_done("0a", s_v.mark()); P.ps_done("0b", s_v.mark())
    RV = s_v.mark()
    P.op("scalar", lambda e: e.activation(uexp[0:8, :], uexp[0:8, :], AF.Exp),
         waits=[RV], inc=s_s)
    RE = s_s.mark()
    zw = [RE] + P.ps_take("1a", "1b")
    for n2 in range(2):
        P.op("tensor", lambda e, n=n2: e.matmul(
            PS[1][0:1, n * 512:(n + 1) * 512], ones8,
            uexp[0:8, n * 512:(n + 1) * 512], start=True, stop=True),
            waits=zw if n2 == 0 else (), inc=s_t if n2 == 1 else None)
    RZ = s_t.mark()
    P.op("vector", lambda e: e.reciprocal(zr[0:1, :], PS[1][0:1, :]),
         waits=[RZ], inc=s_v)
    P.ps_done("1a", s_v.mark()); P.ps_done("1b", s_v.mark())
    RR_ = s_v.mark()
    bw = [RR_] + P.ps_take("2a", "2b")
    for n2 in range(2):
        P.op("tensor", lambda e, n=n2: e.matmul(
            PS[2][0:8, n * 512:(n + 1) * 512], onesr[0:1, 0:8],
            zr[0:1, n * 512:(n + 1) * 512], start=True, stop=True),
            waits=bw if n2 == 0 else (), inc=s_t if n2 == 1 else None)
    RB2 = s_t.mark()
    P.op("vector", lambda e: e.tensor_tensor(
        smt_sb[0:8, :], uexp[0:8, :], PS[2][0:8, :], ALU.mult), waits=[RB2], inc=s_v)
    P.ps_done("2a", s_v.mark()); P.ps_done("2b", s_v.mark())
    SMT = s_v.mark()
    if "smt" in DEBUG_TAPS:
        t_ = tap("smt", (NE, TOK))
        P.op("sync", lambda e, t2=t_: e.dma_start(out=t2, in_=smt_sb[0:8, :]),
             waits=[SMT], inc=s_st, inc_n=16)

    ld3(wk_sb, Wk, 16, q="g", waits=[XN_LAST])
    ld3(wv_sb, Wv, 16, q="g", waits=[XN_LAST])
    WKV = s_w.mark()
    P.op("vector", lambda e: e.reduce_sum(
        sumw_sb.rearrange("p (a b) -> p a b", a=128),
        logc[0:1, :].rearrange("p (a b) -> p a b", a=128), axis=AX.X),
        waits=[LOGC], inc=s_v)
    SW = s_v.mark()
    P.op("vector", lambda e: e.reciprocal(rsumw_sb, sumw_sb), waits=[SW], inc=s_v)
    RSW = s_v.mark()
    P.op("tensor", lambda e: e.matmul(PS[2][:, 0:128], onesr[0:1, 0:128],
                                      rsumw_sb, start=True, stop=True),
         waits=[RSW] + P.ps_take("2a"), inc=s_t)
    P.op("vector", lambda e: e.tensor_copy(rsb_sb, PS[2][:, 0:128]),
         waits=[s_t.mark()], inc=s_v)
    P.ps_done("2a", s_v.mark())
    RSB = s_v.mark()
    exw = [LOGC] + P.ps_take("2b")
    for ttc in range(16):
        P.op("tensor", lambda e, t=ttc: e.matmul(
            PS[2][:, 512 + t:513 + t], logc[0:1, t * 128:(t + 1) * 128],
            onesr[0:1, 0:1], start=True, stop=True),
            waits=exw if ttc == 0 else (), inc=s_t if ttc == 15 else None)
    EXC = s_t.mark()
    P.op("vector", lambda e: e.tensor_copy(expw_sb, PS[2][:, 512:528]),
         waits=[EXC], inc=s_v)
    P.ps_done("2b", s_v.mark())
    EXS = s_v.mark()
    for ttc in range(16):
        P.op("vector", lambda e, t=ttc: e.tensor_scalar(
            wblkb[:, t * 8:(t + 1) * 8], m8_sb, expw_sb[:, t:t + 1], None, ALU.mult),
            waits=[EXS, CONSTS] if ttc == 0 else (), inc=s_v if ttc == 15 else None)
    WBLK = s_v.mark()
    enw = [WBLK, XN_LAST] + P.ps_take("0a", "0b", "1a", "1b")
    for ht in range(16):
        ps = PS[0] if ht < 8 else PS[1]
        col = (ht % 8) * 128
        for ttc in range(16):
            P.op("tensor", lambda e, t=ttc, h=ht, p=ps, c=col: e.matmul(
                p[:, c + t * 8: c + (t + 1) * 8],
                xn[:, t * 2048 + h * 128: t * 2048 + (h + 1) * 128],
                wblkb[:, t * 8:(t + 1) * 8], start=True, stop=True),
                waits=enw if (ht == 0 and ttc == 0) else (),
                inc=s_t if (ht == 15 and ttc == 15) else None)
    ENTM = s_t.mark()
    for ht in range(16):
        ps = PS[0] if ht < 8 else PS[1]
        col = (ht % 8) * 128
        P.op("vector", lambda e, h=ht, p=ps, c=col: e.tensor_tensor(
            entT[:, h * 128:(h + 1) * 128], p[:, c:c + 128], rsb_sb, ALU.mult),
            waits=[ENTM, RSB] if ht == 0 else (), inc=s_v)
    ENT = s_v.mark()
    for k in ("0a", "0b", "1a", "1b"):
        P.ps_done(k, ENT)
    if "entT" in DEBUG_TAPS:
        t_ = tap("entT", (128, 2048), BF16)
        P.op("sync", lambda e, t2=t_: e.dma_start(out=t2, in_=entT), waits=[ENT],
             inc=s_st, inc_n=16)
    ckw = [ENT, CONSTS, WKV] + P.ps_take("2a")
    P.op("tensor", lambda e: e.matmul(PS[2][:, 0:128], brow[0:1, 1024:1152],
                                      onesr[0:1, 0:128], start=True, stop=False),
         waits=ckw)
    for hc in range(16):
        P.op("tensor", lambda e, h=hc: e.matmul(
            PS[2][:, 0:128], wk_sb[:, h * 128:(h + 1) * 128],
            entT[:, h * 128:(h + 1) * 128], start=False, stop=(h == 15)),
            inc=s_t if hc == 15 else None)
    CKM = s_t.mark()
    P.op("scalar", lambda e: e.activation(ckn_sb, PS[2][:, 0:128], AF.Square),
         waits=[CKM], inc=s_s)
    CKSQ = s_s.mark()
    P.op("tensor", lambda e: e.matmul(PS[3][0:1, 0:128], onesc, ckn_sb,
                                      start=True, stop=True),
         waits=[CKSQ] + P.ps_take("3a"), inc=s_t)
    CKSS = s_t.mark()
    P.op("scalar", lambda e: e.activation(ckr_sb, PS[3][0:1, 0:128], AF.Ln,
                                          bias=EPS_RMS, scale=1.0 / 128),
         waits=[CKSS], inc=s_s)
    CKLN = s_s.mark()
    P.ps_done("3a", CKLN)
    P.op("scalar", lambda e: e.activation(ckr_sb, ckr_sb, AF.Exp, scale=-0.5),
         waits=[CKLN], inc=s_s)
    CKIV = s_s.mark()
    P.op("tensor", lambda e: e.matmul(PS[3][:, 128:256], onesr[0:1, 0:128],
                                      ckr_sb, start=True, stop=True),
         waits=[CKIV] + P.ps_take("3a"), inc=s_t)
    P.op("vector", lambda e: e.tensor_copy(invb_sb, PS[3][:, 128:256]),
         waits=[s_t.mark()], inc=s_v)
    P.ps_done("3a", s_v.mark())
    CKBC = s_v.mark()
    P.op("vector", lambda e: e.scalar_tensor_tensor(
        ckn_sb, PS[2][:, 0:128], knc_sb, invb_sb, ALU.mult, ALU.mult),
        waits=[CKBC], inc=s_v)
    P.ps_done("2a", s_v.mark())
    CKN = s_v.mark()
    P.op("vector", lambda e: e.tensor_copy(tmpR[:, 0:128], ckn_sb[32:64, :]),
         waits=[CKN], inc=s_v)
    C2 = s_v.mark()
    P.op("vector", lambda e: e.tensor_tensor(tmpR[:, 128:256], ckn_sb[0:32, :],
                                             cosb_sb, ALU.mult), waits=[CKN], inc=s_v)
    P.op("vector", lambda e: e.tensor_tensor(tmpR[:, 256:384], tmpR[:, 0:128],
                                             sinb_sb, ALU.mult), waits=[C2], inc=s_v)
    CKR1 = s_v.mark()
    P.op("vector", lambda e: e.tensor_tensor(
        ckT_sb[0:32, :], tmpR[:, 128:256], tmpR[:, 256:384], ALU.subtract),
        waits=[CKR1], inc=s_v)
    SUB1 = s_v.mark()
    P.op("vector", lambda e: e.tensor_tensor(tmpR[:, 128:256], ckn_sb[0:32, :],
                                             sinb_sb, ALU.mult), waits=[SUB1], inc=s_v)
    P.op("vector", lambda e: e.tensor_tensor(tmpR[:, 256:384], tmpR[:, 0:128],
                                             cosb_sb, ALU.mult), waits=[SUB1], inc=s_v)
    CKR2 = s_v.mark()
    P.op("vector", lambda e: e.tensor_tensor(
        tmpo, tmpR[:, 128:256], tmpR[:, 256:384], ALU.add), waits=[CKR2], inc=s_v)
    CKO2 = s_v.mark()
    P.op("vector", lambda e: e.tensor_copy(ckT_sb[32:64, :], tmpo),
         waits=[CKO2], inc=s_v)
    P.op("vector", lambda e: e.tensor_copy(ckT_sb[64:128, :], ckn_sb[64:128, :]),
         waits=[CKN], inc=s_v)
    CKT = s_v.mark()
    cvw = [ENT, WKV] + P.ps_take("2b")
    P.op("tensor", lambda e: e.matmul(PS[2][:, 512:640], onesr[0:1, 0:128],
                                      brow[0:1, 1152:1280], start=True, stop=False),
         waits=cvw)
    for hc in range(16):
        P.op("tensor", lambda e, h=hc: e.matmul(
            PS[2][:, 512:640], entT[:, h * 128:(h + 1) * 128],
            wv_sb[:, h * 128:(h + 1) * 128], start=False, stop=(h == 15)),
            inc=s_t if hc == 15 else None)
    CVM = s_t.mark()
    P.op("scalar", lambda e: e.activation(cv_sb, PS[2][:, 512:640], AF.Copy),
         waits=[CVM], inc=s_s)
    P.ps_done("2b", s_s.mark())
    CV = s_s.mark()
    if "ckT" in DEBUG_TAPS:
        t_ = tap("ckT", (128, 128), BF16)
        P.op("sync", lambda e, t2=t_: e.dma_start(out=t2, in_=ckT_sb), waits=[CKT],
             inc=s_st, inc_n=16)
    if "cv" in DEBUG_TAPS:
        t_ = tap("cv", (128, 128), BF16)
        P.op("sync", lambda e, t2=t_: e.dma_start(out=t2, in_=cv_sb), waits=[CV],
             inc=s_st, inc_n=16)

    qbw = [ENTM, CONSTS] + P.ps_take("3b")
    P.op("tensor", lambda e: e.matmul(PS[3][:, 512:640], onesr[0:1, 0:128], qnr_sb,
                                      start=True, stop=True), waits=qbw)
    P.op("tensor", lambda e: e.matmul(PS[3][:, 640:768], onesr[0:1, 0:128], knr_sb,
                                      start=True, stop=True), inc=s_t)
    QNB = s_t.mark()
    P.op("vector", lambda e: e.tensor_copy(qnwbc, PS[3][:, 512:640]), waits=[QNB],
         inc=s_v)
    P.op("vector", lambda e: e.tensor_copy(knwbc, PS[3][:, 640:768]), inc=s_v)
    P.ps_done("3b", s_v.mark())
    QNW = s_v.mark()
    P.op("vector", lambda e: e.memset(qtab2, 0.0),
         waits=[QNW, CONSTS, CKT, CV], inc=s_v)
    P.op("vector", lambda e: e.memset(ktab2, 0.0), inc=s_v)
    P.op("vector", lambda e: e.memset(ktab2t8, 0.0), inc=s_v)
    P.op("vector", lambda e: e.tensor_scalar(
        negq, qnwbc[:, 32:64], -1.0, None, ALU.mult), inc=s_v)
    P.op("vector", lambda e: e.tensor_scalar(
        negk, knwbc[:, 32:64], -1.0, None, ALU.mult), inc=s_v)
    Z0 = s_v.mark()
    for t in range(8):
        P.op("vector", lambda e, t_=t: e.tensor_tensor(
            qtab[:, t_ * 128:t_ * 128 + 32], cosq_sb[:, t_ * 32:(t_ + 1) * 32],
            qnwbc[:, 0:32], ALU.mult), waits=[Z0] if t == 0 else ())
        P.op("vector", lambda e, t_=t: e.tensor_tensor(
            qtab[:, t_ * 128 + 32:t_ * 128 + 64],
            cosq_sb[:, t_ * 32:(t_ + 1) * 32], qnwbc[:, 32:64], ALU.mult))
        P.op("vector", lambda e, t_=t: e.tensor_copy(
            qtab[:, t_ * 128 + 64:t_ * 128 + 128], qnwbc[:, 64:128]))
        P.op("vector", lambda e, t_=t: e.tensor_tensor(
            qtab2[:, t_ * 128:t_ * 128 + 32], sinq_sb[:, t_ * 32:(t_ + 1) * 32],
            negq, ALU.mult))
        P.op("vector", lambda e, t_=t: e.tensor_tensor(
            qtab2[:, t_ * 128 + 32:t_ * 128 + 64],
            sinq_sb[:, t_ * 32:(t_ + 1) * 32], qnwbc[:, 0:32], ALU.mult))
    for t in range(9):
        P.op("vector", lambda e, t_=t: e.tensor_tensor(
            ktab[:, t_ * 128:t_ * 128 + 32], cosk_sb[:, t_ * 32:(t_ + 1) * 32],
            knwbc[:, 0:32], ALU.mult))
        P.op("vector", lambda e, t_=t: e.tensor_tensor(
            ktab[:, t_ * 128 + 32:t_ * 128 + 64],
            cosk_sb[:, t_ * 32:(t_ + 1) * 32], knwbc[:, 32:64], ALU.mult))
        P.op("vector", lambda e, t_=t: e.tensor_copy(
            ktab[:, t_ * 128 + 64:t_ * 128 + 128], knwbc[:, 64:128]))
        P.op("vector", lambda e, t_=t: e.tensor_tensor(
            kt2tile(t_)[:, 0:32], sink_sb[:, t_ * 32:(t_ + 1) * 32],
            negk, ALU.mult))
        P.op("vector", lambda e, t_=t: e.tensor_tensor(
            kt2tile(t_)[:, 32:64],
            sink_sb[:, t_ * 32:(t_ + 1) * 32], knwbc[:, 0:32], ALU.mult),
            inc=s_v if t == 8 else None)
    TABS = s_v.mark()

    def kt_off(kt):
        return 1024 if kt == 0 else (kt - 1) * 128

    wbufs = [wbuf_bf, aC[:, 0:8192]]
    wbuf_free = {}
    bias_free = {}
    ATT0 = [TABS, CV, CKT, W_PRE]
    for g in range(4):
        WLDs = {}
        BLDs = {}
        for wi, (Wsrc, bsrc) in enumerate([(Wq, bq), (Wlk, blk)]):
            wslot = (wi + g) % 2
            wbuf = wbufs[wslot]
            wlw = list(ATT0) if (g == 0) else []
            if wslot in wbuf_free:
                wlw.append(wbuf_free[wslot])
            P.op("gpsimd", lambda e, W=Wsrc, g_=g, wb=wbuf: e.dma_start(
                out=wb.rearrange("p (c f) -> p c f", c=16),
                in_=W[:, g_ * 512:(g_ + 1) * 512].rearrange(
                    "(c p) f -> p c f", p=128)), waits=wlw, inc=s_w, inc_n=16)
            WLDs[wi] = s_w.mark()
            bld_w = [bias_free[wslot]] if wslot in bias_free else []
            ld(brow[0:1, wslot * 512:wslot * 512 + 512],
               bsrc[0:1, g * 512:(g + 1) * 512], waits=bld_w)
            BLDs[wi] = s_c.mark()

        jobs = [(0, t) for t in range(8)] + [(1, t) for t in range(9)]
        NJ = len(jobs)
        MM = {}; SQm = {}; SS = {}; INV = {}; APP = {}
        ROPE = {}; TR = {}
        for i in range(NJ + 2):
            if i < NJ:
                wi, t = jobs[i]
                xt0 = t * 128 if wi == 0 else kt_off(t)
                key = "0a" if i % 2 == 0 else "1a"
                ps = PS[0] if i % 2 == 0 else PS[1]
                wbuf = wbufs[(wi + g) % 2]
                br = brow[0:1, ((wi + g) % 2) * 512:((wi + g) % 2) * 512 + 512]
                mmw = [WLDs[wi], BLDs[wi]] + P.ps_take(key)
                P.op("tensor", lambda e, p=ps, b2=br: e.matmul(
                    p[:, 0:512], onesr[0:1, 0:128], b2,
                    start=True, stop=False), waits=mmw)
                for hc in range(16):
                    P.op("tensor", lambda e, h=hc, x0=xt0, p=ps, wb=wbuf:
                         e.matmul(p[:, 0:512],
                                  xnT[:, h * TKT + x0: h * TKT + x0 + 128],
                                  wb[:, h * 512:(h + 1) * 512],
                                  start=False, stop=(h == 15)),
                         inc=s_t if hc == 15 else None)
                MM[i] = s_t.mark()
                if i == 7:
                    wbuf_free[g % 2] = MM[i]
                    bias_free[g % 2] = MM[i]
                if i == NJ - 1:
                    wbuf_free[1 - g % 2] = MM[i]
                    bias_free[1 - g % 2] = MM[i]
            if i >= 2 and i - 2 < NJ:
                j = i - 2
                trw = [ROPE[j]] + P.ps_take("2t")
                for h4 in range(4):
                    P.op("tensor", lambda e, h=h4: e.transpose(
                        PSbf[2][:, h * 128:(h + 1) * 128],
                        q4r[:, h * 128:(h + 1) * 128], id_sb),
                        waits=trw if h4 == 0 else (),
                        inc=s_t if h4 == 3 else None)
                TR[j] = s_t.mark()
            if i < NJ:
                ps = PS[0] if i % 2 == 0 else PS[1]
                sqw = [MM[i]]
                if i >= 1:
                    sqw.append(SS[i - 1])
                P.op("scalar", lambda e, p=ps: e.activation(
                    sqb, p[:, 0:512], AF.Square), waits=sqw, inc=s_s)
                SQm[i] = s_s.mark()
            if i >= 1 and i - 1 < NJ:
                j = i - 1
                sc0 = (j % 2) * 16
                P.op("scalar", lambda e, c=sc0: e.activation(
                    ssq[:, c + 4:c + 8], ssq[:, c + 0:c + 4], AF.Ln,
                    bias=EPS_RMS, scale=1.0 / 128), waits=[SS[j]], inc=s_s)
                P.op("scalar", lambda e, c=sc0: e.activation(
                    ssq[:, c + 8:c + 12], ssq[:, c + 4:c + 8], AF.Exp,
                    scale=-0.5), inc=s_s)
                INV[j] = s_s.mark()
                psj = PS[0] if j % 2 == 0 else PS[1]
                apw = [INV[j]]
                if j >= 1:
                    apw.append(ROPE[j - 1])
                for h4 in range(4):
                    P.op("scalar", lambda e, h=h4, p=psj, c=sc0: e.activation(
                        qn2[:, h * 128:(h + 1) * 128],
                        p[:, h * 128:(h + 1) * 128],
                        AF.Copy, scale=ssq[:, c + 8 + h:c + 9 + h]),
                        waits=apw if h4 == 0 else (),
                        inc=s_s if h4 == 3 else None)
                APP[j] = s_s.mark()
                P.ps_done("0a" if j % 2 == 0 else "1a", APP[j])
            if i < NJ:
                sc0 = (i % 2) * 16
                P.op("vector", lambda e, c=sc0: e.reduce_sum(
                    ssq[:, c:c + 4].rearrange("p (a b) -> p a b", a=4),
                    sqb.rearrange("p (a b) -> p a b", a=4), axis=AX.X),
                    waits=[SQm[i]], inc=s_v)
                SS[i] = s_v.mark()
            if i >= 1 and i - 1 < NJ:
                j = i - 1
                wiJ, tJ = jobs[j]
                tabsel = qtab if wiJ == 0 else ktab
                t2tile = (qtab2[:, tJ * 128:(tJ + 1) * 128] if wiJ == 0
                          else kt2tile(tJ))
                tab0 = tJ * 128
                rpw = [APP[j]]
                if j >= 1:
                    rpw.append(TR[j - 1])
                qn3 = qn2.rearrange("p (h s q) -> p h s q", h=4, s=4)
                sw3 = swb.rearrange("p (h s q) -> p h s q", h=4, s=4)
                qn2h = qn2.rearrange("p (h s q) -> p h s q", h=4, s=2)
                sw2h = swb.rearrange("p (h s q) -> p h s q", h=4, s=2)
                P.op("vector", lambda e, a=sw3, b=qn3: e.tensor_copy(
                    a[:, :, 0, :], b[:, :, 1, :]), waits=rpw, inc=s_v)
                P.op("vector", lambda e, a=sw3, b=qn3: e.tensor_copy(
                    a[:, :, 1, :], b[:, :, 0, :]), inc=s_v)
                P.op("vector", lambda e, a=sw2h, b=qn2h: e.tensor_copy(
                    a[:, :, 1, :], b[:, :, 1, :]), inc=s_v)
                CPm = s_v.mark()
                for h4 in range(4):
                    c0 = h4 * 128
                    P.op("vector", lambda e, c=c0, t0=tab0, tb=tabsel:
                         e.tensor_tensor(rtmp1[:, c:c + 128], qn2[:, c:c + 128],
                                         tb[:, t0:t0 + 128], ALU.mult),
                         inc=s_v)
                for h4 in range(4):
                    c0 = h4 * 128
                    P.op("vector", lambda e, c=c0, tb=t2tile:
                         e.tensor_tensor(swb[:, c:c + 128], swb[:, c:c + 128],
                                         tb, ALU.mult),
                         waits=[CPm] if h4 == 0 else (), inc=s_v)
                Mm = s_v.mark()
                for h4 in range(4):
                    c0 = h4 * 128
                    P.op("vector", lambda e, c=c0: e.tensor_tensor(
                        q4r[:, c:c + 128], rtmp1[:, c:c + 128],
                        swb[:, c:c + 128], ALU.add),
                        waits=[Mm] if h4 == 0 else (), inc=s_v)
                ROPE[j] = s_v.mark()
            if i >= 2 and i - 2 < NJ:
                j = i - 2
                wiJ, tJ = jobs[j]
                dstT = qT4 if wiJ == 0 else lkT4
                dW = 1024 if wiJ == 0 else 1152
                P.op("vector", lambda e, it_=tJ, d=dstT, w_=dW: e.tensor_copy(
                    d.rearrange("p (h t) -> p h t", t=w_)[
                        :, :, it_ * 128:(it_ + 1) * 128],
                    PSbf[2][:, 0:512].rearrange("p (h t) -> p h t", h=4)),
                    waits=[TR[j]], inc=s_v)
                P.ps_done("2t", s_v.mark())

        sv_ = g % 2
        wlw = [wbuf_free[sv_]]
        P.op("gpsimd", lambda e, g_=g, wb=wbufs[sv_]: e.dma_start(
            out=wb.rearrange("p (c f) -> p c f", c=16),
            in_=Wlv[:, g_ * 512:(g_ + 1) * 512].rearrange(
                "(c p) f -> p c f", p=128)), waits=wlw, inc=s_w, inc_n=16)
        WLDv = s_w.mark()
        ld(brow[0:1, sv_ * 512:sv_ * 512 + 512],
           blv[0:1, g * 512:(g + 1) * 512], waits=[bias_free[sv_]])
        BLDv = s_c.mark()
        wbuf = wbufs[sv_]
        for it in range(9):
            xt0 = kt_off(it)
            key = "0a" if it % 2 == 0 else "1a"
            ps = PS[0] if it % 2 == 0 else PS[1]
            mmw = [WLDv, BLDv] + P.ps_take(key)
            P.op("tensor", lambda e, p=ps, b2=brow[0:1, sv_ * 512:sv_ * 512 + 512]:
                 e.matmul(p[:, 0:512], onesr[0:1, 0:128], b2,
                          start=True, stop=False), waits=mmw)
            for hc in range(16):
                P.op("tensor", lambda e, h=hc, x0=xt0, p=ps, wb=wbuf:
                     e.matmul(p[:, 0:512],
                              xnT[:, h * TKT + x0: h * TKT + x0 + 128],
                              wb[:, h * 512:(h + 1) * 512],
                              start=False, stop=(h == 15)),
                     inc=s_t if hc == 15 else None)
            MMv = s_t.mark()
            P.op("vector", lambda e, it_=it, p=ps: e.tensor_copy(
                lv4.rearrange("p (h i d) -> p h i d", h=4, i=9)[:, :, it_, :],
                p[:, 0:512].rearrange("p (h d) -> p h d", h=4)),
                waits=[MMv], inc=s_v)
            P.ps_done(key, s_v.mark())
        wbuf_free[sv_] = s_t.mark()
        bias_free[sv_] = wbuf_free[sv_]
        PROJ = s_v.mark()
        if g == TAP_GROUP and "qT4" in DEBUG_TAPS:
            t_ = tap("qT4", (128, 4096), BF16)
            P.op("sync", lambda e, t2=t_: e.dma_start(out=t2, in_=qT4),
                 waits=[PROJ], inc=s_st, inc_n=16)
            t_ = tap("lkT4", (128, 4608), BF16)
            P.op("sync", lambda e, t2=t_: e.dma_start(out=t2, in_=lkT4),
                 waits=[PROJ], inc=s_st, inc_n=16)
            t_ = tap("lv4", (128, 4608), BF16)
            P.op("sync", lambda e, t2=t_: e.dma_start(out=t2, in_=lv4),
                 waits=[PROJ], inc=s_st, inc_n=16)
            ATT0.append(s_st.mark())

        SCM = {}; MSKD = {}; EXD = {}; SUMS = {}; SCL = {}; MX = {}
        TRp = {}; EVp = {}; PVm = {}; CTXE = {}
        NPAIR = 16
        def pr_h(p):
            return p // 4
        def pr_qt(p):
            return (p % 4) * 2
        for i in range(NPAIR + 3):
            if i < NPAIR:
                h4 = pr_h(i); qt = pr_qt(i); h = g * 4 + h4
                psL = PS[0][:, 512:1024] if i % 2 == 0 else PS[3][:, 0:512]
                psC = PS[1][:, 512:1024] if i % 2 == 0 else PS[3][:, 512:1024]
                kL = "0b" if i % 2 == 0 else "3a"
                kC = "1b" if i % 2 == 0 else "3b"
                scw = [PROJ] + P.ps_take(kL, kC)
                for k2 in range(2):
                    P.op("tensor", lambda e, h_=h4, q_=qt + k2, p=psL, o=k2 * 256:
                         e.matmul(p[:, o:o + 256],
                                  qT4[:, h_ * 1024 + q_ * 128: h_ * 1024 + (q_ + 1) * 128],
                                  lkT4[:, h_ * 1152 + q_ * 128: h_ * 1152 + q_ * 128 + 256],
                                  start=True, stop=True),
                         waits=scw if k2 == 0 else ())
                    P.op("tensor", lambda e, h_=h4, q_=qt + k2, p=psC, o=k2 * 256:
                         e.matmul(p[:, o:o + 128],
                                  qT4[:, h_ * 1024 + q_ * 128: h_ * 1024 + (q_ + 1) * 128],
                                  ckT_sb, start=True, stop=False))
                    P.op("tensor", lambda e, hh=h, h_=h4, q_=qt + k2, p=psC, o=k2 * 256:
                         e.matmul(p[:, o + 128:o + 132],
                                  qT4[:, h_ * 1024 + q_ * 128: h_ * 1024 + (q_ + 1) * 128],
                                  sinkTk_sb[:, hh:hh + 4], start=False, stop=True),
                         inc=s_t if k2 == 1 else None)
                SCM[i] = s_t.mark()
            if i >= 1 and i - 1 < NPAIR:
                j = i - 1
                mb = 8 + (j % 2) * 12
                psL = PS[0][:, 512:1024] if j % 2 == 0 else PS[3][:, 0:512]
                psC = PS[1][:, 512:1024] if j % 2 == 0 else PS[3][:, 512:1024]
                mxw = [SCM[j]]
                if j >= 2:
                    mxw.append(EXD[j - 2])
                P.op("vector", lambda e, p=psL, b=mb: e.reduce_max(
                    rrp[:, b:b + 2].rearrange("p (a b2) -> p a b2", a=2),
                    p.rearrange("p (a b2) -> p a b2", a=2), axis=AX.X,
                    negate=True), waits=mxw, inc=s_v)
                P.op("vector", lambda e, p=psC, b=mb: e.reduce_max(
                    rrp[:, b + 2:b + 4].rearrange("p (a b2) -> p a b2", a=2),
                    p.rearrange("p (a b2) -> p a b2", a=2)[:, :, 0:132], axis=AX.X,
                    negate=True), inc=s_v)
                P.op("vector", lambda e, b=mb: e.tensor_scalar(
                    rrp[:, b + 4:b + 8], rrp[:, b:b + 4], SCALE, None, ALU.mult),
                    inc=s_v)
                MX[j] = s_v.mark()
            if i >= 3 and i - 3 < NPAIR:
                j = i - 3
                evw = [TRp[j]]
                if j >= 1:
                    evw.append(PVm[j - 1])
                P.op("scalar", lambda e: e.activation(pTl, PSbf[2][:, 0:512],
                                                      AF.Copy), waits=evw, inc=s_s)
                P.op("scalar", lambda e: e.activation(pTc, PSbf[2][:, 512:768],
                                                      AF.Copy), inc=s_s)
                P.op("scalar", lambda e: e.activation(pTs[0:1, :],
                                                      PSbf[2][0:1, 768:1024],
                                                      AF.Copy), inc=s_s)
                EVp[j] = s_s.mark()
                P.ps_done("2t", EVp[j])
            if i >= 2 and i - 2 < NPAIR:
                j = i - 2
                trw = [SCL[j]] + P.ps_take("2t")
                pljj = pl2[(j % 2)]
                pcjj = pc2[(j % 2)]
                for k2 in range(4):
                    P.op("tensor", lambda e, k=k2, pj=pljj: e.transpose(
                        PSbf[2][:, k * 128:(k + 1) * 128],
                        pj[:, k * 128:(k + 1) * 128], id_sb),
                        waits=trw if k2 == 0 else ())
                for k2 in range(2):
                    P.op("tensor", lambda e, k=k2, pj=pcjj: e.transpose(
                        PSbf[2][:, 512 + k * 128: 512 + (k + 1) * 128],
                        pj[:, k * 132:k * 132 + 128], id_sb))
                for k2 in range(2):
                    P.op("tensor", lambda e, k=k2, pj=pcjj: e.transpose(
                        PSbf[2][0:1, 768 + k * 128: 768 + (k + 1) * 128],
                        pj[:, k * 132 + 128:k * 132 + 129], id_sb),
                        inc=s_t if k2 == 1 else None)
                TRp[j] = s_t.mark()
            if i >= 3 and i - 3 < NPAIR:
                j = i - 3
                h4 = pr_h(j); qt = pr_qt(j); h = g * 4 + h4
                cx0 = 512 if j % 2 == 0 else 768
                pvw = [EVp[j]] + P.ps_take("ctx_e" if j % 2 == 0 else "ctx_o")
                for k2 in range(2):
                    q_ = qt + k2
                    pcx = PS[2][:, cx0 + k2 * 128: cx0 + (k2 + 1) * 128]
                    P.op("tensor", lambda e, h_=h4, q2=q_, p=pcx, k=k2: e.matmul(
                        p, lv4[:, h_ * 1152 + q2 * 128: h_ * 1152 + (q2 + 1) * 128],
                        pTl[:, k * 256:k * 256 + 128], start=True, stop=False),
                        waits=pvw if k2 == 0 else ())
                    P.op("tensor", lambda e, h_=h4, q2=q_, p=pcx, k=k2: e.matmul(
                        p, lv4[:, h_ * 1152 + (q2 + 1) * 128: h_ * 1152 + (q2 + 2) * 128],
                        pTl[:, k * 256 + 128:k * 256 + 256], start=False, stop=False))
                    P.op("tensor", lambda e, p=pcx, k=k2: e.matmul(
                        p, cv_sb, pTc[:, k * 128:(k + 1) * 128],
                        start=False, stop=False))
                    P.op("tensor", lambda e, hh=h, p=pcx, k=k2: e.matmul(
                        p, sinkv_sb[0:1, hh * 128:(hh + 1) * 128],
                        pTs[0:1, k * 128:(k + 1) * 128], start=False, stop=True),
                        inc=s_t if k2 == 1 else None)
                PVm[j] = s_t.mark()
            if i >= 1 and i - 1 < NPAIR:
                j = i - 1
                psL = PS[0][:, 512:1024] if j % 2 == 0 else PS[3][:, 0:512]
                psC = PS[1][:, 512:1024] if j % 2 == 0 else PS[3][:, 512:1024]
                plj = pl2[(j % 2)]
                pcj = pc2[(j % 2)]
                mb = 8 + (j % 2) * 12
                exw = [SCM[j], MX[j]]
                if j >= 2:
                    exw.append(SCL[j - 2])
                for k2 in range(2):
                    P.op("scalar", lambda e, p=psL, pj=plj, k=k2, b=mb: e.activation(
                        pj[:, k * 256:(k + 1) * 256], p[:, k * 256:(k + 1) * 256],
                        AF.Exp, scale=SCALE, bias=rrp[:, b + 4 + k:b + 5 + k]),
                        waits=exw if k2 == 0 else (), inc=s_s)
                    P.op("scalar", lambda e, p=psC, pj=pcj, k=k2, b=mb: e.activation(
                        pj[:, k * 132:(k + 1) * 132],
                        p[:, k * 256:k * 256 + 132],
                        AF.Exp, scale=SCALE, bias=rrp[:, b + 6 + k:b + 7 + k]),
                        inc=s_s)
                EXD[j] = s_s.mark()
                P.ps_done("0b" if j % 2 == 0 else "3a", EXD[j])
                P.ps_done("1b" if j % 2 == 0 else "3b", EXD[j])
            if i >= 1 and i - 1 < NPAIR:
                j = i - 1
                qt = pr_qt(j)
                plj = pl2[(j % 2)]
                pcj = pc2[(j % 2)]
                mk = mfirstM_sb if qt == 0 else mrestM_sb
                P.op("vector", lambda e, pj=plj, m=mk: e.tensor_tensor(
                    pj, pj, m, ALU.mult), waits=[EXD[j]], inc=s_v)
                P.op("vector", lambda e, pj=pcj, q_=qt: e.tensor_tensor(
                    pj, pj, cmaskM_sb[:, q_ * CW:(q_ + 2) * CW], ALU.mult), inc=s_v)
                MSKD[j] = s_v.mark()
                P.op("vector", lambda e, pj=plj: e.reduce_sum(
                    rrp[:, 0:2].rearrange("p (a b) -> p a b", a=2),
                    pj.rearrange("p (a b) -> p a b", a=2), axis=AX.X),
                    waits=[MSKD[j]], inc=s_v)
                P.op("vector", lambda e, pj=pcj: e.reduce_sum(
                    rrp[:, 2:4].rearrange("p (a b) -> p a b", a=2),
                    pj.rearrange("p (a b) -> p a b", a=2), axis=AX.X), inc=s_v)
                SUMS[j] = s_v.mark()
                P.op("vector", lambda e: e.reciprocal(rrp[:, 4:8], rrp[:, 0:4]),
                     waits=[SUMS[j]], inc=s_v)
                RCm = s_v.mark()
                for k2 in range(2):
                    P.op("vector", lambda e, pj=plj, k=k2: e.tensor_scalar(
                        pj[:, k * 256:(k + 1) * 256], pj[:, k * 256:(k + 1) * 256],
                        rrp[:, 4 + k:5 + k], 0.5, ALU.mult, ALU.mult),
                        waits=[RCm] if k2 == 0 else (), inc=s_v)
                    P.op("vector", lambda e, pj=pcj, k=k2: e.tensor_scalar(
                        pj[:, k * 132:(k + 1) * 132], pj[:, k * 132:(k + 1) * 132],
                        rrp[:, 6 + k:7 + k], 0.5, ALU.mult, ALU.mult), inc=s_v)
                SCL[j] = s_v.mark()
            if i >= 3 and i - 3 < NPAIR:
                j = i - 3
                h4 = pr_h(j); qt = pr_qt(j); h = g * 4 + h4
                cx0 = 512 if j % 2 == 0 else 768
                P.op("vector", lambda e, hh=h, q_=qt, c=cx0: e.tensor_copy(
                    ctxT[:, hh * 1024 + q_ * 128: hh * 1024 + (q_ + 2) * 128],
                    PS[2][:, c:c + 256]), waits=[PVm[j]], inc=s_v)
                CTXE[j] = s_v.mark()
                P.ps_done("ctx_e" if j % 2 == 0 else "ctx_o", CTXE[j])
    CTX = s_v.mark()
    if "ctxT" in DEBUG_TAPS:
        t_ = tap("ctxT", (128, 16 * 1024), BF16)
        P.op("sync", lambda e, t2=t_: e.dma_start(out=t2, in_=ctxT), waits=[CTX],
             inc=s_st, inc_n=16)

    P.op("gpsimd", lambda e: e.dma_start(out=smt_d, in_=smt_sb[0:8, :]),
         waits=[SMT], inc=s_w, inc_n=16)
    SMT_ST = s_w.mark()
    P.op("gpsimd", lambda e: e.collective_compute(
        "AllGather", ALU.bypass, replica_groups=[list(range(8))],
        ins=[smt_d.opt()], outs=[gath_d.opt()]), waits=[SMT_ST], inc=s_cc)
    for gi in range(2):
        P.op("gpsimd", lambda e, g_=gi: e.dma_start(
            out=bis_sb[g_::2, :],
            in_=gath_d.rearrange("(r e) (g2 f) -> e r g2 f", e=8, g2=2)[:, :, g_, :]),
            waits=[(s_cc, 1)] if gi == 0 else (), inc=s_w, inc_n=16)
    BIS_LD = s_w.mark()
    if "bis" in DEBUG_TAPS:
        t_ = tap("bis", (128, 512))
        P.op("sync", lambda e, t2=t_: e.dma_start(out=t2, in_=bis_sb),
             waits=[BIS_LD], inc=s_st, inc_n=16)

    bis = {"k": 0, "cur": None}

    def bis_init():
        P.op("vector", lambda e: e.memset(bst[0:8, 0:1], 0.0),
             waits=[BIS_LD, CTX], inc=s_v)
        P.op("vector", lambda e: e.memset(bst[0:8, 1:2], 1.1), inc=s_v)
        P.op("vector", lambda e: e.memset(bst[0:8, 2:3], 0.55), inc=s_v)
        P.op("vector", lambda e: e.memset(mid128[:, 0:1], 0.55), inc=s_v)
        bis["cur"] = s_v.mark()

    def bis_iter():
        it = bis["k"]
        bis["k"] += 1
        a = (it % 2) * 8
        b_ = 8 - a
        m128 = mid128[:, it % 2:it % 2 + 1]
        m128n = mid128[:, 1 - it % 2:2 - it % 2]
        P.op("vector", lambda e, m=m128: e.tensor_scalar(
            trashA, bis_sb, m, 0.0, ALU.is_ge, ALU.add, accum_out=cnt128),
            waits=[bis["cur"]], inc=s_v)
        CN = s_v.mark()
        P.op("tensor", lambda e: e.matmul(PS[3][0:8, 256:257], m8_sb, cnt128,
                                          start=True, stop=True),
             waits=[CN] + P.ps_take("3a"), inc=s_t)
        CM = s_t.mark()
        P.op("vector", lambda e, aa=a: e.tensor_scalar(
            bst[0:8, aa + 3:aa + 4], PS[3][0:8, 256:257], KCAP, None, ALU.is_ge),
            waits=[CM], inc=s_v)
        P.ps_done("3a", s_v.mark())
        P.op("vector", lambda e, aa=a: e.tensor_tensor(
            bst[0:8, aa + 4:aa + 5], bst[0:8, aa + 2:aa + 3],
            bst[0:8, aa + 0:aa + 1], ALU.subtract), waits=[s_v.mark()], inc=s_v)
        P.op("vector", lambda e, aa=a: e.tensor_tensor(
            bst[0:8, aa + 5:aa + 6], bst[0:8, aa + 1:aa + 2],
            bst[0:8, aa + 2:aa + 3], ALU.subtract), inc=s_v)
        DD = s_v.mark()
        P.op("vector", lambda e, aa=a, bb=b_: e.scalar_tensor_tensor(
            bst[0:8, bb + 0:bb + 1], bst[0:8, aa + 3:aa + 4],
            bst[0:8, aa + 4:aa + 5], bst[0:8, aa + 0:aa + 1], ALU.mult, ALU.add),
            waits=[DD], inc=s_v)
        P.op("vector", lambda e, aa=a, bb=b_: e.scalar_tensor_tensor(
            bst[0:8, bb + 1:bb + 2], bst[0:8, aa + 3:aa + 4],
            bst[0:8, aa + 5:aa + 6], bst[0:8, aa + 2:aa + 3], ALU.mult, ALU.add),
            inc=s_v)
        LH = s_v.mark()
        P.op("vector", lambda e, bb=b_: e.tensor_tensor(
            bst[0:8, bb + 6:bb + 7], bst[0:8, bb + 1:bb + 2],
            bst[0:8, bb + 0:bb + 1], ALU.subtract), waits=[LH], inc=s_v)
        DD2 = s_v.mark()
        P.op("vector", lambda e, bb=b_: e.scalar_tensor_tensor(
            bst[0:8, bb + 2:bb + 3], bst[0:8, bb + 6:bb + 7], 0.5,
            bst[0:8, bb + 0:bb + 1], ALU.mult, ALU.add), waits=[DD2], inc=s_v)
        MD = s_v.mark()
        if it + 1 < BISECT_ITERS:
            P.op("tensor", lambda e, bb=b_: e.matmul(
                PS[3][:, 384:385], seg16T_sb, bst[0:8, bb + 2:bb + 3],
                start=True, stop=True), waits=[MD] + P.ps_take("3a"), inc=s_t)
            MB = s_t.mark()
            P.op("vector", lambda e, m=m128n: e.tensor_copy(m, PS[3][:, 384:385]),
                 waits=[MB], inc=s_v)
            P.ps_done("3a", s_v.mark())
            bis["cur"] = s_v.mark()
        else:
            bis["cur"] = MD

    bis_init()
    prev_ev = {}
    prev_mmdone = None
    mmdone = {}
    chunk_i = 0
    for oc in range(4):
        woslot = oc % 2
        wosb = wbufs[woslot]
        wol = [wbuf_free[woslot]] if oc < 2 else [mmdone[oc - 2]]
        P.op("gpsimd", lambda e, o=oc, wb=wosb: e.dma_start(
            out=wb.rearrange("p (c f) -> p c f", c=16),
            in_=Wo[:, o * 512:(o + 1) * 512].rearrange(
                "(c p) f -> p c f", p=128)), waits=wol, inc=s_w, inc_n=16)
        WOL = s_w.mark()
        if oc == 0:
            ld(borow, bo)
        BOL = s_c.mark()
        for tt in range(8):
            sl4 = tt % 4
            key = f"{sl4 // 2}{'a' if sl4 % 2 == 0 else 'b'}"
            ps8 = PS[sl4 // 2][:, (sl4 % 2) * 512:(sl4 % 2) * 512 + 512]
            bmw = [WOL, BOL, CTX] + P.ps_take(key)
            P.op("tensor", lambda e, p=ps8, o=oc: e.matmul(
                p, onesr[0:1, 0:128], borow[0:1, o * 512:(o + 1) * 512],
                start=True, stop=False), waits=bmw)
            for dc in range(16):
                P.op("tensor", lambda e, d=dc, t=tt, p=ps8, wb=wosb: e.matmul(
                    p, ctxT[:, d * 1024 + t * 128: d * 1024 + (t + 1) * 128],
                    wb[:, d * 512:(d + 1) * 512], start=False, stop=(d == 15)),
                    inc=s_t if dc == 15 else None)
            WM = s_t.mark()
            evw = [WM]
            if (tt % 2) in prev_ev:
                evw.append(prev_ev[tt % 2])
            P.op("scalar", lambda e, p=ps8, sl_=tt % 2: e.activation(
                wo_stage[:, sl_ * 512:(sl_ + 1) * 512], p, AF.Copy),
                waits=evw, inc=s_s)
            EV = s_s.mark()
            P.ps_done(key, EV)
            P.op("sync", lambda e, t=tt, o=oc, sl_=tt % 2: e.dma_start(
                out=ao_d[t * 128:(t + 1) * 128, o * 512:(o + 1) * 512],
                in_=wo_stage[:, sl_ * 512:(sl_ + 1) * 512]), waits=[EV],
                inc=s_st, inc_n=16)
            prev_ev[tt % 2] = s_st.mark()
            chunk_i += 1
            while bis["k"] < min(BISECT_ITERS, int(chunk_i * 1.4)):
                bis_iter()
        mmdone[oc] = s_t.mark()
        prev_mmdone = mmdone[oc]
    while bis["k"] < BISECT_ITERS:
        bis_iter()
    AO_ST = s_st.mark()
    if "ao" in DEBUG_TAPS:
        t_ = tap("ao", (TOK, HID))
        P.op("sync", lambda e, t2=t_: e.dma_start(out=t2, in_=ao_d),
             waits=[AO_ST], inc=s_st, inc_n=16)

    fin = (BISECT_ITERS % 2) * 8
    P.op("vector", lambda e, ff=fin: e.scalar_tensor_tensor(
        sT_row[0:8, :], smt_sb[0:8, :], bst[0:8, ff:ff + 1], smt_sb[0:8, :],
        ALU.is_ge, ALU.mult), waits=[bis["cur"]], inc=s_v)
    STR = s_v.mark()
    sclw = [STR] + P.ps_take("3b")
    for tt in range(8):
        P.op("tensor", lambda e, t=tt: e.matmul(
            PS[3][:, 512 + t:513 + t], sT_row[0:8, t * 128:(t + 1) * 128],
            ones8, start=True, stop=True),
            waits=sclw if tt == 0 else (), inc=s_t if tt == 7 else None)
    SCM2 = s_t.mark()
    P.op("vector", lambda e: e.tensor_copy(s_col, PS[3][:, 512:520]),
         waits=[SCM2], inc=s_v)
    P.ps_done("3b", s_v.mark())
    SCOL = s_v.mark()
    if "scol" in DEBUG_TAPS:
        t_ = tap("scol", (128, 8))
        P.op("sync", lambda e, t2=t_: e.dma_start(out=t2, in_=s_col), waits=[SCOL],
             inc=s_st, inc_n=16)
    if "thr" in DEBUG_TAPS:
        t_ = tap("thr", (8, 16))
        P.op("sync", lambda e, t2=t_: e.dma_start(out=t2, in_=bst), waits=[SCOL],
             inc=s_st, inc_n=16)

    prev_sg = {}
    prev_hT = {}
    wff_free = {}
    for f in range(NFT):
        wgb = wff[(f % 2) * 2]
        wub = wff[(f % 2) * 2 + 1]
        wwl = [prev_mmdone, AO_ST] if f < 2 else [wff_free[f % 2]]
        P.op("gpsimd", lambda e, ff_=f, wb=wgb: e.dma_start(
            out=wb.rearrange("p (c f) -> p c f", c=16),
            in_=wg[:, ff_ * 128:(ff_ + 1) * 128].rearrange(
                "(c p) f -> p c f", p=128)), waits=wwl, inc=s_w, inc_n=16)
        P.op("gpsimd", lambda e, ff_=f, wb=wub: e.dma_start(
            out=wb.rearrange("p (c f) -> p c f", c=16),
            in_=wu[:, ff_ * 128:(ff_ + 1) * 128].rearrange(
                "(c p) f -> p c f", p=128)), inc=s_w, inc_n=16)
        WF = s_w.mark()
        kg = "0a0b" if f % 2 == 0 else "2a2b"
        psg = PS[0] if f % 2 == 0 else PS[2]
        psu = PS[1] if f % 2 == 0 else PS[3]
        kga, kgb = (("0a", "0b") if f % 2 == 0 else ("2a", "2b"))
        kua, kub = (("1a", "1b") if f % 2 == 0 else ("3a", "3b"))
        gw = [WF, XNT_OWN] + P.ps_take(kga, kgb)
        for n2 in range(2):
            for hc in range(16):
                P.op("tensor", lambda e, h=hc, n=n2, p=psg, wb=wgb: e.matmul(
                    p[:, n * 512:(n + 1) * 512], wb[:, h * 128:(h + 1) * 128],
                    xnT[:, h * TKT + n * 512: h * TKT + (n + 1) * 512],
                    start=(h == 0), stop=(h == 15)),
                    waits=gw if (n2 == 0 and hc == 0) else ())
        uw = P.ps_take(kua, kub)
        for n2 in range(2):
            for hc in range(16):
                P.op("tensor", lambda e, h=hc, n=n2, p=psu, wb=wub: e.matmul(
                    p[:, n * 512:(n + 1) * 512], wb[:, h * 128:(h + 1) * 128],
                    xnT[:, h * TKT + n * 512: h * TKT + (n + 1) * 512],
                    start=(h == 0), stop=(h == 15)),
                    waits=uw if (n2 == 0 and hc == 0) else (),
                    inc=s_t if (n2 == 1 and hc == 15) else None)
        GU = s_t.mark()
        wff_free[f % 2] = GU
        sgb = sg_sb[:, (f % 2) * 1024:(f % 2 + 1) * 1024]
        sgw = [GU, CONSTS]
        if (f % 2) in prev_sg:
            sgw.append(prev_sg[f % 2])
        P.op("scalar", lambda e, ff_=f, p=psg, sg_=sgb: e.activation(
            sg_, p, AF.Silu, bias=bgc_sb[:, ff_:ff_ + 1]), waits=sgw, inc=s_s)
        SG = s_s.mark()
        P.ps_done(kga, SG); P.ps_done(kgb, SG)
        P.op("vector", lambda e, ff_=f, p=psu, sg_=sgb: e.scalar_tensor_tensor(
            hT_tile(ff_), p, buc_sb[:, ff_:ff_ + 1], sg_, ALU.add, ALU.mult),
            waits=[SG], inc=s_v)
        HTm = s_v.mark()
        P.ps_done(kua, HTm); P.ps_done(kub, HTm)
        prev_sg[f % 2] = HTm
        prev_hT[f % 2] = HTm
    HT = s_v.mark()
    if "hT0" in DEBUG_TAPS:
        t_ = tap("hT0", (128, 1024), BF16)
        P.op("sync", lambda e, t2=t_: e.dma_start(out=t2, in_=hT_tile(0)),
             waits=[HT], inc=s_st, inc_n=16)
    if "hT42" in DEBUG_TAPS:
        t_ = tap("hT42", (128, 1024), BF16)
        P.op("sync", lambda e, t2=t_: e.dma_start(out=t2, in_=hT_tile(42)),
             waits=[HT], inc=s_st, inc_n=16)

    FBLK = [list(range(i, min(i + 4, NFT))) for i in range(0, NFT, 4)]
    xnTf = xnT.bitcast(F32)
    ystage4 = [xnTf[:, k * 512:(k + 1) * 512] for k in range(8)]
    aostage4 = [xnTf[:, 4096 + k * 512: 4096 + (k + 1) * 512] for k in range(8)]
    assert xnTf.shape[1] >= 8192
    wdb4 = [aC[:, k * 2048:(k + 1) * 2048] for k in range(4)]
    wd_free = {}
    ystore = {}
    for oc in range(4):
        if oc == 0:
            ld(bdrow, bd)
        BDL = s_c.mark()
        for tt in range(8):
            key = f"{tt // 2}{'a' if tt % 2 == 0 else 'b'}"
            ps8 = PS[tt // 2][:, (tt % 2) * 512:(tt % 2) * 512 + 512]
            wts = [BDL, HT] + P.ps_take(key)
            P.op("tensor", lambda e, p=ps8, o=oc: e.matmul(
                p, onesr[0:1, 0:128], bdrow[0:1, o * 512:(o + 1) * 512],
                start=True, stop=False), waits=wts)
        DNt = {}
        for bi, fb in enumerate(FBLK):
            gbi = oc * len(FBLK) + bi
            wdb = wdb4[gbi % 4]
            nf = len(fb)
            fin_blk = (bi == len(FBLK) - 1)
            wdw = [HT] if gbi < 4 else [wd_free[gbi % 4]]
            P.op("gpsimd", lambda e, o=oc, f0=fb[0], n=nf, wb=wdb: e.dma_start(
                out=wb[:, 0:n * 512].rearrange("p (c f) -> p c f", c=n),
                in_=wd[f0 * 128:(f0 + n) * 128, o * 512:(o + 1) * 512].rearrange(
                    "(c p) f -> p c f", p=128)), waits=wdw, inc=s_w, inc_n=16)
            WDL = s_w.mark()
            for fi, f in enumerate(fb):
                for tt in range(8):
                    ps8 = PS[tt // 2][:, (tt % 2) * 512:(tt % 2) * 512 + 512]
                    last = (f == NFT - 1)
                    blk_last = (fi == nf - 1 and (tt == 7 or fin_blk))
                    P.op("tensor", lambda e, p=ps8, ff_=f, t=tt, fi_=fi, wb=wdb: e.matmul(
                        p, hT_tile(ff_)[:, t * 128:(t + 1) * 128],
                        wb[:, fi_ * 512: fi_ * 512 + 512],
                        start=False, stop=last),
                        waits=[WDL] if (fi == 0 and tt == 0) else (),
                        inc=s_t if blk_last else None)
                    if fi == nf - 1 and fin_blk:
                        DNt[tt] = s_t.mark()
            wd_free[gbi % 4] = s_t.mark()
        DN = s_t.mark()
        for tt in range(8):
            key = f"{tt // 2}{'a' if tt % 2 == 0 else 'b'}"
            sl4 = tt
            ps8 = PS[tt // 2][:, (tt % 2) * 512:(tt % 2) * 512 + 512]
            ysl = ystage4[sl4]
            aosl = aostage4[sl4]
            xw_ = [SCOL, AO_ST, HT]
            if sl4 in ystore:
                xw_.append(ystore[sl4])
            P.op("sync", lambda e, t=tt, o=oc, y_=ysl: e.dma_start(
                out=y_, in_=x_own[t * 128:(t + 1) * 128, o * 512:(o + 1) * 512]),
                waits=xw_, inc=s_x, inc_n=16)
            P.op("sync", lambda e, t=tt, o=oc, a_=aosl: e.dma_start(
                out=a_, in_=ao_d[t * 128:(t + 1) * 128, o * 512:(o + 1) * 512]),
                inc=s_x, inc_n=16)
            XT = s_x.mark()
            P.op("vector", lambda e, p=ps8, t=tt, y_=ysl: e.scalar_tensor_tensor(
                y_, p, s_col[:, t:t + 1], y_, ALU.mult, ALU.add),
                waits=[XT, DNt[tt], SCOL], inc=s_v)
            P.op("vector", lambda e, y_=ysl, a_=aosl: e.tensor_tensor(
                y_, y_, a_, ALU.add), inc=s_v)
            YV = s_v.mark()
            P.ps_done(key, YV)
            if oc == 0 and tt == 0 and "yraw" in DEBUG_TAPS:
                t_ = tap("yraw", (128, 512))
                P.op("sync", lambda e, t2=t_, y_=ysl: e.dma_start(out=t2, in_=y_),
                     waits=[YV], inc=s_st, inc_n=16)
            P.op("sync", lambda e, t=tt, o=oc, y_=ysl: e.dma_start(
                out=y[t * 128:(t + 1) * 128, o * 512:(o + 1) * 512], in_=y_),
                waits=[YV], inc=s_st, inc_n=16)
            ystore[sl4] = s_st.mark()
    P.op("sync", lambda e: e.nop(), waits=[s_st.mark()])
    P.op("gpsimd", lambda e: e.nop(), waits=[s_st.mark()])

    finals = {s_x: s_x.n, s_x2: s_x2.n, s_c: s_c.n, s_w: s_w.n, s_t: s_t.n,
              s_v: s_v.n, s_s: s_s.n, s_cc: s_cc.n, s_st: s_st.n}
    for eng, lw in P.last_wait.items():
        for sem, v in lw.items():
            assert v <= finals[sem], (eng, v, finals[sem])

    with nc.Block() as block:
        P.emit(block)
    return nc, taps



def run_raw(inputs):
    nc, taps = build_program()
    in_maps = [host_inputs(c, inputs) for c in range(8)]
    res = run_bass_kernel_spmd(nc, in_maps, core_ids=list(range(8)))
    return res


def kernel(**inputs):
    res = run_raw(inputs)
    out = np.zeros((B, S, HID), np.float32)
    for c in range(8):
        b, half = c // 2, c % 2
        out[b, half * TOK:(half + 1) * TOK] = res.results[c]["y"]
    return out
